# version 29
# baseline (speedup 1.0000x reference)
import numpy as np
import concourse.bass as bass
import concourse.mybir as mybir
from concourse.bass_utils import run_bass_kernel_spmd

F32 = mybir.dt.float32
BF16 = mybir.dt.bfloat16
I32 = mybir.dt.int32
AF = mybir.ActivationFunctionType
ALU = mybir.AluOpType
AX = mybir.AxisListType

DM = 1024
DI = 2048
NH = 16
HD = 128
CONVD = 2304
JPROJ = 4368
HID = 3072
CH = 128
T = 256
EPS = 1e-5
QS = 1.0 / 127.0
NEG = -1e9

SW, GW, ST, SV, SS, SO = "sw", "gw", "st", "sv", "ss", "so"
SG = "sg"


class Prog:
    ENGS = ("t", "v", "s", "y", "g")
    POOLS = {"w": ["w0", "w1", "w2", "w3"],
             "g": ["g0", "g1", "g2", "g3", "g4", "g5"],
             "o": ["o0", "o1"], "p": ["p0", "p1", "p2", "p3"]}

    def __init__(self):
        self.q = {e: [] for e in self.ENGS}
        keys = [ST, SV, SS, SG] + [s for p in self.POOLS.values() for s in p]
        self.cnt = {k: 0 for k in keys}
        self.rr = {p: 0 for p in self.POOLS}

    def emit(self, eng, fn, waits=(), inc=None, inc_by=1):
        self.q[eng].append((tuple(waits), fn, inc, inc_by))
        if inc is not None:
            self.cnt[inc] += inc_by
            return self.cnt[inc]
        return None

    def at(self, sem):
        return self.cnt[sem]

    def dma(self, eng, pool, fn, waits=()):
        sems = self.POOLS[pool]
        s = sems[self.rr[pool] % len(sems)]
        self.rr[pool] += 1
        w = list(waits)
        if self.cnt[s] > 0:
            w.append((s, self.cnt[s]))
        v = self.emit(eng, fn, waits=w, inc=s, inc_by=16)
        return (s, v)

    def snap(self, pool):
        return [(s, self.cnt[s]) for s in self.POOLS[pool] if self.cnt[s] > 0]


def build_nc(n_outer=8, dbg=False):
    L = n_outer * T
    nc = bass.Bass("TRN2", target_bir_lowering=False)

    u_h = nc.dram_tensor("u", [L, DM], F32, kind="ExternalInput").ap()
    winT = nc.dram_tensor("winT", [DM, JPROJ], F32, kind="ExternalInput").ap()
    convw_h = nc.dram_tensor("convw", [128, 18 * 4], F32, kind="ExternalInput").ap()
    convb_h = nc.dram_tensor("convb", [128, 18], F32, kind="ExternalInput").ap()
    dtb_h = nc.dram_tensor("dtb", [NH, 1], F32, kind="ExternalInput").ap()
    alog_h = nc.dram_tensor("alog", [NH, 1], F32, kind="ExternalInput").ap()
    dvec_h = nc.dram_tensor("dvec", [1, NH], F32, kind="ExternalInput").ap()
    nw_h = nc.dram_tensor("nw", [128, NH], F32, kind="ExternalInput").ap()
    woutT = nc.dram_tensor("woutT", [DI, DM], F32, kind="ExternalInput").ap()
    wgT = nc.dram_tensor("wgT", [DM, HID], I32, kind="ExternalInput").ap()
    wuT = nc.dram_tensor("wuT", [DM, HID], I32, kind="ExternalInput").ap()
    wdT = nc.dram_tensor("wdT", [HID, DM], I32, kind="ExternalInput").ap()
    out_h = nc.dram_tensor("out", [L, DM], F32, kind="ExternalOutput").ap()
    wg16_h = nc.dram_tensor("wg16", [DM, HID], BF16, kind="Internal").ap()
    wu16_h = nc.dram_tensor("wu16", [DM, HID], BF16, kind="Internal").ap()
    wd16_h = nc.dram_tensor("wd16", [HID, DM], BF16, kind="Internal").ap()
    win16h_h = nc.dram_tensor("win16h", [DM, JPROJ], BF16, kind="Internal").ap()
    win16l_h = nc.dram_tensor("win16l", [DM, JPROJ], BF16, kind="Internal").ap()
    acsf_d = nc.dram_tensor("acsf_d", [1, 2048], F32, kind="Internal").ap()
    nacsf_d = nc.dram_tensor("nacsf_d", [1, 2048], F32, kind="Internal").ap()
    wout16h_h = nc.dram_tensor("wout16h", [DI, DM], BF16, kind="Internal").ap()
    wout16l_h = nc.dram_tensor("wout16l", [DI, DM], BF16, kind="Internal").ap()

    dbg_h = {}
    if dbg:
        for nm, shape in [("d_xnf", [128, 8 * T]), ("d_sz", [128, 16 * T]),
                          ("d_xbcc", [128, 18 * T]), ("d_dts", [16, 4 * T]),
                          ("d_ynw", [128, 16 * T]), ("d_att", [128, 2 * DM]),
                          ("d_carry", [128, NH * HD]),
                          ("d_mt", [128, 512]), ("d_cst", [128, 512]),
                          ("d_em", [128, 512]), ("d_er", [128, 512])]:
            dbg_h[nm] = nc.dram_tensor(nm, shape, F32, kind="ExternalOutput").ap()

    ident_h = nc.inline_tensor(np.eye(128, dtype=np.float32), name="identc").ap()
    onesc_h = nc.inline_tensor(np.ones((128, 1), dtype=np.float32), name="onescc").ap()
    onesr_h = nc.inline_tensor(np.ones((1, 128), dtype=np.float32), name="onesrc").ap()
    mt_np = np.where(np.arange(128)[None, :] >= np.arange(128)[:, None],
                     1.0, 0.0).astype(np.float32)
    masktri_h = nc.inline_tensor(mt_np, name="masktric").ap()
    tr_np = (np.arange(128)[:, None] <= np.arange(128)[None, :]).astype(np.float32)
    trilones_h = nc.inline_tensor(tr_np, name="trilonesc").ap()

    P = Prog()
    ctx = []

    def sb(name, p, cols, dt=F32):
        g = nc.sbuf_tensor(name, [p, cols], dt)
        ctx.append(g)
        return g.__enter__()

    ident = sb("ident", 128, 128)
    ones_col = sb("ones_col", 128, 1)
    ones_row = sb("ones_row", 1, 128)
    masktri = sb("masktri", 128, 128)
    trilones = sb("trilones", 128, 128)
    convw_sb = sb("convw_sb", 128, 18 * 4)
    convb_sb = sb("convb_sb", 128, 18)
    dtb_sb = sb("dtb_sb", NH, 1)
    c_one = sb("c_one", NH, 1)
    c_tiny = sb("c_tiny", NH, 1)
    alog_sb = sb("alog_sb", NH, 1)
    a_tmp = sb("a_tmp", NH, 1)
    a_col = sb("a_col", NH, 1)
    d_row = sb("d_row", 1, NH)
    db128 = sb("db128", 128, NH)
    nw_sb = sb("nw_sb", 128, NH)
    di_sb = sb("di_sb", 128, NH * HD)
    carry = sb("carry", 128, NH * HD)
    tail_sb = sb("tail_sb", 128, 18 * 3)

    u_sb = sb("u_sb", 128, 2 * DM)
    ln_s1 = sb("ln_s1", 128, 8)
    xn_f = sb("xn_f", 128, 8 * T)
    arenaA = sb("arenaA", 128, 18 * (T + 3))
    actB = sb("actB", 128, 8704)
    dts = sb("dts", NH, 4 * T)
    att_sb = sb("att_sb", 128, 2 * DM)
    out_sb = sb("out_sb", 128, 2 * DM)

    dAT = sb("dAT", 128, NH)
    acs_h = sb("acs_h", NH, CH)
    nacs_h = sb("nacs_h", NH, CH)
    dch = sb("dch", NH, CH)
    sc_h = sb("sc_h", NH, CH)
    scT = sb("scT", 128, NH)
    cdb128 = sb("cdb128", 128, NH)
    btok = sb("btok", 128, CH)
    bsc = sb("bsc", 128, 4 * CH)
    xtok = sb("xtok", 128, NH * CH)
    expmt4 = sb("expmt4", 128, 512)
    expr4 = sb("expr4", 128, 512)
    mt_sb = sb("mt_sb", 128, 4 * CH)
    cst = sb("cst", 128, 4 * CH)
    y1_sb = sb("y1_sb", 128, 4 * CH)
    ysq = sb("ysq", 128, 4 * CH)
    ssqrow = sb("ssqrow", 1, T)
    rstdrow = sb("rstdrow", 1, T)
    rstd_col = sb("rstd_col", 128, 2)
    ffacc = sb("ffacc", 128, DM)
    sg = sb("sg", 128, 2 * T)
    wbufW = sb("wbufW", 128, 16384, BF16)
    wbuf16 = sb("wbuf16", 128, 12288, BF16)
    xnh = sb("xnh", 128, 8 * T, BF16)
    xnl = sb("xnl", 128, 8 * T, BF16)
    hhb = sb("hhb", 128, 2 * T, BF16)
    hlb = sb("hlb", 128, 2 * T, BF16)

    convtmp = ffacc[:, 0:T]

    pbg = [nc.psum_tensor(f"pb{i}", [128, 512], F32) for i in range(8)]
    ctx.extend(pbg)
    pb = [g.__enter__() for g in pbg]

    def xbcpre(f):
        return arenaA[:, f * (T + 3):(f + 1) * (T + 3)]

    def ynw(h):
        return arenaA[:, h * T:(h + 1) * T]

    def szt(f):
        return actB[:, f * T:(f + 1) * T]

    def xbcc(f):
        return actB[:, 4096 + f * T:4096 + (f + 1) * T]

    def hffn(j):
        return actB[:, j * T:(j + 1) * T]

    def wslabW(i, lo):
        return wbufW[:, i * 8192 + lo * 4096:i * 8192 + lo * 4096 + 4096]

    def wqW(pos, lo):
        return wbufW[:, pos * 2048 + lo * 1024:pos * 2048 + (lo + 1) * 1024]

    def wslab16(i):
        return wbuf16[:, i * 4096:(i + 1) * 4096]

    def wq16(pos):
        return wbuf16[:, pos * 1024:(pos + 1) * 1024]

    def d(fn):
        return P.dma("y", "w", fn)

    d(lambda e: e.dma_start(out=ident[:], in_=ident_h[:, :]))
    d(lambda e: e.dma_start(out=ones_col[:], in_=onesc_h[:, :]))
    d(lambda e: e.dma_start(out=ones_row[:], in_=onesr_h[:, :]))
    d(lambda e: e.dma_start(out=masktri[:], in_=masktri_h[:, :]))
    d(lambda e: e.dma_start(out=trilones[:], in_=trilones_h[:, :]))
    d(lambda e: e.dma_start(out=convw_sb[:], in_=convw_h[:, :]))
    d(lambda e: e.dma_start(out=convb_sb[:], in_=convb_h[:, :]))
    d(lambda e: e.dma_start(out=dtb_sb[:], in_=dtb_h[:, :]))
    d(lambda e: e.dma_start(out=alog_sb[:], in_=alog_h[:, :]))
    d(lambda e: e.dma_start(out=d_row[:], in_=dvec_h[:, :]))
    d(lambda e: e.dma_start(out=nw_sb[:], in_=nw_h[:, :]))
    d(lambda e: e.dma_start(
        out=u_sb[:, 0:2 * DM],
        in_=u_h.rearrange("(a p) n -> p a n", p=128)[:, 0:2, :]))
    m_init = P.snap("w")

    m_prep = [P.dma("g", "g", lambda e, dst=dst, s_=s_: e.dma_start(
        out=dst[:, :], in_=s_[:, :]))
        for dst, s_ in [(wg16_h, wgT), (wu16_h, wuT), (wd16_h, wdT)]]
    P.emit("v", lambda e: e.memset(carry[:], 0.0), inc=SV)
    P.emit("v", lambda e: e.memset(tail_sb[:], 0.0), inc=SV)
    P.emit("v", lambda e: e.memset(c_one[:], 1.0), inc=SV)
    P.emit("v", lambda e: e.memset(c_tiny[:], 1e-30), inc=SV)
    P.emit("s", lambda e: e.activation(a_tmp[:], alog_sb[:], AF.Exp),
           waits=m_init, inc=SS)
    P.emit("v", lambda e: e.tensor_scalar(a_col[:], a_tmp[:], -1.0, None, ALU.mult),
           waits=[(SS, P.at(SS))], inc=SV)
    P.emit("t", lambda e: e.matmul(pb[0][0:128, 0:NH], ones_row[:], d_row[:],
                                   start=True, stop=True),
           waits=m_init, inc=ST)
    P.emit("v", lambda e: e.tensor_copy(db128[:], pb[0][0:128, 0:NH]),
           waits=[(ST, P.at(ST))], inc=SV)
    m_db = P.at(SV)
    for h in range(NH):
        P.emit("v", lambda e, h=h: e.tensor_scalar(
            di_sb[:, h * HD:(h + 1) * HD], ident[:], db128[:, h:h + 1], None, ALU.mult),
            waits=[(SV, m_db)] if h == 0 else (), inc=SV)

    xn_tok = arenaA
    lnsq_a, lnsq_b = expmt4, expr4

    def emit_LN_half(o, tsub):
        if True:
            ut = u_sb[:, tsub * DM:(tsub + 1) * DM]
            c0 = tsub * 4
            P.emit("v", lambda e, ut=ut, c0=c0: e.reduce_sum(
                ln_s1[:, c0:c0 + 1], ut, axis=AX.X),
                waits=[*m_u[o][tsub]], inc=SV)
            P.emit("s", lambda e, ut=ut: e.activation(
                lnsq_a[:], ut[:, 0:512], AF.Square),
                waits=[*m_u[o][tsub], (SV, P.at(SV))], inc=SS)
            P.emit("s", lambda e, ut=ut: e.activation(
                lnsq_b[:], ut[:, 512:1024], AF.Square), inc=SS)
            m_sq = P.at(SS)
            P.emit("v", lambda e, c0=c0: e.reduce_sum(
                ln_s1[:, c0 + 1:c0 + 2], lnsq_a[:], axis=AX.X),
                waits=[(SS, m_sq)], inc=SV)
            P.emit("v", lambda e, c0=c0: e.reduce_sum(
                ln_s1[:, c0 + 3:c0 + 4], lnsq_b[:], axis=AX.X), inc=SV)
            P.emit("v", lambda e, c0=c0: e.tensor_tensor(
                ln_s1[:, c0 + 1:c0 + 2], ln_s1[:, c0 + 1:c0 + 2],
                ln_s1[:, c0 + 3:c0 + 4], ALU.add), inc=SV)
            P.emit("v", lambda e, c0=c0: e.tensor_scalar(
                ln_s1[:, c0:c0 + 1], ln_s1[:, c0:c0 + 1], 1.0 / DM, None, ALU.mult),
                inc=SV)
            P.emit("v", lambda e, c0=c0: e.tensor_scalar(
                ln_s1[:, c0 + 1:c0 + 2], ln_s1[:, c0 + 1:c0 + 2], 1.0 / DM, EPS,
                ALU.mult, ALU.add), inc=SV)
            P.emit("v", lambda e, c0=c0: e.tensor_tensor(
                ln_s1[:, c0 + 2:c0 + 3], ln_s1[:, c0:c0 + 1], ln_s1[:, c0:c0 + 1],
                ALU.mult), inc=SV)
            P.emit("v", lambda e, c0=c0: e.tensor_tensor(
                ln_s1[:, c0 + 1:c0 + 2], ln_s1[:, c0 + 1:c0 + 2],
                ln_s1[:, c0 + 2:c0 + 3], ALU.subtract), inc=SV)
            P.emit("s", lambda e, c0=c0: e.activation(
                ln_s1[:, c0 + 3:c0 + 4], ln_s1[:, c0 + 1:c0 + 2], AF.Sqrt),
                waits=[(SV, P.at(SV))], inc=SS)
            P.emit("v", lambda e, c0=c0: e.reciprocal(
                ln_s1[:, c0 + 2:c0 + 3], ln_s1[:, c0 + 3:c0 + 4]),
                waits=[(SS, P.at(SS))], inc=SV)
            P.emit("v", lambda e, c0=c0: e.tensor_tensor(
                ln_s1[:, c0 + 3:c0 + 4], ln_s1[:, c0:c0 + 1],
                ln_s1[:, c0 + 2:c0 + 3], ALU.mult), inc=SV)
            P.emit("v", lambda e, ut=ut, tsub=tsub, c0=c0: e.tensor_scalar(
                xn_tok[:, tsub * DM:(tsub + 1) * DM], ut, ln_s1[:, c0 + 2:c0 + 3],
                ln_s1[:, c0 + 3:c0 + 4], ALU.mult, ALU.subtract),
                waits=[(ST, P.at(ST))] if tsub == 0 else (), inc=SV)
        return P.at(SV)

    def emit_LN_tp(m_xntok):
        for k in range(8):
            bk = pb[2 + (k % 2)]
            for tsub in range(2):
                P.emit("t", lambda e, k=k, tsub=tsub, bk=bk: e.transpose(
                    bk[0:128, tsub * 128:(tsub + 1) * 128],
                    xn_tok[:, tsub * DM + k * 128: tsub * DM + (k + 1) * 128],
                    ident[:]),
                    waits=[(SV, m_xntok), (SS, P.at(SS))] if (tsub == 0 and k < 2)
                    else ([(SV, P.at(SV))] if tsub == 0 else ()), inc=ST)
            m_tp = P.at(ST)
            P.emit("v", lambda e, k=k, bk=bk: e.tensor_copy(
                xn_f[:, k * T:(k + 1) * T], bk[0:128, 0:256]),
                waits=[(ST, m_tp)], inc=SV)
        return P.at(SV)

    NWG = 9
    PSCHED = [0, 0, 1, 1, 1, 1, 1, 1, 1, 1, 1, 1, 1, 1, 1, 1]
    m_prev_store = []
    m_prev_tensor_end = 0
    m_prev_gu_end = 0
    m_prev_ffev = 0
    m_prev_attmm = 0
    m_prev_guDE = 0
    m_prev_hv = 0
    m_prev_gu_ss = 0
    m_gu_start = 0

    m_u = [[m_init, m_init]] + [None] * n_outer

    GSEQ = [4, 5, 6, 7, 8, 0, 1, 2, 3]
    PCH = []
    for g_ in GSEQ:
        gw_ = 512 if g_ < 8 else JPROJ - 8 * 512
        PCH.append(("wi", g_, 0, 256))
        PCH.append(("wi", g_, 256, gw_ - 256))
    for k_ in range(16):
        PCH.append(("wo", k_, 0, 1024))
    NPCH = len(PCH)
    m_pcp = [0] * NPCH
    m_psb = [0] * NPCH
    m_pmm = [0] * NPCH
    pst_snap = [[] for _ in range(NPCH)]

    def pstag(c, coloff, w, lo):
        s3 = c % 3
        return wbuf16[:, s3 * 4096 + lo * 2048 + coloff:
                      s3 * 4096 + lo * 2048 + coloff + w]

    def emit_prep_chunk(i, ldwaits=()):
        kind, idx, j0, jw = PCH[i]
        ncols = 8 * jw if kind == "wi" else jw
        sf = att_sb if i % 2 == 0 else out_sb
        fv = sf[:, 0:ncols]
        hv = pstag(i, 0, ncols, 0)
        lv = pstag(i, 0, ncols, 1)
        if kind == "wi":
            j0a = idx * 512 + j0
            src = winT.rearrange("(k p) j -> p k j", p=128)[:, :, j0a:j0a + jw]
            dsth = win16h_h.rearrange("(k p) j -> p k j", p=128)[:, :, j0a:j0a + jw]
            dstl = win16l_h.rearrange("(k p) j -> p k j", p=128)[:, :, j0a:j0a + jw]
            fvr = fv.rearrange("p (k j) -> p k j", k=8)
            hvr = hv.rearrange("p (k j) -> p k j", k=8)
            lvr = lv.rearrange("p (k j) -> p k j", k=8)
        else:
            src = woutT.rearrange("(a p) n -> a p n", p=128)[idx]
            dsth = wout16h_h.rearrange("(a p) n -> a p n", p=128)[idx]
            dstl = wout16l_h.rearrange("(a p) n -> a p n", p=128)[idx]
            fvr, hvr, lvr = fv, hv, lv
        wld = list(ldwaits)
        if i >= 2:
            wld += [(SS, m_pcp[i - 2]), (SV, m_psb[i - 2])]
        ld = P.dma("y", "w", lambda e: e.dma_start(out=fvr, in_=src), waits=wld)
        wcp = [ld]
        if i >= 3:
            if m_pmm[i - 3] > 0:
                wcp.append((ST, m_pmm[i - 3]))
            wcp += pst_snap[i - 3]
        m_pcp[i] = P.emit("s", lambda e: e.copy(hv, fv), waits=wcp, inc=SS)
        m_psb[i] = P.emit("v", lambda e: e.tensor_tensor(
            lv, fv, hv, ALU.subtract), waits=[(SS, m_pcp[i])], inc=SV)
        s1 = P.dma("g", "p", lambda e: e.dma_start(out=dsth, in_=hvr),
                   waits=[(SS, m_pcp[i])])
        s2 = P.dma("g", "p", lambda e: e.dma_start(out=dstl, in_=lvr),
                   waits=[(SV, m_psb[i])])
        pst_snap[i] = [s1, s2]

    def emit_xnsplit(m_xnf_v, m_guend):
        for k in range(8):
            P.emit("s", lambda e, k=k: e.copy(
                xnh[:, k * T:(k + 1) * T], xn_f[:, k * T:(k + 1) * T]),
                waits=[(ST, m_guend), (SV, m_xnf_v)] if k == 0 else (),
                inc=SS)
            P.emit("v", lambda e, k=k: e.tensor_tensor(
                xnl[:, k * T:(k + 1) * T], xn_f[:, k * T:(k + 1) * T],
                xnh[:, k * T:(k + 1) * T], ALU.subtract),
                waits=[(SS, P.at(SS))], inc=SV)
        return P.at(SV)

    ln_state = {}
    xnsplit_state = {}
    for o in range(n_outer):
        ub = 0
        if o in xnsplit_state:
            m_xnsplit = xnsplit_state.pop(o)
            ln_state.pop(o, None)
        else:
            if o in ln_state:
                m_xnf = ln_state.pop(o)
            else:
                emit_LN_half(o, 0)
                m_xnf = emit_LN_tp(emit_LN_half(o, 1))
            m_xnsplit = emit_xnsplit(m_xnf, m_prev_gu_end)

        m_gu = [[None, None] for _ in range(6)]
        GUSLOTS = [wbuf16[:, 0:4096], wbuf16[:, 4096:8192],
                   wbufW[:, 0:4096], wbufW[:, 4096:8192],
                   wbufW[:, 8192:12288], wbufW[:, 12288:16384]]

        def emit_gu_dma(g, gu, waits):
            src = wg16_h if gu == 0 else wu16_h
            buf = GUSLOTS[(2 * g + gu) % 6]
            m_gu[g][gu] = [P.dma(
                "g", "g", lambda e, g=g, src=src, buf=buf: e.dma_start(
                    out=buf.rearrange("p (k j) -> p k j", k=8)[:, :, :],
                    in_=src.rearrange("(k p) j -> p k j", p=128)
                    [:, :, g * 512:(g + 1) * 512]),
                waits=waits)]

        if o > 0:
            emit_gu_dma(0, 0, [(ST, m_prev_gu_end)])
            emit_gu_dma(0, 1, [(ST, m_prev_gu_end)])

        m_wg = [None] * NWG
        m_csilu = [0] * 18

        def emit_win_dma(pos, waits):
            g = GSEQ[pos]
            gw = 512 if g < 8 else JPROJ - 8 * 512
            w2 = list(waits)
            if o == 1:
                w2 += pst_snap[2 * pos] + pst_snap[2 * pos + 1]
            dmas = []
            for lo, srcw, eng, pool in ((0, win16h_h, "y", "w"),
                                        (1, win16l_h, "g", "g")):
                buf = wslabW(pos % 2, lo)
                dmas.append(P.dma(
                    eng, pool, lambda e, g=g, gw=gw, buf=buf, srcw=srcw:
                    e.dma_start(
                        out=buf.rearrange("p (k j) -> p k j", k=8)[:, :, 0:gw],
                        in_=srcw.rearrange("(k p) j -> p k j", p=128)
                        [:, :, g * 512:g * 512 + gw]),
                    waits=w2))
            m_wg[g] = dmas

        def emit_conv(f):
            cacc = sg[:, (f % 2) * T:(f % 2) * T + T]
            P.emit("v", lambda e, f=f: e.tensor_copy(
                xbcpre(f)[:, 0:3], tail_sb[:, f * 3:(f + 1) * 3]), inc=SV)
            P.emit("v", lambda e, f=f, cacc=cacc: e.tensor_scalar(
                cacc, xbcpre(f)[:, 0:T],
                convw_sb[:, f * 4 + 0: f * 4 + 1], None, ALU.mult),
                waits=[(SS, m_csilu[f - 2])] if f >= 2 else (), inc=SV)
            for k in range(1, 4):
                P.emit("v", lambda e, f=f, k=k: e.tensor_scalar(
                    convtmp, xbcpre(f)[:, k:k + T],
                    convw_sb[:, f * 4 + k: f * 4 + k + 1], None, ALU.mult), inc=SV)
                P.emit("v", lambda e, cacc=cacc: e.tensor_tensor(
                    cacc, cacc, convtmp, ALU.add), inc=SV)
            m_cacc = P.at(SV)
            m_csilu[f] = P.emit("s", lambda e, f=f, cacc=cacc: e.activation(
                xbcc(f), cacc, AF.Silu, bias=convb_sb[:, f:f + 1]),
                waits=[(SV, m_cacc)], inc=SS)
            P.emit("v", lambda e, f=f: e.tensor_copy(
                tail_sb[:, f * 3:(f + 1) * 3], xbcpre(f)[:, T:T + 3]), inc=SV)

        ssm_units = []
        ssm_res = {}

        def build_prep_units(cc, svgate, ssgate):
            col = cc * CH
            tb0 = pb[2] if cc == 0 else pb[0]
            tb1 = pb[3] if cc == 0 else pb[1]
            res = {}
            U = []

            def u1():
                P.emit("t", lambda e: e.transpose(
                    tb0[0:128, 0:NH], dts[:, 2 * T + col:2 * T + col + CH],
                    ident[0:NH, 0:NH]),
                    waits=[(SV, svgate), (SS, P.at(SS))], inc=ST)
                res["t1"] = P.at(ST)
            U.append(u1)

            def u2():
                P.emit("v", lambda e: e.tensor_copy(dAT[:], tb0[0:128, 0:NH]),
                       waits=[(ST, res["t1"])], inc=SV)
                m_datv = P.at(SV)
                P.emit("t", lambda e: e.matmul(tb1[0:16, 0:CH], dAT[:, 0:NH],
                                               trilones[:],
                                               start=True, stop=True),
                       waits=[(SV, m_datv)], inc=ST)
                res["acs"] = P.at(ST)
            U.append(u2)

            def u3():
                P.emit("v", lambda e: e.tensor_copy(acs_h[:], tb1[0:16, 0:CH]),
                       waits=[(ST, res["acs"])], inc=SV)
                res["acshv"] = P.at(SV)
                P.emit("s", lambda e: e.activation(
                    dch[:], acs_h[:], AF.Exp, scale=-1.0,
                    bias=acs_h[:, CH - 1:CH]),
                    waits=[(SV, res["acshv"])], inc=SS)
                res["nacshv"] = P.emit("v", lambda e: e.tensor_tensor(
                    nacs_h[:], dts[:, 3 * T + col:3 * T + col + CH],
                    acs_h[:], ALU.subtract), inc=SV)
            U.append(u3)

            def u4():
                P.emit("v", lambda e: e.tensor_tensor(
                    sc_h[:], dch[:], dts[:, T + col:T + col + CH], ALU.mult),
                    waits=[(SS, P.at(SS))], inc=SV)
                m_sch = P.at(SV)
                P.emit("t", lambda e: e.transpose(tb1[0:128, NH:2 * NH],
                                                  sc_h[:], ident[0:NH, 0:NH]),
                       waits=[(SV, m_sch)], inc=ST)
                res["t2"] = P.at(ST)
            U.append(u4)

            def u5():
                P.emit("v", lambda e: e.tensor_copy(scT[:],
                                                    tb1[0:128, NH:2 * NH]),
                       waits=[(ST, res["t2"])], inc=SV)
            U.append(u5)

            def u6():
                w1 = [(SV, res["acshv"])]
                d1a = P.dma("y", "w", lambda e: e.dma_start(
                    out=acsf_d.rearrange("o (h q) -> (o h) q", q=CH),
                    in_=acs_h[:, :]), waits=w1)
                d2a = P.dma("y", "w", lambda e: e.dma_start(
                    out=nacsf_d.rearrange("o (h q) -> (o h) q", q=CH),
                    in_=nacs_h[:, :]), waits=[(SV, res["nacshv"])])
                w1b = [d1a]
                w2b = [d2a] + m_prev_store
                if o == 0 and cc == 0:
                    w1b += [(SS, m_pcp[16]), (SV, m_psb[16])]
                    w2b += [(SS, m_pcp[17]), (SV, m_psb[17])]
                d1b = P.dma("y", "w", lambda e: e.dma_start(
                    out=att_sb[0:1, 0:2048], in_=acsf_d[:, :]), waits=w1b)
                d2b = P.dma("y", "w", lambda e: e.dma_start(
                    out=out_sb[0:1, 0:2048], in_=nacsf_d[:, :]), waits=w2b)
                res["acsflat_w"] = [d1b, d2b]
            U.append(u6)

            def u10():
                P.emit("t", lambda e: e.matmul(
                    tb1[0:128, 2 * NH:3 * NH], ones_row[:],
                    att_sb[0:1, 0:2048].rearrange("o (h q) -> o h q", q=CH)
                    [:, :, CH - 1:CH], start=True, stop=True),
                    waits=list(res["acsflat_w"]), inc=ST)
                m_cdbt = P.at(ST)
                P.emit("s", lambda e: e.activation(
                    cdb128[:], tb1[0:128, 2 * NH:3 * NH], AF.Exp),
                    waits=[(ST, m_cdbt)], inc=SS)
                res["cdb"] = P.at(SS)
                P.emit("t", lambda e: e.transpose(
                    tb0[0:128, 128:256], xbcc(16)[:, col:col + CH], ident[:]),
                    waits=[(SS, ssgate), (SV, P.at(SV))], inc=ST)
                m_bt = P.at(ST)
                P.emit("v", lambda e: e.tensor_copy(btok[:],
                                                    tb0[0:128, 128:256]),
                       waits=[(ST, m_bt)], inc=SV)
            U.append(u10)

            def mk_xt(hs):
                def ux():
                    for h in hs:
                        bk = tb0 if h % 2 == 0 else tb1
                        P.emit("t", lambda e, h=h, bk=bk: e.transpose(
                            bk[0:128, 256:384], xbcc(h)[:, col:col + CH],
                            ident[:]), waits=[(SV, P.at(SV))], inc=ST)
                        m_xt = P.at(ST)
                        P.emit("v", lambda e, h=h, bk=bk: e.tensor_copy(
                            xtok[:, h * CH:(h + 1) * CH], bk[0:128, 256:384]),
                            waits=[(ST, m_xt)], inc=SV)
                return ux
            for hs in ([0, 1, 2, 3], [4, 5, 6, 7], [8, 9, 10, 11],
                       [12, 13, 14, 15]):
                U.append(mk_xt(hs))

            def u15():
                m_xtv = P.at(SV)
                P.emit("t", lambda e: e.matmul(
                    tb1[0:128, 384:512], xbcc(16)[:, col:col + CH],
                    xbcc(17)[:, col:col + CH], start=True, stop=True),
                    waits=[(SV, m_xtv), (SS, P.at(SS))], inc=ST)
                m_cbt = P.at(ST)
                P.emit("v", lambda e: e.tensor_tensor(
                    sg[:, 0:CH], tb1[0:128, 384:512], masktri[:], ALU.mult),
                    waits=[(ST, m_cbt)], inc=SV)
                res["prepv"] = P.at(SV)
            U.append(u15)
            return U, res

        if o == 0:
            for ci in range(3):
                emit_prep_chunk(ci)
        else:
            emit_win_dma(0, [(ST, m_prev_guDE)])
            emit_win_dma(1, [(ST, m_prev_gu_end)])

        bank_free = {i: (ST, 0) for i in range(4)}
        m_preS_sv = m_preS_ss = None
        zslot = 0
        for pos in range(NWG):
            g = GSEQ[pos]
            njt = 4 if g < 8 else 3
            for jj in range(njt):
                j = g * 4 + jj
                is_dt = (g == 8 and jj == 2)
                jw = 16 if is_dt else 128
                if pos == 0:
                    bank, bkey = pb[4 + jj], None
                elif pos >= 5:
                    bank, bkey = pb[jj % 2], jj % 2
                else:
                    bank, bkey = pb[j % 4], j % 4
                pview = bank[0:jw, 0:T]
                c = 2 * pos + (0 if jj < 2 else 1)
                if o == 0:
                    jwc = PCH[c][3]
                    joff = jj * 128 - PCH[c][2]

                    def wv(k, lo, c=c, jwc=jwc, joff=joff, jw=jw):
                        return pstag(c, k * jwc + joff, jw, lo)
                else:
                    def wv(k, lo, pos=pos, jj=jj, jw=jw):
                        return wslabW(pos % 2, lo)[:, k * 512 + jj * 128:
                                                   k * 512 + jj * 128 + jw]
                for k in range(8):
                    waits = []
                    if k == 0:
                        if o == 0:
                            waits += [(SS, m_pcp[c]), (SV, m_psb[c])]
                        else:
                            waits.extend(m_wg[g])
                        waits.append((SV, m_xnsplit))
                        if pos == 0:
                            if m_prev_hv > 0:
                                waits += [(SV, m_prev_hv), (SS, m_prev_gu_ss)]
                        else:
                            bf = bank_free[bkey]
                            if bf[1] > 0:
                                waits.append(bf)
                            elif m_prev_ffev > 0:
                                waits.append((SS, m_prev_ffev))
                    for pi, (lo, xs) in enumerate(((0, xnh), (0, xnl), (1, xnh))):
                        P.emit("t", lambda e, k=k, pi=pi, lo=lo, xs=xs, wv=wv,
                               pview=pview: e.matmul(
                            pview, wv(k, lo), xs[:, k * T:(k + 1) * T],
                            start=(k == 0 and pi == 0),
                            stop=(k == 7 and pi == 2)),
                            waits=waits if pi == 0 else (),
                            inc=(ST if (k == 7 and pi == 2) else None))
                        waits = []
                if o == 0:
                    m_pmm[c] = P.at(ST)
                m_mm = P.at(ST)
                if j < 16:
                    mev = P.emit("s", lambda e, j=j, pview=pview: e.activation(
                        szt(j), pview, AF.Silu), waits=[(ST, m_mm)], inc=SS)
                    bank_free[bkey] = (SS, mev)
                elif not is_dt:
                    f = j - 16
                    mev = P.emit("v", lambda e, f=f, pview=pview: e.tensor_copy(
                        xbcpre(f)[:, 3:3 + T], pview), waits=[(ST, m_mm)], inc=SV)
                    if pos > 0:
                        bank_free[bkey] = (SV, mev)
                    emit_conv(f)
                else:
                    mev = P.emit("v", lambda e, pview=pview: e.tensor_copy(
                        dts[:, 0:T], pview), waits=[(ST, m_mm)], inc=SV)
                    bank_free[bkey] = (SV, mev)
                if ssm_units:
                    if PSCHED[min(zslot, 15)]:
                        ssm_units.pop(0)()
                    zslot += 1
            if o == 0:
                for ci in (2 * pos + 3, 2 * pos + 4):
                    if ci < 18:
                        emit_prep_chunk(ci)
            elif pos + 2 < NWG:
                emit_win_dma(pos + 2, [(ST, P.at(ST))])
            if g == 8:
                m_dt_ev = P.at(SV)
                P.emit("s", lambda e: e.activation(dts[:, T:2 * T], dts[:, 0:T],
                                                   AF.Exp, bias=dtb_sb[:, 0:1]),
                       waits=[(SV, m_dt_ev)], inc=SS)
                P.emit("s", lambda e: e.activation(dts[:, T:2 * T],
                                                   dts[:, T:2 * T], AF.Ln,
                                                   bias=c_one[:, 0:1]), inc=SS)
                P.emit("s", lambda e: e.activation(dts[:, 3 * T:4 * T],
                                                   dts[:, T:2 * T], AF.Ln,
                                                   bias=c_tiny[:, 0:1]), inc=SS)
                m_dt_ss = P.at(SS)
                P.emit("v", lambda e: e.tensor_scalar(dts[:, 2 * T:3 * T],
                                                      dts[:, T:2 * T],
                                                      a_col[:, 0:1], None,
                                                      ALU.mult),
                       waits=[(SS, m_dt_ss)], inc=SV)
                m_preS_sv = P.at(SV)
                m_preS_ss = P.at(SS)
                if o > 0:
                    us0, r0 = build_prep_units(0, m_preS_sv, m_preS_ss)
                    ssm_units.extend(us0)
                    ssm_res[0] = r0
        while ssm_units:
            ssm_units.pop(0)()
        m_conv_s = m_preS_ss
        m_conv_v = m_preS_sv
        m_inproj_t = P.at(ST)

        dbg_gate = [[]]
        m_ssq_ev = 0
        m_prev_ssqfin = 0
        for cc in range(2):
            col = cc * CH

            if cc == 0 and o > 0:
                res = ssm_res[0]
            else:
                usx, res = build_prep_units(cc, max(m_preS_sv, m_ssq_ev),
                                            m_conv_s)
                for u_ in usx:
                    u_()
            m_acsflat_w = res["acsflat_w"]
            m_cdb = res["cdb"]
            m_prepv = res["prepv"]
            acs_flat = att_sb
            nacs_flat = out_sb

            NG = 4
            m_bankA = [None] * NG
            m_bankC = [None] * NG
            m_e4 = [None] * NG
            m_expmt4 = [None] * NG
            m_expr4 = [None] * NG
            m_mtc = [None] * NG
            m_bsc4 = [None] * NG
            m_Ylast = [None] * NG
            m_stlast = [None] * NG
            m_cmlast = [None] * NG
            m_y14 = [None] * NG
            m_ysq4 = [None] * NG
            m_ynw4 = [None] * NG
            m_ssqg = [None] * NG
            m_Y = [None] * NH

            mcbt_rep = bass.AP(sg, 0, [[512, 128], [0, 4], [1, 128]])
            c_rep = bass.AP(actB, 4096 + 17 * T + col, [[8704, 128], [0, 4], [1, 128]])

            def emit_bank(g):
                bankA = pb[2 + (g % 2)]
                ybank = pb[4 + (g % 2)]
                wA = list(m_acsflat_w)
                if g < 2:
                    wA.append((SV, m_prepv))
                if g >= 2:
                    wA.append((SS, m_expr4[g - 2]))
                m_bankA[g] = P.emit("t", lambda e, g=g, bankA=bankA: e.matmul(
                    bankA[0:128, 0:512], ones_row[:],
                    acs_flat[0:1, g * 512:(g + 1) * 512],
                    start=True, stop=True), waits=wA, inc=ST)
                wC = [(SV, m_y14[g - 2] if g >= 2 else P.at(SV))]
                P.emit("t", lambda e, g=g, ybank=ybank: e.matmul(
                    ybank[0:128, 0:512], ones_row[:],
                    acs_flat[0:1, g * 512:(g + 1) * 512],
                    start=True, stop=False), waits=wC)
                for hh in range(4):
                    h = 4 * g + hh
                    P.emit("t", lambda e, h=h, hh=hh, ybank=ybank: e.matmul(
                        ybank[0:128, hh * CH:(hh + 1) * CH],
                        nacs_flat[0:1, h * CH:(h + 1) * CH], ones_row[:],
                        start=False, stop=(hh == 3)),
                        inc=(ST if hh == 3 else None))
                m_bankC[g] = P.at(ST)

            def emit_prep_group(g):
                ybank = pb[4 + (g % 2)]
                bankA = pb[2 + (g % 2)]
                w = [(ST, m_bankC[g])]
                if g >= 1:
                    w.append((SS, m_expmt4[g - 1]))
                if dbg_gate[0] and g == 2:
                    w.extend(dbg_gate[0]); dbg_gate[0] = []
                m_e4[g] = P.emit("v", lambda e, ybank=ybank: e.tensor_scalar(
                    expmt4[:], ybank[0:128, 0:512], 80.0, None, ALU.min),
                    waits=w, inc=SV)
                m_expmt4[g] = P.emit("s", lambda e: e.activation(
                    expmt4[:], expmt4[:], AF.Exp), waits=[(SV, m_e4[g])], inc=SS)
                m_expr4[g] = P.emit("s", lambda e, bankA=bankA: e.activation(
                    expr4[:], bankA[0:128, 0:512], AF.Exp),
                    waits=[(SV, m_e4[g])], inc=SS)
                for hh in range(4):
                    h = 4 * g + hh
                    P.emit("s", lambda e, h=h, hh=hh: e.mul(
                        bsc[:, hh * CH:(hh + 1) * CH], btok[:], scT[:, h:h + 1]),
                        waits=[(SV, P.at(SV)),
                               (ST, m_stlast[g - 1] if g >= 1 else P.at(ST))]
                        if hh == 0 else (), inc=SS)
                m_bsc4[g] = P.at(SS)
                mw = [(SS, m_expmt4[g])]
                if g >= 1:
                    mw.append((ST, m_Ylast[g - 1]))
                P.emit("v", lambda e: e.tensor_tensor(
                    mt_sb[:], mcbt_rep, expmt4[:], ALU.mult), waits=mw, inc=SV)
                P.emit("v", lambda e, g=g: e.tensor_tensor(
                    mt_sb[:], mt_sb[:], di_sb[:, g * 512:(g + 1) * 512], ALU.add),
                    inc=SV)
                cw = [(SS, m_expr4[g])]
                if g >= 1:
                    cw.append((ST, m_Ylast[g - 1]))
                P.emit("v", lambda e, c_rep=c_rep: e.tensor_tensor(
                    cst[:], c_rep, expr4[:], ALU.mult), waits=cw, inc=SV)
                m_mtc[g] = P.at(SV)
                if dbg and o == n_outer - 1 and cc == 0 and g == 1:
                    dbg_gate[0] = [
                        P.dma("y", "o", lambda e, nm=nm, srcb=srcb: e.dma_start(
                            out=dbg_h[nm][:, :], in_=srcb[:]),
                            waits=[(SV, m_mtc[g]), (SS, m_expr4[g])])
                        for nm, srcb in [("d_mt", mt_sb), ("d_cst", cst),
                                         ("d_em", expmt4), ("d_er", expr4)]]

            emit_bank(0)
            emit_prep_group(0)
            for g in range(NG):
                ybank = pb[4 + (g % 2)]
                sbank = pb[6 + (g % 2)]
                if g + 1 < NG:
                    emit_bank(g + 1)
                for hh in range(4):
                    h = 4 * g + hh
                    P.emit("t", lambda e, h=h, hh=hh, ybank=ybank: e.matmul(
                        ybank[0:128, hh * CH:(hh + 1) * CH],
                        xtok[:, h * CH:(h + 1) * CH],
                        mt_sb[:, hh * CH:(hh + 1) * CH], start=True, stop=False),
                        waits=[(SV, m_mtc[g])] if hh == 0 else ())
                    m_Y[h] = P.emit("t", lambda e, h=h, hh=hh, ybank=ybank: e.matmul(
                        ybank[0:128, hh * CH:(hh + 1) * CH],
                        carry[:, h * HD:(h + 1) * HD],
                        cst[:, hh * CH:(hh + 1) * CH], start=False, stop=True),
                        inc=ST)
                m_Ylast[g] = m_Y[4 * g + 3]
                for hh in range(4):
                    h = 4 * g + hh
                    P.emit("t", lambda e, h=h, hh=hh, sbank=sbank: e.matmul(
                        sbank[0:128, hh * CH:(hh + 1) * CH],
                        bsc[:, hh * CH:(hh + 1) * CH],
                        xtok[:, h * CH:(h + 1) * CH], start=(hh == 0),
                        stop=(hh == 3)),
                        waits=[(SS, m_bsc4[g]), (SV, P.at(SV))] if hh == 0 else (),
                        inc=(ST if hh == 3 else None))
                m_stlast[g] = P.at(ST)
                if g >= 1:
                    for hh in range(4):
                        hq = 4 * (g - 1) + hh
                        P.emit("t", lambda e, g=g, hh=hh: e.matmul(
                            pb[0][0:1, 384:384 + CH], ones_col[:],
                            ysq[:, hh * CH:(hh + 1) * CH],
                            start=(g == 1 and hh == 0), stop=False),
                            waits=[(SV, m_ysq4[g - 1])] if hh == 0 else (),
                            inc=(ST if hh == 3 else None))
                    m_ssqg[g - 1] = P.at(ST)
                if g + 1 < NG:
                    emit_prep_group(g + 1)
                for hh in range(4):
                    h = 4 * g + hh
                    P.emit("s", lambda e, h=h: e.mul(
                        carry[:, h * HD:(h + 1) * HD],
                        carry[:, h * HD:(h + 1) * HD], cdb128[:, h:h + 1]),
                        waits=[(ST, m_Y[h]), (SS, m_cdb)] if hh == 0 else
                              [(ST, m_Y[h])], inc=SS)
                m_cmlast[g] = P.at(SS)
                sz_ap = bass.AP(actB, 4 * g * T + col, [[8704, 128], [T, 4], [1, 128]])
                m_y14[g] = P.emit("v", lambda e, g=g, ybank=ybank, sz_ap=sz_ap:
                                  e.tensor_tensor(
                    y1_sb[:], ybank[0:128, 0:512], sz_ap, ALU.mult),
                    waits=[(ST, m_Ylast[g]),
                           (SS, m_ynw4[g - 1] if g >= 1 else 0)], inc=SV)
                m_ysq4[g] = P.emit("v", lambda e: e.tensor_tensor(
                    ysq[:], y1_sb[:], y1_sb[:], ALU.mult),
                    waits=[(ST, m_ssqg[g - 1] if g >= 1 else m_prev_ssqfin)],
                    inc=SV)
                P.emit("v", lambda e, g=g, sbank=sbank: e.tensor_tensor(
                    carry[:, g * 512:(g + 1) * 512],
                    carry[:, g * 512:(g + 1) * 512],
                    sbank[0:128, 0:512], ALU.add),
                    waits=[(ST, m_stlast[g]), (SS, m_cmlast[g])], inc=SV)
                for hh in range(4):
                    h = 4 * g + hh
                    P.emit("s", lambda e, h=h, hh=hh, col=col: e.mul(
                        ynw(h)[:, col:col + CH], y1_sb[:, hh * CH:(hh + 1) * CH],
                        nw_sb[:, h:h + 1]),
                        waits=[(SV, m_y14[g])] if hh == 0 else (), inc=SS)
                m_ynw4[g] = P.at(SS)
            for hh in range(4):
                P.emit("t", lambda e, hh=hh: e.matmul(
                    pb[0][0:1, 384:384 + CH], ones_col[:],
                    ysq[:, hh * CH:(hh + 1) * CH],
                    start=False, stop=(hh == 3)),
                    waits=[(SV, m_ysq4[NG - 1])] if hh == 0 else (),
                    inc=(ST if hh == 3 else None))
            m_ssqfin = P.at(ST)
            m_prev_ssqfin = m_ssqfin
            m_ssq_ev = P.emit("v", lambda e, col=col: e.tensor_copy(
                ssqrow[0:1, col:col + CH], pb[0][0:1, 384:384 + CH]),
                waits=[(ST, m_ssqfin)], inc=SV)
        m_ssm_v = P.at(SV)
        m_ssm_s = P.at(SS)

        if dbg and o == n_outer - 1:
            P.dma("s", "o", lambda e: e.dma_start(
                out=dbg_h["d_carry"][:, :], in_=carry[:]),
                waits=[(SV, m_ssm_v)])

        P.emit("v", lambda e: e.tensor_scalar(rstdrow[0:1, :], ssqrow[0:1, :],
                                              1.0 / DI, EPS, ALU.mult, ALU.add),
               waits=[(SV, m_ssm_v)], inc=SV)
        P.emit("s", lambda e: e.activation(rstdrow[0:1, :], rstdrow[0:1, :], AF.Sqrt),
               waits=[(SV, P.at(SV))], inc=SS)
        P.emit("v", lambda e: e.reciprocal(rstdrow[0:1, :], rstdrow[0:1, :]),
               waits=[(SS, P.at(SS))], inc=SV)
        m_rstdr = P.at(SV)

        m_ssm_t = P.at(ST)
        m_wo = [None] * 16

        def emit_wo_dma(k, waits):
            w2 = list(waits)
            if o == 1:
                w2 += pst_snap[18 + k]
            m_wo[k] = [P.dma(
                "y", "w", lambda e, k=k, lo=lo, srcw=srcw: e.dma_start(
                    out=wqW(k % 8, lo),
                    in_=srcw.rearrange("(a p) n -> a p n", p=128)[k, :, :]),
                waits=w2)
                for lo, srcw in ((0, wout16h_h), (1, wout16l_h))]

        m_ysp = [0] * 16

        def emit_ynw_split(k, extra=()):
            r = (k % 2) * T
            P.emit("v", lambda e, k=k, r=r: e.tensor_copy(
                hhb[:, r:r + T], ynw(k)), waits=extra, inc=SV)
            m_ysp[k] = P.emit("v", lambda e, k=k, r=r: e.tensor_tensor(
                hlb[:, r:r + T], ynw(k), hhb[:, r:r + T], ALU.subtract),
                inc=SV)

        if o == 0:
            emit_prep_chunk(18, ldwaits=[(ST, m_ssm_t)])
            emit_prep_chunk(19, ldwaits=[(ST, m_ssm_t)])
        else:
            for k in range(8):
                emit_wo_dma(k, [(ST, m_inproj_t)] if k < 2 else ())
        emit_ynw_split(0, extra=[(SS, m_ssm_s), (ST, m_ssm_t)])
        emit_ynw_split(1)
        for k in range(16):
            if o == 0:
                waits = [(SS, m_pcp[18 + k]), (SV, m_psb[18 + k])]
                s3w = (18 + k) % 3
                wqh = wbuf16[:, s3w * 4096:s3w * 4096 + 1024]
                wql = wbuf16[:, s3w * 4096 + 2048:s3w * 4096 + 3072]
            else:
                waits = list(m_wo[k])
                wqh = wqW(k % 8, 0)
                wql = wqW(k % 8, 1)
            waits.append((SV, m_ysp[k]))
            if k == 0:
                waits += [(SV, m_ssm_v), (SS, m_ssm_s)]
            for tsub in range(2):
                bks = (pb[0], pb[1]) if tsub == 0 else (pb[2], pb[3])
                r = (k % 2) * T + tsub * 128
                yh = hhb[:, r:r + 128]
                yl = hlb[:, r:r + 128]
                seq = [(yh, wqh, 0), (yh, wqh, 1), (yh, wql, 0),
                       (yh, wql, 1), (yl, wqh, 0), (yl, wqh, 1)]
                for si, (ys, ws, nn_) in enumerate(seq):
                    P.emit("t", lambda e, k=k, si=si, ys=ys, ws=ws, nn_=nn_,
                           bks=bks: e.matmul(
                        bks[nn_][0:128, 0:512], ys,
                        ws[:, nn_ * 512:(nn_ + 1) * 512],
                        start=(k == 0 and si < 2),
                        stop=(k == 15 and si >= 4)),
                        waits=waits if (tsub == 0 and si == 0) else (),
                        inc=(ST if (tsub == 1 and si == 5) else None))
                    waits = []
            if o == 0:
                m_pmm[18 + k] = P.at(ST)
                if 20 + k < NPCH:
                    emit_prep_chunk(20 + k)
            elif k + 8 < 16:
                emit_wo_dma(k + 8, [(ST, P.at(ST))])
            if k + 2 < 16:
                emit_ynw_split(k + 2, extra=[(ST, P.at(ST))])
            if k == 3:
                for tsub in range(2):
                    P.emit("t", lambda e, tsub=tsub: e.transpose(
                        pb[4][0:128, tsub:tsub + 1],
                        rstdrow[0:1, tsub * 128:(tsub + 1) * 128], ident[0:1, 0:1]),
                        waits=[(SV, m_rstdr)] if tsub == 0 else (), inc=ST)
                m_rstdt = P.at(ST)
                P.emit("v", lambda e: e.tensor_copy(
                    rstd_col[:, 0:2], pb[4][0:128, 0:2]),
                    waits=[(ST, m_rstdt)], inc=SV)
                m_rstdc = P.at(SV)
        m_att_mm = P.at(ST)
        m_prev_attmm = m_att_mm
        for tsub in range(2):
            bks = (pb[0], pb[1]) if tsub == 0 else (pb[2], pb[3])
            wev = [(ST, m_att_mm)] if tsub == 0 else []
            if o == 0 and tsub == 0:
                wev += [(SS, m_pcp[32]), (SV, m_psb[32])]
            for nn_ in range(2):
                P.emit("v", lambda e, tsub=tsub, bks=bks, nn_=nn_, wev=wev:
                       e.scalar_tensor_tensor(
                    att_sb[:, tsub * DM + nn_ * 512:tsub * DM + nn_ * 512 + 512],
                    bks[nn_][0:128, 0:512], rstd_col[:, tsub:tsub + 1],
                    u_sb[:, tsub * DM + nn_ * 512:tsub * DM + nn_ * 512 + 512],
                    ALU.mult, ALU.add),
                    waits=wev if nn_ == 0 else (), inc=SV)
                wev = []
        m_att_v = P.at(SV)
        m_u_done = P.at(SV)
        if o + 1 < n_outer:
            m_u[o + 1] = [[P.dma("y", "w", lambda e, o=o, tsub=tsub: e.dma_start(
                out=u_sb[:, tsub * DM:(tsub + 1) * DM],
                in_=u_h.rearrange("(a p) n -> p a n", p=128)
                [:, 2 * (o + 1) + tsub, :]),
                waits=[(SV, m_u_done)])] for tsub in range(2)]
            emit_LN_half(o + 1, 0)
            ln_half_pending = True
        else:
            ln_half_pending = False
        ln_xntok_next = None

        m_dbgst = []
        if dbg and o == n_outer - 1:
            for nm, src in [("d_xnf", xn_f[:]), ("d_sz", actB[:, 0:4096]),
                            ("d_xbcc", actB[:, 4096:8704]), ("d_dts", dts[:]),
                            ("d_ynw", arenaA[:, 0:4096]), ("d_att", att_sb[:])]:
                P.dma("s", "o", lambda e, nm=nm, src=src: e.dma_start(
                    out=dbg_h[nm][:, :], in_=src),
                    waits=[(SV, m_att_v)])
            m_dbgst = P.snap("o")

        gu0_extra = (m_prep + P.snap("p") + [(ST, P.at(ST))]) if o == 0 else []
        if o == 0:
            emit_gu_dma(0, 0, gu0_extra)
            emit_gu_dma(0, 1, gu0_extra)
        emit_gu_dma(1, 0, [(ST, P.at(ST))] + gu0_extra)
        emit_gu_dma(1, 1, [(ST, P.at(ST))] + gu0_extra)

        m_silu = [0] * 24
        m_hev = [0] * 24
        for g in range(6):
            for jj in range(4):
                j = g * 4 + jj
                for gu in range(2):
                    buf = GUSLOTS[(2 * g + gu) % 6]
                    bank = pb[4 + (j % 2)] if gu == 0 else pb[6 + (j % 2)]
                    for k in range(8):
                        waits = []
                        if k == 0:
                            waits.extend(m_gu[g][gu])
                            if gu == 0:
                                waits.append((SS, m_silu[j - 2]) if j >= 2
                                             else (SV, m_att_v))
                            else:
                                waits.append((SV, m_hev[j - 2]) if j >= 2
                                             else (SV, m_att_v))
                        for xsrc in (xnh, xnl):
                            P.emit("t", lambda e, buf=buf, k=k, jj=jj, bank=bank,
                                   xsrc=xsrc: e.matmul(
                                bank[0:128, 0:T],
                                buf[:, k * 512 + jj * 128:k * 512 + (jj + 1) * 128],
                                xsrc[:, k * T:(k + 1) * T],
                                start=(k == 0 and xsrc is xnh),
                                stop=(k == 7 and xsrc is xnl)),
                                waits=waits if xsrc is xnh else (),
                                inc=(ST if (k == 7 and xsrc is xnl) else None))
                            waits = []
                    m_mm = P.at(ST)
                    if gu == 0:
                        w2 = [(ST, m_mm)]
                        if j >= 2:
                            w2.append((SV, m_hev[j - 2]))
                        m_silu[j] = P.emit("s", lambda e, j=j, bank=bank: e.activation(
                            sg[:, (j % 2) * T:(j % 2) * T + T], bank[0:128, 0:T],
                            AF.Silu, scale=QS), waits=w2, inc=SS)
                    else:
                        m_hev[j] = P.emit("v", lambda e, j=j, bank=bank: e.tensor_tensor(
                            hffn(j), bank[0:128, 0:T],
                            sg[:, (j % 2) * T:(j % 2) * T + T], ALU.mult),
                            waits=[(ST, m_mm), (SS, m_silu[j])]
                            + (m_dbgst if j == 0 else []), inc=SV)
            for pn in (2 * g + 4, 2 * g + 5):
                if pn < 12:
                    emit_gu_dma(pn // 2, pn % 2, [(ST, P.at(ST))])
            if g == 1 and ln_half_pending:
                ln_xntok_next = emit_LN_half(o + 1, 1)
            if g == 2 and ln_xntok_next is not None:
                ln_state[o + 1] = emit_LN_tp(ln_xntok_next)
            if g == 4:
                m_guDE = P.at(ST)
        m_h_v = P.at(SV)
        m_prev_hv = m_h_v
        m_prev_gu_ss = P.at(SS)
        m_prev_guDE = m_guDE
        m_prev_gu_end = P.at(ST)
        if o + 1 < n_outer and (o + 1) in ln_state:
            xnsplit_state[o + 1] = emit_xnsplit(ln_state[o + 1],
                                                m_prev_gu_end)

        m_wd = [None] * 24

        def emit_wd_dma(k, waits):
            m_wd[k] = [P.dma("g", "g", lambda e, k=k: e.dma_start(
                out=wbuf16[:, 8192 + (k % 4) * 1024:8192 + (k % 4) * 1024 + 1024],
                in_=wd16_h.rearrange("(a p) n -> a p n", p=128)[k, :, :]),
                waits=waits)]

        for k in range(4):
            emit_wd_dma(k, ([(ST, m_gu_start)]
                            + ((m_prep + P.snap("p")) if o == 0 else []))
                        if k == 0 else ())
        m_hsp = [0] * 24
        m_dnk = [0] * 24
        for k in range(24):
            hc = (k % 2) * T
            wsp = [(SV, m_hev[k])] if False else []
            if k >= 2:
                wsp.append((ST, m_dnk[k - 2]))
            P.emit("v", lambda e, k=k, hc=hc: e.tensor_copy(
                hhb[:, hc:hc + T], hffn(k)), waits=wsp, inc=SV)
            m_hsp[k] = P.emit("v", lambda e, k=k, hc=hc: e.tensor_tensor(
                hlb[:, hc:hc + T], hffn(k), hhb[:, hc:hc + T], ALU.subtract),
                inc=SV)
            waits = list(m_wd[k]) + [(SV, m_hsp[k])]
            if k == 0:
                waits += [(SV, m_h_v), (SV, m_att_v)]
            for tsub in range(2):
                bks = (pb[0], pb[1]) if tsub == 0 else (pb[2], pb[3])
                for nn_ in range(2):
                    for hsrc in (hhb, hlb):
                        P.emit("t", lambda e, k=k, hc=hc, tsub=tsub, nn_=nn_,
                               bks=bks, hsrc=hsrc: e.matmul(
                            bks[nn_][0:128, 0:512],
                            hsrc[:, hc + tsub * 128:hc + (tsub + 1) * 128],
                            wbuf16[:, 8192 + (k % 4) * 1024 + nn_ * 512:
                                   8192 + (k % 4) * 1024 + (nn_ + 1) * 512],
                            start=(k == 0 and hsrc is hhb),
                            stop=(k == 23 and hsrc is hlb)),
                            waits=waits if (tsub == 0 and nn_ == 0 and hsrc is hhb)
                            else (),
                            inc=(ST if (tsub == 1 and nn_ == 1 and hsrc is hlb)
                                 else None))
            m_dnk[k] = P.at(ST)
            if k + 4 < 24:
                emit_wd_dma(k + 4, [(ST, P.at(ST))])
        m_dn = P.at(ST)
        for tsub in range(2):
            bks = (pb[0], pb[1]) if tsub == 0 else (pb[2], pb[3])
            w1 = [(ST, m_dn)]
            if tsub == 1:
                w1.append((SV, P.at(SV)))
            P.emit("s", lambda e, bks=bks: e.mul(
                ffacc[:, 0:512], bks[0][0:128, 0:512], QS * QS), waits=w1, inc=SS)
            P.emit("s", lambda e, bks=bks: e.mul(
                ffacc[:, 512:1024], bks[1][0:128, 0:512], QS * QS), inc=SS)
            m_ff = P.at(SS)
            wout_w = [(SS, m_ff)] + (m_prev_store if tsub == 0 else [])
            if o == 0 and tsub == 0:
                wout_w += [(SS, m_pcp[33]), (SV, m_psb[33])]
            P.emit("v", lambda e, tsub=tsub, wout_w=wout_w: e.tensor_tensor(
                out_sb[:, tsub * DM:(tsub + 1) * DM], ffacc[:],
                att_sb[:, tsub * DM:(tsub + 1) * DM], ALU.add),
                waits=wout_w, inc=SV)
            m_out_v = P.at(SV)
            P.dma("s", "o", lambda e, o=o, tsub=tsub: e.dma_start(
                out=out_h.rearrange("(a p) n -> p a n", p=128)[:, 2 * o + tsub, :],
                in_=out_sb[:, tsub * DM:(tsub + 1) * DM]),
                waits=[(SV, m_out_v)])
        m_prev_store = P.snap("o")
        m_prev_tensor_end = P.at(ST)
        m_gu_start = m_prev_tensor_end
        m_prev_ffev = P.at(SS)

    final_waits = [(k, v) for k, v in P.cnt.items()
                   if v > 0 and k not in (ST, SV, SS)]

    import contextlib
    with contextlib.ExitStack() as stack:
        sems = {k: stack.enter_context(nc.semaphore(f"sem_{k}"))
                for k in P.cnt}
        block = stack.enter_context(nc.Block())

        own_sem = {"v": SV, "s": SS, "g": SG}

        def runner(eng_key):
            def _run(eng):
                own = own_sem.get(eng_key)
                prev = 0
                for waits, fn, inc, inc_by in P.q[eng_key]:
                    if own is not None and prev > 0:
                        eng.wait_ge(sems[own], prev)
                    for sem_key, val in waits:
                        if val and val > 0:
                            eng.wait_ge(sems[sem_key], val)
                    ins = fn(eng)
                    if inc is not None and ins is not None:
                        ins.then_inc(sems[inc], inc_by)
                    if inc is not None and inc == own:
                        prev += inc_by
                if eng_key == "y":
                    for sem_key, val in final_waits:
                        eng.wait_ge(sems[sem_key], val)
            return _run

        block.tensor(runner("t"))
        block.vector(runner("v"))
        block.scalar(runner("s"))
        block.sync(runner("y"))
        block.gpsimd(runner("g"))

    for g in reversed(ctx):
        g.__exit__(None, None, None)
    return nc


def make_in_maps(inputs, n_cores=8):
    c = np.ascontiguousarray
    base = {
        "winT": c(np.asarray(inputs["L0_attn_W_in"], dtype=np.float32).T),
        "convw": c(np.asarray(inputs["L0_attn_conv_w"], dtype=np.float32)
                   .reshape(18, 128, 4).transpose(1, 0, 2).reshape(128, 72)),
        "convb": c(np.asarray(inputs["L0_attn_conv_b"], dtype=np.float32)
                   .reshape(18, 128).T),
        "dtb": c(np.asarray(inputs["L0_attn_dt_bias"], dtype=np.float32)
                 .reshape(NH, 1)),
        "alog": c(np.asarray(inputs["L0_attn_A_log"], dtype=np.float32)
                  .reshape(NH, 1)),
        "dvec": c(np.asarray(inputs["L0_attn_D"], dtype=np.float32).reshape(1, NH)),
        "nw": c(np.asarray(inputs["L0_attn_norm_weight"], dtype=np.float32)
                .reshape(NH, 128).T),
        "woutT": c(np.asarray(inputs["L0_attn_W_out"], dtype=np.float32).T),
        "wgT": c(np.asarray(inputs["L0_ffn_w_gate"], dtype=np.int32).T),
        "wuT": c(np.asarray(inputs["L0_ffn_w_up"], dtype=np.int32).T),
        "wdT": c(np.asarray(inputs["L0_ffn_w_down"], dtype=np.int32).T),
    }
    u = np.asarray(inputs["L0_attn_u"], dtype=np.float32)
    maps = []
    for i in range(n_cores):
        m = dict(base)
        m["u"] = c(u[i])
        maps.append(m)
    return maps


def kernel(**inputs):
    nc = build_nc(n_outer=8)
    in_maps = make_in_maps(inputs, n_cores=8)
    res = run_bass_kernel_spmd(nc, in_maps, core_ids=list(range(8)))
    return np.stack([np.asarray(r["out"], dtype=np.float32) for r in res.results],
                    axis=0)


# version 30
# speedup vs baseline: 1.0057x; 1.0057x over previous
import numpy as np
import concourse.bass as bass
import concourse.mybir as mybir
from concourse.bass_utils import run_bass_kernel_spmd

F32 = mybir.dt.float32
BF16 = mybir.dt.bfloat16
I32 = mybir.dt.int32
AF = mybir.ActivationFunctionType
ALU = mybir.AluOpType
AX = mybir.AxisListType

DM = 1024
DI = 2048
NH = 16
HD = 128
CONVD = 2304
JPROJ = 4368
HID = 3072
CH = 128
T = 256
EPS = 1e-5
QS = 1.0 / 127.0
NEG = -1e9

SW, GW, ST, SV, SS, SO = "sw", "gw", "st", "sv", "ss", "so"
SG = "sg"


class Prog:
    ENGS = ("t", "v", "s", "y", "g")
    POOLS = {"w": ["w0", "w1", "w2", "w3"],
             "g": ["g0", "g1", "g2", "g3", "g4", "g5"],
             "o": ["o0", "o1"], "p": ["p0", "p1", "p2", "p3"]}

    def __init__(self):
        self.q = {e: [] for e in self.ENGS}
        keys = [ST, SV, SS, SG] + [s for p in self.POOLS.values() for s in p]
        self.cnt = {k: 0 for k in keys}
        self.rr = {p: 0 for p in self.POOLS}

    def emit(self, eng, fn, waits=(), inc=None, inc_by=1):
        self.q[eng].append((tuple(waits), fn, inc, inc_by))
        if inc is not None:
            self.cnt[inc] += inc_by
            return self.cnt[inc]
        return None

    def at(self, sem):
        return self.cnt[sem]

    def dma(self, eng, pool, fn, waits=()):
        sems = self.POOLS[pool]
        s = sems[self.rr[pool] % len(sems)]
        self.rr[pool] += 1
        w = list(waits)
        if self.cnt[s] > 0:
            w.append((s, self.cnt[s]))
        v = self.emit(eng, fn, waits=w, inc=s, inc_by=16)
        return (s, v)

    def snap(self, pool):
        return [(s, self.cnt[s]) for s in self.POOLS[pool] if self.cnt[s] > 0]


def build_nc(n_outer=8, dbg=False):
    L = n_outer * T
    nc = bass.Bass("TRN2", target_bir_lowering=False)

    u_h = nc.dram_tensor("u", [L, DM], F32, kind="ExternalInput").ap()
    winT = nc.dram_tensor("winT", [DM, JPROJ], F32, kind="ExternalInput").ap()
    convw_h = nc.dram_tensor("convw", [128, 18 * 4], F32, kind="ExternalInput").ap()
    convb_h = nc.dram_tensor("convb", [128, 18], F32, kind="ExternalInput").ap()
    dtb_h = nc.dram_tensor("dtb", [NH, 1], F32, kind="ExternalInput").ap()
    alog_h = nc.dram_tensor("alog", [NH, 1], F32, kind="ExternalInput").ap()
    dvec_h = nc.dram_tensor("dvec", [1, NH], F32, kind="ExternalInput").ap()
    nw_h = nc.dram_tensor("nw", [128, NH], F32, kind="ExternalInput").ap()
    woutT = nc.dram_tensor("woutT", [DI, DM], F32, kind="ExternalInput").ap()
    wgT = nc.dram_tensor("wgT", [DM, HID], I32, kind="ExternalInput").ap()
    wuT = nc.dram_tensor("wuT", [DM, HID], I32, kind="ExternalInput").ap()
    wdT = nc.dram_tensor("wdT", [HID, DM], I32, kind="ExternalInput").ap()
    out_h = nc.dram_tensor("out", [L, DM], F32, kind="ExternalOutput").ap()
    wg16_h = nc.dram_tensor("wg16", [DM, HID], BF16, kind="Internal").ap()
    wu16_h = nc.dram_tensor("wu16", [DM, HID], BF16, kind="Internal").ap()
    wd16_h = nc.dram_tensor("wd16", [HID, DM], BF16, kind="Internal").ap()
    win16h_h = nc.dram_tensor("win16h", [DM, JPROJ], BF16, kind="Internal").ap()
    win16l_h = nc.dram_tensor("win16l", [DM, JPROJ], BF16, kind="Internal").ap()
    acsf_d = nc.dram_tensor("acsf_d", [1, 2048], F32, kind="Internal").ap()
    nacsf_d = nc.dram_tensor("nacsf_d", [1, 2048], F32, kind="Internal").ap()
    wout16h_h = nc.dram_tensor("wout16h", [DI, DM], BF16, kind="Internal").ap()
    wout16l_h = nc.dram_tensor("wout16l", [DI, DM], BF16, kind="Internal").ap()

    dbg_h = {}
    if dbg:
        for nm, shape in [("d_xnf", [128, 8 * T]), ("d_sz", [128, 16 * T]),
                          ("d_xbcc", [128, 18 * T]), ("d_dts", [16, 4 * T]),
                          ("d_ynw", [128, 16 * T]), ("d_att", [128, 2 * DM]),
                          ("d_carry", [128, NH * HD]),
                          ("d_mt", [128, 512]), ("d_cst", [128, 512]),
                          ("d_em", [128, 512]), ("d_er", [128, 512])]:
            dbg_h[nm] = nc.dram_tensor(nm, shape, F32, kind="ExternalOutput").ap()

    ident_h = nc.inline_tensor(np.eye(128, dtype=np.float32), name="identc").ap()
    onesc_h = nc.inline_tensor(np.ones((128, 1), dtype=np.float32), name="onescc").ap()
    onesr_h = nc.inline_tensor(np.ones((1, 128), dtype=np.float32), name="onesrc").ap()
    mt_np = np.where(np.arange(128)[None, :] >= np.arange(128)[:, None],
                     1.0, 0.0).astype(np.float32)
    masktri_h = nc.inline_tensor(mt_np, name="masktric").ap()
    tr_np = (np.arange(128)[:, None] <= np.arange(128)[None, :]).astype(np.float32)
    trilones_h = nc.inline_tensor(tr_np, name="trilonesc").ap()

    P = Prog()
    ctx = []

    def sb(name, p, cols, dt=F32):
        g = nc.sbuf_tensor(name, [p, cols], dt)
        ctx.append(g)
        return g.__enter__()

    ident = sb("ident", 128, 128)
    ones_col = sb("ones_col", 128, 1)
    ones_row = sb("ones_row", 1, 128)
    masktri = sb("masktri", 128, 128)
    trilones = sb("trilones", 128, 128)
    convw_sb = sb("convw_sb", 128, 18 * 4)
    convb_sb = sb("convb_sb", 128, 18)
    dtb_sb = sb("dtb_sb", NH, 1)
    c_one = sb("c_one", NH, 1)
    c_tiny = sb("c_tiny", NH, 1)
    alog_sb = sb("alog_sb", NH, 1)
    a_tmp = sb("a_tmp", NH, 1)
    a_col = sb("a_col", NH, 1)
    d_row = sb("d_row", 1, NH)
    db128 = sb("db128", 128, NH)
    nw_sb = sb("nw_sb", 128, NH)
    di_sb = sb("di_sb", 128, NH * HD)
    carry = sb("carry", 128, NH * HD)
    tail_sb = sb("tail_sb", 128, 18 * 3)

    u_sb = sb("u_sb", 128, 2 * DM)
    ln_s1 = sb("ln_s1", 128, 8)
    xn_f = sb("xn_f", 128, 8 * T)
    arenaA = sb("arenaA", 128, 18 * (T + 3))
    actB = sb("actB", 128, 8704)
    dts = sb("dts", NH, 4 * T)
    att_sb = sb("att_sb", 128, 2 * DM)
    out_sb = sb("out_sb", 128, 2 * DM)

    dAT = sb("dAT", 128, NH)
    acs_h = sb("acs_h", NH, CH)
    nacs_h = sb("nacs_h", NH, CH)
    dch = sb("dch", NH, CH)
    sc_h = sb("sc_h", NH, CH)
    scT = sb("scT", 128, NH)
    cdb128 = sb("cdb128", 128, NH)
    btok = sb("btok", 128, CH)
    bsc = sb("bsc", 128, 4 * CH)
    xtok = sb("xtok", 128, NH * CH)
    expmt4 = sb("expmt4", 128, 512)
    expr4 = sb("expr4", 128, 512)
    mt_sb = sb("mt_sb", 128, 4 * CH)
    cst = sb("cst", 128, 4 * CH)
    y1_sb = sb("y1_sb", 128, 4 * CH)
    ysq = sb("ysq", 128, 4 * CH)
    ssqrow = sb("ssqrow", 1, T)
    rstdrow = sb("rstdrow", 1, T)
    rstd_col = sb("rstd_col", 128, 2)
    ffacc = sb("ffacc", 128, DM)
    sg = sb("sg", 128, 2 * T)
    wbufW = sb("wbufW", 128, 16384, BF16)
    wbuf16 = sb("wbuf16", 128, 12288, BF16)
    xnh = sb("xnh", 128, 8 * T, BF16)
    xnl = sb("xnl", 128, 8 * T, BF16)
    hhb = sb("hhb", 128, 2 * T, BF16)
    hlb = sb("hlb", 128, 2 * T, BF16)

    convtmp = ffacc[:, 0:T]

    pbg = [nc.psum_tensor(f"pb{i}", [128, 512], F32) for i in range(8)]
    ctx.extend(pbg)
    pb = [g.__enter__() for g in pbg]

    def xbcpre(f):
        return arenaA[:, f * (T + 3):(f + 1) * (T + 3)]

    def ynw(h):
        return arenaA[:, h * T:(h + 1) * T]

    def szt(f):
        return actB[:, f * T:(f + 1) * T]

    def xbcc(f):
        return actB[:, 4096 + f * T:4096 + (f + 1) * T]

    def hffn(j):
        return actB[:, j * T:(j + 1) * T]

    def wslabW(i, lo):
        return wbufW[:, i * 8192 + lo * 4096:i * 8192 + lo * 4096 + 4096]

    def wqW(pos, lo):
        return wbufW[:, pos * 2048 + lo * 1024:pos * 2048 + (lo + 1) * 1024]

    def wslab16(i):
        return wbuf16[:, i * 4096:(i + 1) * 4096]

    def wq16(pos):
        return wbuf16[:, pos * 1024:(pos + 1) * 1024]

    def d(fn):
        return P.dma("y", "w", fn)

    d(lambda e: e.dma_start(out=ident[:], in_=ident_h[:, :]))
    d(lambda e: e.dma_start(out=ones_col[:], in_=onesc_h[:, :]))
    d(lambda e: e.dma_start(out=ones_row[:], in_=onesr_h[:, :]))
    d(lambda e: e.dma_start(out=masktri[:], in_=masktri_h[:, :]))
    d(lambda e: e.dma_start(out=trilones[:], in_=trilones_h[:, :]))
    d(lambda e: e.dma_start(out=convw_sb[:], in_=convw_h[:, :]))
    d(lambda e: e.dma_start(out=convb_sb[:], in_=convb_h[:, :]))
    d(lambda e: e.dma_start(out=dtb_sb[:], in_=dtb_h[:, :]))
    d(lambda e: e.dma_start(out=alog_sb[:], in_=alog_h[:, :]))
    d(lambda e: e.dma_start(out=d_row[:], in_=dvec_h[:, :]))
    d(lambda e: e.dma_start(out=nw_sb[:], in_=nw_h[:, :]))
    d(lambda e: e.dma_start(
        out=u_sb[:, 0:2 * DM],
        in_=u_h.rearrange("(a p) n -> p a n", p=128)[:, 0:2, :]))
    m_init = P.snap("w")

    m_prep = [P.dma("g", "g", lambda e, dst=dst, s_=s_: e.dma_start(
        out=dst[:, :], in_=s_[:, :]))
        for dst, s_ in [(wg16_h, wgT), (wu16_h, wuT), (wd16_h, wdT)]]
    P.emit("v", lambda e: e.memset(carry[:], 0.0), inc=SV)
    P.emit("v", lambda e: e.memset(tail_sb[:], 0.0), inc=SV)
    P.emit("v", lambda e: e.memset(c_one[:], 1.0), inc=SV)
    P.emit("v", lambda e: e.memset(c_tiny[:], 1e-30), inc=SV)
    P.emit("s", lambda e: e.activation(a_tmp[:], alog_sb[:], AF.Exp),
           waits=m_init, inc=SS)
    P.emit("v", lambda e: e.tensor_scalar(a_col[:], a_tmp[:], -1.0, None, ALU.mult),
           waits=[(SS, P.at(SS))], inc=SV)
    P.emit("t", lambda e: e.matmul(pb[0][0:128, 0:NH], ones_row[:], d_row[:],
                                   start=True, stop=True),
           waits=m_init, inc=ST)
    P.emit("v", lambda e: e.tensor_copy(db128[:], pb[0][0:128, 0:NH]),
           waits=[(ST, P.at(ST))], inc=SV)
    m_db = P.at(SV)
    for h in range(NH):
        P.emit("v", lambda e, h=h: e.tensor_scalar(
            di_sb[:, h * HD:(h + 1) * HD], ident[:], db128[:, h:h + 1], None, ALU.mult),
            waits=[(SV, m_db)] if h == 0 else (), inc=SV)

    xn_tok = arenaA
    lnsq_a, lnsq_b = expmt4, expr4

    def emit_LN_half(o, tsub):
        if True:
            ut = u_sb[:, tsub * DM:(tsub + 1) * DM]
            c0 = tsub * 4
            P.emit("v", lambda e, ut=ut, c0=c0: e.reduce_sum(
                ln_s1[:, c0:c0 + 1], ut, axis=AX.X),
                waits=[*m_u[o][tsub]], inc=SV)
            P.emit("s", lambda e, ut=ut: e.activation(
                lnsq_a[:], ut[:, 0:512], AF.Square),
                waits=[*m_u[o][tsub], (SV, P.at(SV))], inc=SS)
            P.emit("s", lambda e, ut=ut: e.activation(
                lnsq_b[:], ut[:, 512:1024], AF.Square), inc=SS)
            m_sq = P.at(SS)
            P.emit("v", lambda e, c0=c0: e.reduce_sum(
                ln_s1[:, c0 + 1:c0 + 2], lnsq_a[:], axis=AX.X),
                waits=[(SS, m_sq)], inc=SV)
            P.emit("v", lambda e, c0=c0: e.reduce_sum(
                ln_s1[:, c0 + 3:c0 + 4], lnsq_b[:], axis=AX.X), inc=SV)
            P.emit("v", lambda e, c0=c0: e.tensor_tensor(
                ln_s1[:, c0 + 1:c0 + 2], ln_s1[:, c0 + 1:c0 + 2],
                ln_s1[:, c0 + 3:c0 + 4], ALU.add), inc=SV)
            P.emit("v", lambda e, c0=c0: e.tensor_scalar(
                ln_s1[:, c0:c0 + 1], ln_s1[:, c0:c0 + 1], 1.0 / DM, None, ALU.mult),
                inc=SV)
            P.emit("v", lambda e, c0=c0: e.tensor_scalar(
                ln_s1[:, c0 + 1:c0 + 2], ln_s1[:, c0 + 1:c0 + 2], 1.0 / DM, EPS,
                ALU.mult, ALU.add), inc=SV)
            P.emit("v", lambda e, c0=c0: e.tensor_tensor(
                ln_s1[:, c0 + 2:c0 + 3], ln_s1[:, c0:c0 + 1], ln_s1[:, c0:c0 + 1],
                ALU.mult), inc=SV)
            P.emit("v", lambda e, c0=c0: e.tensor_tensor(
                ln_s1[:, c0 + 1:c0 + 2], ln_s1[:, c0 + 1:c0 + 2],
                ln_s1[:, c0 + 2:c0 + 3], ALU.subtract), inc=SV)
            P.emit("s", lambda e, c0=c0: e.activation(
                ln_s1[:, c0 + 3:c0 + 4], ln_s1[:, c0 + 1:c0 + 2], AF.Sqrt),
                waits=[(SV, P.at(SV))], inc=SS)
            P.emit("v", lambda e, c0=c0: e.reciprocal(
                ln_s1[:, c0 + 2:c0 + 3], ln_s1[:, c0 + 3:c0 + 4]),
                waits=[(SS, P.at(SS))], inc=SV)
            P.emit("v", lambda e, c0=c0: e.tensor_tensor(
                ln_s1[:, c0 + 3:c0 + 4], ln_s1[:, c0:c0 + 1],
                ln_s1[:, c0 + 2:c0 + 3], ALU.mult), inc=SV)
            P.emit("v", lambda e, ut=ut, tsub=tsub, c0=c0: e.tensor_scalar(
                xn_tok[:, tsub * DM:(tsub + 1) * DM], ut, ln_s1[:, c0 + 2:c0 + 3],
                ln_s1[:, c0 + 3:c0 + 4], ALU.mult, ALU.subtract),
                waits=[(ST, P.at(ST))] if tsub == 0 else (), inc=SV)
        return P.at(SV)

    def emit_LN_tp(m_xntok):
        for k in range(8):
            bk = pb[2 + (k % 2)]
            for tsub in range(2):
                P.emit("t", lambda e, k=k, tsub=tsub, bk=bk: e.transpose(
                    bk[0:128, tsub * 128:(tsub + 1) * 128],
                    xn_tok[:, tsub * DM + k * 128: tsub * DM + (k + 1) * 128],
                    ident[:]),
                    waits=[(SV, m_xntok), (SS, P.at(SS))] if (tsub == 0 and k < 2)
                    else ([(SV, P.at(SV))] if tsub == 0 else ()), inc=ST)
            m_tp = P.at(ST)
            P.emit("v", lambda e, k=k, bk=bk: e.tensor_copy(
                xn_f[:, k * T:(k + 1) * T], bk[0:128, 0:256]),
                waits=[(ST, m_tp)], inc=SV)
        return P.at(SV)

    NWG = 9
    PSCHED = [0, 0, 1, 1, 1, 1, 1, 1, 1, 1, 1, 1, 1, 1, 1, 1]
    m_prev_store = []
    m_prev_tensor_end = 0
    m_prev_gu_end = 0
    m_prev_ffev = 0
    m_prev_attmm = 0
    m_prev_guDE = 0
    m_prev_hv = 0
    m_prev_gu_ss = 0
    m_gu_start = 0

    m_u = [[m_init, m_init]] + [None] * n_outer

    GSEQ = [4, 5, 6, 7, 8, 0, 1, 2, 3]
    PCH = []
    for g_ in GSEQ:
        gw_ = 512 if g_ < 8 else JPROJ - 8 * 512
        PCH.append(("wi", g_, 0, 256))
        PCH.append(("wi", g_, 256, gw_ - 256))
    for k_ in range(16):
        PCH.append(("wo", k_, 0, 1024))
    NPCH = len(PCH)
    m_pcp = [0] * NPCH
    m_psb = [0] * NPCH
    m_pmm = [0] * NPCH
    pst_snap = [[] for _ in range(NPCH)]

    def pstag(c, coloff, w, lo):
        s3 = c % 3
        return wbuf16[:, s3 * 4096 + lo * 2048 + coloff:
                      s3 * 4096 + lo * 2048 + coloff + w]

    def emit_prep_chunk(i, ldwaits=()):
        kind, idx, j0, jw = PCH[i]
        ncols = 8 * jw if kind == "wi" else jw
        sf = att_sb if i % 2 == 0 else out_sb
        fv = sf[:, 0:ncols]
        hv = pstag(i, 0, ncols, 0)
        lv = pstag(i, 0, ncols, 1)
        if kind == "wi":
            j0a = idx * 512 + j0
            src = winT.rearrange("(k p) j -> p k j", p=128)[:, :, j0a:j0a + jw]
            dsth = win16h_h.rearrange("(k p) j -> p k j", p=128)[:, :, j0a:j0a + jw]
            dstl = win16l_h.rearrange("(k p) j -> p k j", p=128)[:, :, j0a:j0a + jw]
            fvr = fv.rearrange("p (k j) -> p k j", k=8)
            hvr = hv.rearrange("p (k j) -> p k j", k=8)
            lvr = lv.rearrange("p (k j) -> p k j", k=8)
        else:
            src = woutT.rearrange("(a p) n -> a p n", p=128)[idx]
            dsth = wout16h_h.rearrange("(a p) n -> a p n", p=128)[idx]
            dstl = wout16l_h.rearrange("(a p) n -> a p n", p=128)[idx]
            fvr, hvr, lvr = fv, hv, lv
        wld = list(ldwaits)
        if i >= 2:
            wld += [(SS, m_pcp[i - 2]), (SV, m_psb[i - 2])]
        ld = P.dma("y", "w", lambda e: e.dma_start(out=fvr, in_=src), waits=wld)
        wcp = [ld]
        if i >= 3:
            if m_pmm[i - 3] > 0:
                wcp.append((ST, m_pmm[i - 3]))
            wcp += pst_snap[i - 3]
        m_pcp[i] = P.emit("s", lambda e: e.copy(hv, fv), waits=wcp, inc=SS)
        m_psb[i] = P.emit("v", lambda e: e.tensor_tensor(
            lv, fv, hv, ALU.subtract), waits=[(SS, m_pcp[i])], inc=SV)
        s1 = P.dma("g", "p", lambda e: e.dma_start(out=dsth, in_=hvr),
                   waits=[(SS, m_pcp[i])])
        s2 = P.dma("g", "p", lambda e: e.dma_start(out=dstl, in_=lvr),
                   waits=[(SV, m_psb[i])])
        pst_snap[i] = [s1, s2]

    def emit_xnsplit(m_xnf_v, m_guend):
        for k in range(8):
            P.emit("s", lambda e, k=k: e.copy(
                xnh[:, k * T:(k + 1) * T], xn_f[:, k * T:(k + 1) * T]),
                waits=[(ST, m_guend), (SV, m_xnf_v)] if k == 0 else (),
                inc=SS)
            P.emit("v", lambda e, k=k: e.tensor_tensor(
                xnl[:, k * T:(k + 1) * T], xn_f[:, k * T:(k + 1) * T],
                xnh[:, k * T:(k + 1) * T], ALU.subtract),
                waits=[(SS, P.at(SS))], inc=SV)
        return P.at(SV)

    ln_state = {}
    xnsplit_state = {}
    for o in range(n_outer):
        ub = 0
        if o in xnsplit_state:
            m_xnsplit = xnsplit_state.pop(o)
            ln_state.pop(o, None)
        else:
            if o in ln_state:
                m_xnf = ln_state.pop(o)
            else:
                emit_LN_half(o, 0)
                m_xnf = emit_LN_tp(emit_LN_half(o, 1))
            m_xnsplit = emit_xnsplit(m_xnf, m_prev_gu_end)

        m_gu = [[None, None] for _ in range(6)]
        GUSLOTS = [wbuf16[:, 0:4096], wbuf16[:, 4096:8192],
                   wbufW[:, 0:4096], wbufW[:, 4096:8192],
                   wbufW[:, 8192:12288], wbufW[:, 12288:16384]]

        def emit_gu_dma(g, gu, waits):
            src = wg16_h if gu == 0 else wu16_h
            buf = GUSLOTS[(2 * g + gu) % 6]
            m_gu[g][gu] = [P.dma(
                "g", "g", lambda e, g=g, src=src, buf=buf: e.dma_start(
                    out=buf.rearrange("p (k j) -> p k j", k=8)[:, :, :],
                    in_=src.rearrange("(k p) j -> p k j", p=128)
                    [:, :, g * 512:(g + 1) * 512]),
                waits=waits)]

        if o > 0:
            emit_gu_dma(0, 0, [(ST, m_prev_gu_end)])
            emit_gu_dma(0, 1, [(ST, m_prev_gu_end)])

        m_wg = [None] * NWG
        m_csilu = [0] * 18

        def emit_win_dma(pos, waits):
            g = GSEQ[pos]
            gw = 512 if g < 8 else JPROJ - 8 * 512
            w2 = list(waits)
            if o == 1:
                w2 += pst_snap[2 * pos] + pst_snap[2 * pos + 1]
            dmas = []
            for lo, srcw in ((0, win16h_h), (1, win16l_h)):
                buf = wslabW(pos % 2, lo)
                dmas.append(P.dma(
                    "y", "w", lambda e, g=g, gw=gw, buf=buf, srcw=srcw: e.dma_start(
                        out=buf.rearrange("p (k j) -> p k j", k=8)[:, :, 0:gw],
                        in_=srcw.rearrange("(k p) j -> p k j", p=128)
                        [:, :, g * 512:g * 512 + gw]),
                    waits=w2))
            m_wg[g] = dmas

        def emit_conv(f):
            cacc = sg[:, (f % 2) * T:(f % 2) * T + T]
            P.emit("v", lambda e, f=f: e.tensor_copy(
                xbcpre(f)[:, 0:3], tail_sb[:, f * 3:(f + 1) * 3]), inc=SV)
            P.emit("v", lambda e, f=f, cacc=cacc: e.tensor_scalar(
                cacc, xbcpre(f)[:, 0:T],
                convw_sb[:, f * 4 + 0: f * 4 + 1], None, ALU.mult),
                waits=[(SS, m_csilu[f - 2])] if f >= 2 else (), inc=SV)
            for k in range(1, 4):
                P.emit("v", lambda e, f=f, k=k: e.tensor_scalar(
                    convtmp, xbcpre(f)[:, k:k + T],
                    convw_sb[:, f * 4 + k: f * 4 + k + 1], None, ALU.mult), inc=SV)
                P.emit("v", lambda e, cacc=cacc: e.tensor_tensor(
                    cacc, cacc, convtmp, ALU.add), inc=SV)
            m_cacc = P.at(SV)
            m_csilu[f] = P.emit("s", lambda e, f=f, cacc=cacc: e.activation(
                xbcc(f), cacc, AF.Silu, bias=convb_sb[:, f:f + 1]),
                waits=[(SV, m_cacc)], inc=SS)
            P.emit("v", lambda e, f=f: e.tensor_copy(
                tail_sb[:, f * 3:(f + 1) * 3], xbcpre(f)[:, T:T + 3]), inc=SV)

        ssm_units = []
        ssm_res = {}

        def build_prep_units(cc, svgate, ssgate):
            col = cc * CH
            tb0 = pb[2] if cc == 0 else pb[0]
            tb1 = pb[3] if cc == 0 else pb[1]
            res = {}
            U = []

            def u1():
                P.emit("t", lambda e: e.transpose(
                    tb0[0:128, 0:NH], dts[:, 2 * T + col:2 * T + col + CH],
                    ident[0:NH, 0:NH]),
                    waits=[(SV, svgate), (SS, P.at(SS))], inc=ST)
                res["t1"] = P.at(ST)
            U.append(u1)

            def u2():
                P.emit("v", lambda e: e.tensor_copy(dAT[:], tb0[0:128, 0:NH]),
                       waits=[(ST, res["t1"])], inc=SV)
                m_datv = P.at(SV)
                P.emit("t", lambda e: e.matmul(tb1[0:16, 0:CH], dAT[:, 0:NH],
                                               trilones[:],
                                               start=True, stop=True),
                       waits=[(SV, m_datv)], inc=ST)
                res["acs"] = P.at(ST)
            U.append(u2)

            def u3():
                P.emit("v", lambda e: e.tensor_copy(acs_h[:], tb1[0:16, 0:CH]),
                       waits=[(ST, res["acs"])], inc=SV)
                res["acshv"] = P.at(SV)
                P.emit("s", lambda e: e.activation(
                    dch[:], acs_h[:], AF.Exp, scale=-1.0,
                    bias=acs_h[:, CH - 1:CH]),
                    waits=[(SV, res["acshv"])], inc=SS)
                res["nacshv"] = P.emit("v", lambda e: e.tensor_tensor(
                    nacs_h[:], dts[:, 3 * T + col:3 * T + col + CH],
                    acs_h[:], ALU.subtract), inc=SV)
            U.append(u3)

            def u4():
                P.emit("v", lambda e: e.tensor_tensor(
                    sc_h[:], dch[:], dts[:, T + col:T + col + CH], ALU.mult),
                    waits=[(SS, P.at(SS))], inc=SV)
                m_sch = P.at(SV)
                P.emit("t", lambda e: e.transpose(tb1[0:128, NH:2 * NH],
                                                  sc_h[:], ident[0:NH, 0:NH]),
                       waits=[(SV, m_sch)], inc=ST)
                res["t2"] = P.at(ST)
            U.append(u4)

            def u5():
                P.emit("v", lambda e: e.tensor_copy(scT[:],
                                                    tb1[0:128, NH:2 * NH]),
                       waits=[(ST, res["t2"])], inc=SV)
            U.append(u5)

            def u6():
                w1 = [(SV, res["acshv"])]
                d1a = P.dma("y", "w", lambda e: e.dma_start(
                    out=acsf_d.rearrange("o (h q) -> (o h) q", q=CH),
                    in_=acs_h[:, :]), waits=w1)
                d2a = P.dma("y", "w", lambda e: e.dma_start(
                    out=nacsf_d.rearrange("o (h q) -> (o h) q", q=CH),
                    in_=nacs_h[:, :]), waits=[(SV, res["nacshv"])])
                w1b = [d1a]
                w2b = [d2a] + m_prev_store
                if o == 0 and cc == 0:
                    w1b += [(SS, m_pcp[16]), (SV, m_psb[16])]
                    w2b += [(SS, m_pcp[17]), (SV, m_psb[17])]
                d1b = P.dma("y", "w", lambda e: e.dma_start(
                    out=att_sb[0:1, 0:2048], in_=acsf_d[:, :]), waits=w1b)
                d2b = P.dma("y", "w", lambda e: e.dma_start(
                    out=out_sb[0:1, 0:2048], in_=nacsf_d[:, :]), waits=w2b)
                res["acsflat_w"] = [d1b, d2b]
            U.append(u6)

            def u10():
                P.emit("t", lambda e: e.matmul(
                    tb1[0:128, 2 * NH:3 * NH], ones_row[:],
                    att_sb[0:1, 0:2048].rearrange("o (h q) -> o h q", q=CH)
                    [:, :, CH - 1:CH], start=True, stop=True),
                    waits=list(res["acsflat_w"]), inc=ST)
                m_cdbt = P.at(ST)
                P.emit("s", lambda e: e.activation(
                    cdb128[:], tb1[0:128, 2 * NH:3 * NH], AF.Exp),
                    waits=[(ST, m_cdbt)], inc=SS)
                res["cdb"] = P.at(SS)
                P.emit("t", lambda e: e.transpose(
                    tb0[0:128, 128:256], xbcc(16)[:, col:col + CH], ident[:]),
                    waits=[(SS, ssgate), (SV, P.at(SV))], inc=ST)
                m_bt = P.at(ST)
                P.emit("v", lambda e: e.tensor_copy(btok[:],
                                                    tb0[0:128, 128:256]),
                       waits=[(ST, m_bt)], inc=SV)
            U.append(u10)

            def mk_xt(hs):
                def ux():
                    for h in hs:
                        bk = tb0 if h % 2 == 0 else tb1
                        P.emit("t", lambda e, h=h, bk=bk: e.transpose(
                            bk[0:128, 256:384], xbcc(h)[:, col:col + CH],
                            ident[:]), waits=[(SV, P.at(SV))], inc=ST)
                        m_xt = P.at(ST)
                        P.emit("v", lambda e, h=h, bk=bk: e.tensor_copy(
                            xtok[:, h * CH:(h + 1) * CH], bk[0:128, 256:384]),
                            waits=[(ST, m_xt)], inc=SV)
                return ux
            for hs in ([0, 1, 2, 3], [4, 5, 6, 7], [8, 9, 10, 11],
                       [12, 13, 14, 15]):
                U.append(mk_xt(hs))

            def u15():
                m_xtv = P.at(SV)
                P.emit("t", lambda e: e.matmul(
                    tb1[0:128, 384:512], xbcc(16)[:, col:col + CH],
                    xbcc(17)[:, col:col + CH], start=True, stop=True),
                    waits=[(SV, m_xtv), (SS, P.at(SS))], inc=ST)
                m_cbt = P.at(ST)
                P.emit("v", lambda e: e.tensor_tensor(
                    sg[:, 0:CH], tb1[0:128, 384:512], masktri[:], ALU.mult),
                    waits=[(ST, m_cbt)], inc=SV)
                res["prepv"] = P.at(SV)
            U.append(u15)
            return U, res

        if o == 0:
            for ci in range(3):
                emit_prep_chunk(ci)
        else:
            emit_win_dma(0, [(ST, m_prev_guDE)])
            emit_win_dma(1, [(ST, m_prev_gu_end)])

        bank_free = {i: (ST, 0) for i in range(4)}
        m_preS_sv = m_preS_ss = None
        zslot = 0
        for pos in range(NWG):
            g = GSEQ[pos]
            njt = 4 if g < 8 else 3
            for jj in range(njt):
                j = g * 4 + jj
                is_dt = (g == 8 and jj == 2)
                jw = 16 if is_dt else 128
                if pos == 0:
                    bank, bkey = pb[4 + jj], None
                elif pos >= 5:
                    bank, bkey = pb[jj % 2], jj % 2
                else:
                    bank, bkey = pb[j % 4], j % 4
                pview = bank[0:jw, 0:T]
                c = 2 * pos + (0 if jj < 2 else 1)
                if o == 0:
                    jwc = PCH[c][3]
                    joff = jj * 128 - PCH[c][2]

                    def wv(k, lo, c=c, jwc=jwc, joff=joff, jw=jw):
                        return pstag(c, k * jwc + joff, jw, lo)
                else:
                    def wv(k, lo, pos=pos, jj=jj, jw=jw):
                        return wslabW(pos % 2, lo)[:, k * 512 + jj * 128:
                                                   k * 512 + jj * 128 + jw]
                for k in range(8):
                    waits = []
                    if k == 0:
                        if o == 0:
                            waits += [(SS, m_pcp[c]), (SV, m_psb[c])]
                        else:
                            waits.extend(m_wg[g])
                        waits.append((SV, m_xnsplit))
                        if pos == 0:
                            if m_prev_hv > 0:
                                waits += [(SV, m_prev_hv), (SS, m_prev_gu_ss)]
                        else:
                            bf = bank_free[bkey]
                            if bf[1] > 0:
                                waits.append(bf)
                            elif m_prev_ffev > 0:
                                waits.append((SS, m_prev_ffev))
                    for pi, (lo, xs) in enumerate(((0, xnh), (0, xnl), (1, xnh))):
                        P.emit("t", lambda e, k=k, pi=pi, lo=lo, xs=xs, wv=wv,
                               pview=pview: e.matmul(
                            pview, wv(k, lo), xs[:, k * T:(k + 1) * T],
                            start=(k == 0 and pi == 0),
                            stop=(k == 7 and pi == 2)),
                            waits=waits if pi == 0 else (),
                            inc=(ST if (k == 7 and pi == 2) else None))
                        waits = []
                if o == 0:
                    m_pmm[c] = P.at(ST)
                m_mm = P.at(ST)
                if j < 16:
                    mev = P.emit("s", lambda e, j=j, pview=pview: e.activation(
                        szt(j), pview, AF.Silu), waits=[(ST, m_mm)], inc=SS)
                    bank_free[bkey] = (SS, mev)
                elif not is_dt:
                    f = j - 16
                    mev = P.emit("v", lambda e, f=f, pview=pview: e.tensor_copy(
                        xbcpre(f)[:, 3:3 + T], pview), waits=[(ST, m_mm)], inc=SV)
                    if pos > 0:
                        bank_free[bkey] = (SV, mev)
                    emit_conv(f)
                else:
                    mev = P.emit("v", lambda e, pview=pview: e.tensor_copy(
                        dts[:, 0:T], pview), waits=[(ST, m_mm)], inc=SV)
                    bank_free[bkey] = (SV, mev)
                if ssm_units:
                    if PSCHED[min(zslot, 15)]:
                        ssm_units.pop(0)()
                    zslot += 1
            if o == 0:
                for ci in (2 * pos + 3, 2 * pos + 4):
                    if ci < 18:
                        emit_prep_chunk(ci)
            elif pos + 2 < NWG:
                emit_win_dma(pos + 2, [(ST, P.at(ST))])
            if g == 8:
                m_dt_ev = P.at(SV)
                P.emit("s", lambda e: e.activation(dts[:, T:2 * T], dts[:, 0:T],
                                                   AF.Exp, bias=dtb_sb[:, 0:1]),
                       waits=[(SV, m_dt_ev)], inc=SS)
                P.emit("s", lambda e: e.activation(dts[:, T:2 * T],
                                                   dts[:, T:2 * T], AF.Ln,
                                                   bias=c_one[:, 0:1]), inc=SS)
                P.emit("s", lambda e: e.activation(dts[:, 3 * T:4 * T],
                                                   dts[:, T:2 * T], AF.Ln,
                                                   bias=c_tiny[:, 0:1]), inc=SS)
                m_dt_ss = P.at(SS)
                P.emit("v", lambda e: e.tensor_scalar(dts[:, 2 * T:3 * T],
                                                      dts[:, T:2 * T],
                                                      a_col[:, 0:1], None,
                                                      ALU.mult),
                       waits=[(SS, m_dt_ss)], inc=SV)
                m_preS_sv = P.at(SV)
                m_preS_ss = P.at(SS)
                if o > 0:
                    us0, r0 = build_prep_units(0, m_preS_sv, m_preS_ss)
                    ssm_units.extend(us0)
                    ssm_res[0] = r0
        while ssm_units:
            ssm_units.pop(0)()
        m_conv_s = m_preS_ss
        m_conv_v = m_preS_sv
        m_inproj_t = P.at(ST)

        dbg_gate = [[]]
        m_ssq_ev = 0
        m_prev_ssqfin = 0
        for cc in range(2):
            col = cc * CH

            if cc == 0 and o > 0:
                res = ssm_res[0]
            else:
                usx, res = build_prep_units(cc, max(m_preS_sv, m_ssq_ev),
                                            m_conv_s)
                for u_ in usx:
                    u_()
            m_acsflat_w = res["acsflat_w"]
            m_cdb = res["cdb"]
            m_prepv = res["prepv"]
            acs_flat = att_sb
            nacs_flat = out_sb

            NG = 4
            m_bankA = [None] * NG
            m_bankC = [None] * NG
            m_e4 = [None] * NG
            m_expmt4 = [None] * NG
            m_expr4 = [None] * NG
            m_mtc = [None] * NG
            m_bsc4 = [None] * NG
            m_Ylast = [None] * NG
            m_stlast = [None] * NG
            m_cmlast = [None] * NG
            m_y14 = [None] * NG
            m_ysq4 = [None] * NG
            m_ynw4 = [None] * NG
            m_ssqg = [None] * NG
            m_Y = [None] * NH

            mcbt_rep = bass.AP(sg, 0, [[512, 128], [0, 4], [1, 128]])
            c_rep = bass.AP(actB, 4096 + 17 * T + col, [[8704, 128], [0, 4], [1, 128]])

            def emit_bank(g):
                bankA = pb[2 + (g % 2)]
                ybank = pb[4 + (g % 2)]
                wA = list(m_acsflat_w)
                if g < 2:
                    wA.append((SV, m_prepv))
                if g >= 2:
                    wA.append((SS, m_expr4[g - 2]))
                m_bankA[g] = P.emit("t", lambda e, g=g, bankA=bankA: e.matmul(
                    bankA[0:128, 0:512], ones_row[:],
                    acs_flat[0:1, g * 512:(g + 1) * 512],
                    start=True, stop=True), waits=wA, inc=ST)
                wC = [(SV, m_y14[g - 2] if g >= 2 else P.at(SV))]
                P.emit("t", lambda e, g=g, ybank=ybank: e.matmul(
                    ybank[0:128, 0:512], ones_row[:],
                    acs_flat[0:1, g * 512:(g + 1) * 512],
                    start=True, stop=False), waits=wC)
                for hh in range(4):
                    h = 4 * g + hh
                    P.emit("t", lambda e, h=h, hh=hh, ybank=ybank: e.matmul(
                        ybank[0:128, hh * CH:(hh + 1) * CH],
                        nacs_flat[0:1, h * CH:(h + 1) * CH], ones_row[:],
                        start=False, stop=(hh == 3)),
                        inc=(ST if hh == 3 else None))
                m_bankC[g] = P.at(ST)

            def emit_prep_group(g):
                ybank = pb[4 + (g % 2)]
                bankA = pb[2 + (g % 2)]
                w = [(ST, m_bankC[g])]
                if g >= 1:
                    w.append((SS, m_expmt4[g - 1]))
                if dbg_gate[0] and g == 2:
                    w.extend(dbg_gate[0]); dbg_gate[0] = []
                m_e4[g] = P.emit("v", lambda e, ybank=ybank: e.tensor_scalar(
                    expmt4[:], ybank[0:128, 0:512], 80.0, None, ALU.min),
                    waits=w, inc=SV)
                m_expmt4[g] = P.emit("s", lambda e: e.activation(
                    expmt4[:], expmt4[:], AF.Exp), waits=[(SV, m_e4[g])], inc=SS)
                m_expr4[g] = P.emit("s", lambda e, bankA=bankA: e.activation(
                    expr4[:], bankA[0:128, 0:512], AF.Exp),
                    waits=[(SV, m_e4[g])], inc=SS)
                for hh in range(4):
                    h = 4 * g + hh
                    P.emit("s", lambda e, h=h, hh=hh: e.mul(
                        bsc[:, hh * CH:(hh + 1) * CH], btok[:], scT[:, h:h + 1]),
                        waits=[(SV, P.at(SV)),
                               (ST, m_stlast[g - 1] if g >= 1 else P.at(ST))]
                        if hh == 0 else (), inc=SS)
                m_bsc4[g] = P.at(SS)
                mw = [(SS, m_expmt4[g])]
                if g >= 1:
                    mw.append((ST, m_Ylast[g - 1]))
                P.emit("v", lambda e: e.tensor_tensor(
                    mt_sb[:], mcbt_rep, expmt4[:], ALU.mult), waits=mw, inc=SV)
                P.emit("v", lambda e, g=g: e.tensor_tensor(
                    mt_sb[:], mt_sb[:], di_sb[:, g * 512:(g + 1) * 512], ALU.add),
                    inc=SV)
                cw = [(SS, m_expr4[g])]
                if g >= 1:
                    cw.append((ST, m_Ylast[g - 1]))
                P.emit("v", lambda e, c_rep=c_rep: e.tensor_tensor(
                    cst[:], c_rep, expr4[:], ALU.mult), waits=cw, inc=SV)
                m_mtc[g] = P.at(SV)
                if dbg and o == n_outer - 1 and cc == 0 and g == 1:
                    dbg_gate[0] = [
                        P.dma("y", "o", lambda e, nm=nm, srcb=srcb: e.dma_start(
                            out=dbg_h[nm][:, :], in_=srcb[:]),
                            waits=[(SV, m_mtc[g]), (SS, m_expr4[g])])
                        for nm, srcb in [("d_mt", mt_sb), ("d_cst", cst),
                                         ("d_em", expmt4), ("d_er", expr4)]]

            emit_bank(0)
            emit_prep_group(0)
            for g in range(NG):
                ybank = pb[4 + (g % 2)]
                sbank = pb[6 + (g % 2)]
                if g + 1 < NG:
                    emit_bank(g + 1)
                for hh in range(4):
                    h = 4 * g + hh
                    P.emit("t", lambda e, h=h, hh=hh, ybank=ybank: e.matmul(
                        ybank[0:128, hh * CH:(hh + 1) * CH],
                        xtok[:, h * CH:(h + 1) * CH],
                        mt_sb[:, hh * CH:(hh + 1) * CH], start=True, stop=False),
                        waits=[(SV, m_mtc[g])] if hh == 0 else ())
                    m_Y[h] = P.emit("t", lambda e, h=h, hh=hh, ybank=ybank: e.matmul(
                        ybank[0:128, hh * CH:(hh + 1) * CH],
                        carry[:, h * HD:(h + 1) * HD],
                        cst[:, hh * CH:(hh + 1) * CH], start=False, stop=True),
                        inc=ST)
                m_Ylast[g] = m_Y[4 * g + 3]
                for hh in range(4):
                    h = 4 * g + hh
                    P.emit("t", lambda e, h=h, hh=hh, sbank=sbank: e.matmul(
                        sbank[0:128, hh * CH:(hh + 1) * CH],
                        bsc[:, hh * CH:(hh + 1) * CH],
                        xtok[:, h * CH:(h + 1) * CH], start=(hh == 0),
                        stop=(hh == 3)),
                        waits=[(SS, m_bsc4[g]), (SV, P.at(SV))] if hh == 0 else (),
                        inc=(ST if hh == 3 else None))
                m_stlast[g] = P.at(ST)
                if g >= 1:
                    for hh in range(4):
                        hq = 4 * (g - 1) + hh
                        P.emit("t", lambda e, g=g, hh=hh: e.matmul(
                            pb[0][0:1, 384:384 + CH], ones_col[:],
                            ysq[:, hh * CH:(hh + 1) * CH],
                            start=(g == 1 and hh == 0), stop=False),
                            waits=[(SV, m_ysq4[g - 1])] if hh == 0 else (),
                            inc=(ST if hh == 3 else None))
                    m_ssqg[g - 1] = P.at(ST)
                if g + 1 < NG:
                    emit_prep_group(g + 1)
                for hh in range(4):
                    h = 4 * g + hh
                    P.emit("s", lambda e, h=h: e.mul(
                        carry[:, h * HD:(h + 1) * HD],
                        carry[:, h * HD:(h + 1) * HD], cdb128[:, h:h + 1]),
                        waits=[(ST, m_Y[h]), (SS, m_cdb)] if hh == 0 else
                              [(ST, m_Y[h])], inc=SS)
                m_cmlast[g] = P.at(SS)
                sz_ap = bass.AP(actB, 4 * g * T + col, [[8704, 128], [T, 4], [1, 128]])
                m_y14[g] = P.emit("v", lambda e, g=g, ybank=ybank, sz_ap=sz_ap:
                                  e.tensor_tensor(
                    y1_sb[:], ybank[0:128, 0:512], sz_ap, ALU.mult),
                    waits=[(ST, m_Ylast[g]),
                           (SS, m_ynw4[g - 1] if g >= 1 else 0)], inc=SV)
                m_ysq4[g] = P.emit("v", lambda e: e.tensor_tensor(
                    ysq[:], y1_sb[:], y1_sb[:], ALU.mult),
                    waits=[(ST, m_ssqg[g - 1] if g >= 1 else m_prev_ssqfin)],
                    inc=SV)
                P.emit("v", lambda e, g=g, sbank=sbank: e.tensor_tensor(
                    carry[:, g * 512:(g + 1) * 512],
                    carry[:, g * 512:(g + 1) * 512],
                    sbank[0:128, 0:512], ALU.add),
                    waits=[(ST, m_stlast[g]), (SS, m_cmlast[g])], inc=SV)
                for hh in range(4):
                    h = 4 * g + hh
                    P.emit("s", lambda e, h=h, hh=hh, col=col: e.mul(
                        ynw(h)[:, col:col + CH], y1_sb[:, hh * CH:(hh + 1) * CH],
                        nw_sb[:, h:h + 1]),
                        waits=[(SV, m_y14[g])] if hh == 0 else (), inc=SS)
                m_ynw4[g] = P.at(SS)
            for hh in range(4):
                P.emit("t", lambda e, hh=hh: e.matmul(
                    pb[0][0:1, 384:384 + CH], ones_col[:],
                    ysq[:, hh * CH:(hh + 1) * CH],
                    start=False, stop=(hh == 3)),
                    waits=[(SV, m_ysq4[NG - 1])] if hh == 0 else (),
                    inc=(ST if hh == 3 else None))
            m_ssqfin = P.at(ST)
            m_prev_ssqfin = m_ssqfin
            m_ssq_ev = P.emit("v", lambda e, col=col: e.tensor_copy(
                ssqrow[0:1, col:col + CH], pb[0][0:1, 384:384 + CH]),
                waits=[(ST, m_ssqfin)], inc=SV)
        m_ssm_v = P.at(SV)
        m_ssm_s = P.at(SS)

        if dbg and o == n_outer - 1:
            P.dma("s", "o", lambda e: e.dma_start(
                out=dbg_h["d_carry"][:, :], in_=carry[:]),
                waits=[(SV, m_ssm_v)])

        P.emit("v", lambda e: e.tensor_scalar(rstdrow[0:1, :], ssqrow[0:1, :],
                                              1.0 / DI, EPS, ALU.mult, ALU.add),
               waits=[(SV, m_ssm_v)], inc=SV)
        P.emit("s", lambda e: e.activation(rstdrow[0:1, :], rstdrow[0:1, :], AF.Sqrt),
               waits=[(SV, P.at(SV))], inc=SS)
        P.emit("v", lambda e: e.reciprocal(rstdrow[0:1, :], rstdrow[0:1, :]),
               waits=[(SS, P.at(SS))], inc=SV)
        m_rstdr = P.at(SV)

        m_ssm_t = P.at(ST)
        m_wo = [None] * 16

        def emit_wo_dma(k, waits):
            w2 = list(waits)
            if o == 1:
                w2 += pst_snap[18 + k]
            m_wo[k] = [P.dma(
                "y", "w", lambda e, k=k, lo=lo, srcw=srcw: e.dma_start(
                    out=wqW(k % 8, lo),
                    in_=srcw.rearrange("(a p) n -> a p n", p=128)[k, :, :]),
                waits=w2)
                for lo, srcw in ((0, wout16h_h), (1, wout16l_h))]

        m_ysp = [0] * 16

        def emit_ynw_split(k, extra=()):
            r = (k % 2) * T
            P.emit("v", lambda e, k=k, r=r: e.tensor_copy(
                hhb[:, r:r + T], ynw(k)), waits=extra, inc=SV)
            m_ysp[k] = P.emit("v", lambda e, k=k, r=r: e.tensor_tensor(
                hlb[:, r:r + T], ynw(k), hhb[:, r:r + T], ALU.subtract),
                inc=SV)

        if o == 0:
            emit_prep_chunk(18, ldwaits=[(ST, m_ssm_t)])
            emit_prep_chunk(19, ldwaits=[(ST, m_ssm_t)])
        else:
            for k in range(8):
                emit_wo_dma(k, [(ST, m_inproj_t)] if k < 2 else ())
        emit_ynw_split(0, extra=[(SS, m_ssm_s), (ST, m_ssm_t)])
        emit_ynw_split(1)
        for k in range(16):
            if o == 0:
                waits = [(SS, m_pcp[18 + k]), (SV, m_psb[18 + k])]
                s3w = (18 + k) % 3
                wqh = wbuf16[:, s3w * 4096:s3w * 4096 + 1024]
                wql = wbuf16[:, s3w * 4096 + 2048:s3w * 4096 + 3072]
            else:
                waits = list(m_wo[k])
                wqh = wqW(k % 8, 0)
                wql = wqW(k % 8, 1)
            waits.append((SV, m_ysp[k]))
            if k == 0:
                waits += [(SV, m_ssm_v), (SS, m_ssm_s)]
            for tsub in range(2):
                bks = (pb[0], pb[1]) if tsub == 0 else (pb[2], pb[3])
                r = (k % 2) * T + tsub * 128
                yh = hhb[:, r:r + 128]
                yl = hlb[:, r:r + 128]
                seq = [(yh, wqh, 0), (yh, wqh, 1), (yh, wql, 0),
                       (yh, wql, 1), (yl, wqh, 0), (yl, wqh, 1)]
                for si, (ys, ws, nn_) in enumerate(seq):
                    P.emit("t", lambda e, k=k, si=si, ys=ys, ws=ws, nn_=nn_,
                           bks=bks: e.matmul(
                        bks[nn_][0:128, 0:512], ys,
                        ws[:, nn_ * 512:(nn_ + 1) * 512],
                        start=(k == 0 and si < 2),
                        stop=(k == 15 and si >= 4)),
                        waits=waits if (tsub == 0 and si == 0) else (),
                        inc=(ST if (tsub == 1 and si == 5) else None))
                    waits = []
            if o == 0:
                m_pmm[18 + k] = P.at(ST)
                if 20 + k < NPCH:
                    emit_prep_chunk(20 + k)
            elif k + 8 < 16:
                emit_wo_dma(k + 8, [(ST, P.at(ST))])
            if k + 2 < 16:
                emit_ynw_split(k + 2, extra=[(ST, P.at(ST))])
            if k == 3:
                for tsub in range(2):
                    P.emit("t", lambda e, tsub=tsub: e.transpose(
                        pb[4][0:128, tsub:tsub + 1],
                        rstdrow[0:1, tsub * 128:(tsub + 1) * 128], ident[0:1, 0:1]),
                        waits=[(SV, m_rstdr)] if tsub == 0 else (), inc=ST)
                m_rstdt = P.at(ST)
                P.emit("v", lambda e: e.tensor_copy(
                    rstd_col[:, 0:2], pb[4][0:128, 0:2]),
                    waits=[(ST, m_rstdt)], inc=SV)
                m_rstdc = P.at(SV)
        m_att_mm = P.at(ST)
        m_prev_attmm = m_att_mm
        for tsub in range(2):
            bks = (pb[0], pb[1]) if tsub == 0 else (pb[2], pb[3])
            wev = [(ST, m_att_mm)] if tsub == 0 else []
            if o == 0 and tsub == 0:
                wev += [(SS, m_pcp[32]), (SV, m_psb[32])]
            for nn_ in range(2):
                P.emit("v", lambda e, tsub=tsub, bks=bks, nn_=nn_, wev=wev:
                       e.scalar_tensor_tensor(
                    att_sb[:, tsub * DM + nn_ * 512:tsub * DM + nn_ * 512 + 512],
                    bks[nn_][0:128, 0:512], rstd_col[:, tsub:tsub + 1],
                    u_sb[:, tsub * DM + nn_ * 512:tsub * DM + nn_ * 512 + 512],
                    ALU.mult, ALU.add),
                    waits=wev if nn_ == 0 else (), inc=SV)
                wev = []
        m_att_v = P.at(SV)
        m_u_done = P.at(SV)
        if o + 1 < n_outer:
            m_u[o + 1] = [[P.dma("y", "w", lambda e, o=o, tsub=tsub: e.dma_start(
                out=u_sb[:, tsub * DM:(tsub + 1) * DM],
                in_=u_h.rearrange("(a p) n -> p a n", p=128)
                [:, 2 * (o + 1) + tsub, :]),
                waits=[(SV, m_u_done)])] for tsub in range(2)]
            emit_LN_half(o + 1, 0)
            ln_half_pending = True
        else:
            ln_half_pending = False
        ln_xntok_next = None

        m_dbgst = []
        if dbg and o == n_outer - 1:
            for nm, src in [("d_xnf", xn_f[:]), ("d_sz", actB[:, 0:4096]),
                            ("d_xbcc", actB[:, 4096:8704]), ("d_dts", dts[:]),
                            ("d_ynw", arenaA[:, 0:4096]), ("d_att", att_sb[:])]:
                P.dma("s", "o", lambda e, nm=nm, src=src: e.dma_start(
                    out=dbg_h[nm][:, :], in_=src),
                    waits=[(SV, m_att_v)])
            m_dbgst = P.snap("o")

        gu0_extra = (m_prep + P.snap("p") + [(ST, P.at(ST))]) if o == 0 else []
        if o == 0:
            emit_gu_dma(0, 0, gu0_extra)
            emit_gu_dma(0, 1, gu0_extra)
        emit_gu_dma(1, 0, [(ST, P.at(ST))] + gu0_extra)
        emit_gu_dma(1, 1, [(ST, P.at(ST))] + gu0_extra)

        m_silu = [0] * 24
        m_hev = [0] * 24
        for g in range(6):
            for jj in range(4):
                j = g * 4 + jj
                for gu in range(2):
                    buf = GUSLOTS[(2 * g + gu) % 6]
                    bank = pb[4 + (j % 2)] if gu == 0 else pb[6 + (j % 2)]
                    for k in range(8):
                        waits = []
                        if k == 0:
                            waits.extend(m_gu[g][gu])
                            if gu == 0:
                                waits.append((SS, m_silu[j - 2]) if j >= 2
                                             else (SV, m_att_v))
                            else:
                                waits.append((SV, m_hev[j - 2]) if j >= 2
                                             else (SV, m_att_v))
                        for xsrc in (xnh, xnl):
                            P.emit("t", lambda e, buf=buf, k=k, jj=jj, bank=bank,
                                   xsrc=xsrc: e.matmul(
                                bank[0:128, 0:T],
                                buf[:, k * 512 + jj * 128:k * 512 + (jj + 1) * 128],
                                xsrc[:, k * T:(k + 1) * T],
                                start=(k == 0 and xsrc is xnh),
                                stop=(k == 7 and xsrc is xnl)),
                                waits=waits if xsrc is xnh else (),
                                inc=(ST if (k == 7 and xsrc is xnl) else None))
                            waits = []
                    m_mm = P.at(ST)
                    if gu == 0:
                        w2 = [(ST, m_mm)]
                        if j >= 2:
                            w2.append((SV, m_hev[j - 2]))
                        m_silu[j] = P.emit("s", lambda e, j=j, bank=bank: e.activation(
                            sg[:, (j % 2) * T:(j % 2) * T + T], bank[0:128, 0:T],
                            AF.Silu, scale=QS), waits=w2, inc=SS)
                    else:
                        m_hev[j] = P.emit("v", lambda e, j=j, bank=bank: e.tensor_tensor(
                            hffn(j), bank[0:128, 0:T],
                            sg[:, (j % 2) * T:(j % 2) * T + T], ALU.mult),
                            waits=[(ST, m_mm), (SS, m_silu[j])]
                            + (m_dbgst if j == 0 else []), inc=SV)
            for pn in (2 * g + 4, 2 * g + 5):
                if pn < 12:
                    emit_gu_dma(pn // 2, pn % 2, [(ST, P.at(ST))])
            if g == 1 and ln_half_pending:
                ln_xntok_next = emit_LN_half(o + 1, 1)
            if g == 2 and ln_xntok_next is not None:
                ln_state[o + 1] = emit_LN_tp(ln_xntok_next)
            if g == 4:
                m_guDE = P.at(ST)
        m_h_v = P.at(SV)
        m_prev_hv = m_h_v
        m_prev_gu_ss = P.at(SS)
        m_prev_guDE = m_guDE
        m_prev_gu_end = P.at(ST)
        if o + 1 < n_outer and (o + 1) in ln_state:
            xnsplit_state[o + 1] = emit_xnsplit(ln_state[o + 1],
                                                m_prev_gu_end)

        m_wd = [None] * 24

        def emit_wd_dma(k, waits):
            m_wd[k] = [P.dma("g", "g", lambda e, k=k: e.dma_start(
                out=wbuf16[:, 8192 + (k % 4) * 1024:8192 + (k % 4) * 1024 + 1024],
                in_=wd16_h.rearrange("(a p) n -> a p n", p=128)[k, :, :]),
                waits=waits)]

        for k in range(4):
            emit_wd_dma(k, ([(ST, m_gu_start)]
                            + ((m_prep + P.snap("p")) if o == 0 else []))
                        if k == 0 else ())
        m_hsp = [0] * 24
        m_dnk = [0] * 24
        for k in range(24):
            hc = (k % 2) * T
            wsp = [(SV, m_hev[k])] if False else []
            if k >= 2:
                wsp.append((ST, m_dnk[k - 2]))
            P.emit("v", lambda e, k=k, hc=hc: e.tensor_copy(
                hhb[:, hc:hc + T], hffn(k)), waits=wsp, inc=SV)
            m_hsp[k] = P.emit("v", lambda e, k=k, hc=hc: e.tensor_tensor(
                hlb[:, hc:hc + T], hffn(k), hhb[:, hc:hc + T], ALU.subtract),
                inc=SV)
            waits = list(m_wd[k]) + [(SV, m_hsp[k])]
            if k == 0:
                waits += [(SV, m_h_v), (SV, m_att_v)]
            for tsub in range(2):
                bks = (pb[0], pb[1]) if tsub == 0 else (pb[2], pb[3])
                for nn_ in range(2):
                    for hsrc in (hhb, hlb):
                        P.emit("t", lambda e, k=k, hc=hc, tsub=tsub, nn_=nn_,
                               bks=bks, hsrc=hsrc: e.matmul(
                            bks[nn_][0:128, 0:512],
                            hsrc[:, hc + tsub * 128:hc + (tsub + 1) * 128],
                            wbuf16[:, 8192 + (k % 4) * 1024 + nn_ * 512:
                                   8192 + (k % 4) * 1024 + (nn_ + 1) * 512],
                            start=(k == 0 and hsrc is hhb),
                            stop=(k == 23 and hsrc is hlb)),
                            waits=waits if (tsub == 0 and nn_ == 0 and hsrc is hhb)
                            else (),
                            inc=(ST if (tsub == 1 and nn_ == 1 and hsrc is hlb)
                                 else None))
            m_dnk[k] = P.at(ST)
            if k + 4 < 24:
                emit_wd_dma(k + 4, [(ST, P.at(ST))])
        m_dn = P.at(ST)
        for tsub in range(2):
            bks = (pb[0], pb[1]) if tsub == 0 else (pb[2], pb[3])
            w1 = [(ST, m_dn)]
            if tsub == 1:
                w1.append((SV, P.at(SV)))
            P.emit("s", lambda e, bks=bks: e.mul(
                ffacc[:, 0:512], bks[0][0:128, 0:512], QS * QS), waits=w1, inc=SS)
            P.emit("s", lambda e, bks=bks: e.mul(
                ffacc[:, 512:1024], bks[1][0:128, 0:512], QS * QS), inc=SS)
            m_ff = P.at(SS)
            wout_w = [(SS, m_ff)] + (m_prev_store if tsub == 0 else [])
            if o == 0 and tsub == 0:
                wout_w += [(SS, m_pcp[33]), (SV, m_psb[33])]
            P.emit("v", lambda e, tsub=tsub, wout_w=wout_w: e.tensor_tensor(
                out_sb[:, tsub * DM:(tsub + 1) * DM], ffacc[:],
                att_sb[:, tsub * DM:(tsub + 1) * DM], ALU.add),
                waits=wout_w, inc=SV)
            m_out_v = P.at(SV)
            P.dma("s", "o", lambda e, o=o, tsub=tsub: e.dma_start(
                out=out_h.rearrange("(a p) n -> p a n", p=128)[:, 2 * o + tsub, :],
                in_=out_sb[:, tsub * DM:(tsub + 1) * DM]),
                waits=[(SV, m_out_v)])
        m_prev_store = P.snap("o")
        m_prev_tensor_end = P.at(ST)
        m_gu_start = m_prev_tensor_end
        m_prev_ffev = P.at(SS)

    final_waits = [(k, v) for k, v in P.cnt.items()
                   if v > 0 and k not in (ST, SV, SS)]

    import contextlib
    with contextlib.ExitStack() as stack:
        sems = {k: stack.enter_context(nc.semaphore(f"sem_{k}"))
                for k in P.cnt}
        block = stack.enter_context(nc.Block())

        own_sem = {"v": SV, "s": SS, "g": SG}

        def runner(eng_key):
            def _run(eng):
                own = own_sem.get(eng_key)
                prev = 0
                for waits, fn, inc, inc_by in P.q[eng_key]:
                    if own is not None and prev > 0:
                        eng.wait_ge(sems[own], prev)
                    for sem_key, val in waits:
                        if val and val > 0:
                            eng.wait_ge(sems[sem_key], val)
                    ins = fn(eng)
                    if inc is not None and ins is not None:
                        ins.then_inc(sems[inc], inc_by)
                    if inc is not None and inc == own:
                        prev += inc_by
                if eng_key == "y":
                    for sem_key, val in final_waits:
                        eng.wait_ge(sems[sem_key], val)
            return _run

        block.tensor(runner("t"))
        block.vector(runner("v"))
        block.scalar(runner("s"))
        block.sync(runner("y"))
        block.gpsimd(runner("g"))

    for g in reversed(ctx):
        g.__exit__(None, None, None)
    return nc


def make_in_maps(inputs, n_cores=8):
    c = np.ascontiguousarray
    base = {
        "winT": c(np.asarray(inputs["L0_attn_W_in"], dtype=np.float32).T),
        "convw": c(np.asarray(inputs["L0_attn_conv_w"], dtype=np.float32)
                   .reshape(18, 128, 4).transpose(1, 0, 2).reshape(128, 72)),
        "convb": c(np.asarray(inputs["L0_attn_conv_b"], dtype=np.float32)
                   .reshape(18, 128).T),
        "dtb": c(np.asarray(inputs["L0_attn_dt_bias"], dtype=np.float32)
                 .reshape(NH, 1)),
        "alog": c(np.asarray(inputs["L0_attn_A_log"], dtype=np.float32)
                  .reshape(NH, 1)),
        "dvec": c(np.asarray(inputs["L0_attn_D"], dtype=np.float32).reshape(1, NH)),
        "nw": c(np.asarray(inputs["L0_attn_norm_weight"], dtype=np.float32)
                .reshape(NH, 128).T),
        "woutT": c(np.asarray(inputs["L0_attn_W_out"], dtype=np.float32).T),
        "wgT": c(np.asarray(inputs["L0_ffn_w_gate"], dtype=np.int32).T),
        "wuT": c(np.asarray(inputs["L0_ffn_w_up"], dtype=np.int32).T),
        "wdT": c(np.asarray(inputs["L0_ffn_w_down"], dtype=np.int32).T),
    }
    u = np.asarray(inputs["L0_attn_u"], dtype=np.float32)
    maps = []
    for i in range(n_cores):
        m = dict(base)
        m["u"] = c(u[i])
        maps.append(m)
    return maps


def kernel(**inputs):
    nc = build_nc(n_outer=8)
    in_maps = make_in_maps(inputs, n_cores=8)
    res = run_bass_kernel_spmd(nc, in_maps, core_ids=list(range(8)))
    return np.stack([np.asarray(r["out"], dtype=np.float32) for r in res.results],
                    axis=0)


# version 31 (final)
# speedup vs baseline: 1.0285x; 1.0227x over previous
import numpy as np
import concourse.bass as bass
import concourse.mybir as mybir
from concourse.bass_utils import run_bass_kernel_spmd

F32 = mybir.dt.float32
BF16 = mybir.dt.bfloat16
I32 = mybir.dt.int32
AF = mybir.ActivationFunctionType
ALU = mybir.AluOpType
AX = mybir.AxisListType

DM = 1024
DI = 2048
NH = 16
HD = 128
CONVD = 2304
JPROJ = 4368
HID = 3072
CH = 128
T = 256
EPS = 1e-5
QS = 1.0 / 127.0
NEG = -1e9

SW, GW, ST, SV, SS, SO = "sw", "gw", "st", "sv", "ss", "so"
SG = "sg"


class Prog:
    ENGS = ("t", "v", "s", "y", "g")
    POOLS = {"w": ["w0", "w1", "w2", "w3"],
             "g": ["g0", "g1", "g2", "g3", "g4", "g5"],
             "o": ["o0", "o1"], "p": ["p0", "p1", "p2", "p3"]}

    def __init__(self):
        self.q = {e: [] for e in self.ENGS}
        keys = [ST, SV, SS, SG] + [s for p in self.POOLS.values() for s in p]
        self.cnt = {k: 0 for k in keys}
        self.rr = {p: 0 for p in self.POOLS}

    def emit(self, eng, fn, waits=(), inc=None, inc_by=1):
        self.q[eng].append((tuple(waits), fn, inc, inc_by))
        if inc is not None:
            self.cnt[inc] += inc_by
            return self.cnt[inc]
        return None

    def at(self, sem):
        return self.cnt[sem]

    def dma(self, eng, pool, fn, waits=()):
        sems = self.POOLS[pool]
        s = sems[self.rr[pool] % len(sems)]
        self.rr[pool] += 1
        w = list(waits)
        if self.cnt[s] > 0:
            w.append((s, self.cnt[s]))
        v = self.emit(eng, fn, waits=w, inc=s, inc_by=16)
        return (s, v)

    def snap(self, pool):
        return [(s, self.cnt[s]) for s in self.POOLS[pool] if self.cnt[s] > 0]


def build_nc(n_outer=8, dbg=False):
    L = n_outer * T
    nc = bass.Bass("TRN2", target_bir_lowering=False)

    u_h = nc.dram_tensor("u", [L, DM], F32, kind="ExternalInput").ap()
    winT = nc.dram_tensor("winT", [DM, JPROJ], F32, kind="ExternalInput").ap()
    convw_h = nc.dram_tensor("convw", [128, 18 * 4], F32, kind="ExternalInput").ap()
    convb_h = nc.dram_tensor("convb", [128, 18], F32, kind="ExternalInput").ap()
    dtb_h = nc.dram_tensor("dtb", [NH, 1], F32, kind="ExternalInput").ap()
    alog_h = nc.dram_tensor("alog", [NH, 1], F32, kind="ExternalInput").ap()
    dvec_h = nc.dram_tensor("dvec", [1, NH], F32, kind="ExternalInput").ap()
    nw_h = nc.dram_tensor("nw", [128, NH], F32, kind="ExternalInput").ap()
    woutT = nc.dram_tensor("woutT", [DI, DM], F32, kind="ExternalInput").ap()
    wgT = nc.dram_tensor("wgT", [DM, HID], I32, kind="ExternalInput").ap()
    wuT = nc.dram_tensor("wuT", [DM, HID], I32, kind="ExternalInput").ap()
    wdT = nc.dram_tensor("wdT", [HID, DM], I32, kind="ExternalInput").ap()
    out_h = nc.dram_tensor("out", [L, DM], F32, kind="ExternalOutput").ap()
    wg16_h = nc.dram_tensor("wg16", [DM, HID], BF16, kind="Internal").ap()
    wu16_h = nc.dram_tensor("wu16", [DM, HID], BF16, kind="Internal").ap()
    wd16_h = nc.dram_tensor("wd16", [HID, DM], BF16, kind="Internal").ap()
    win16h_h = nc.dram_tensor("win16h", [DM, JPROJ], BF16, kind="Internal").ap()
    win16l_h = nc.dram_tensor("win16l", [DM, JPROJ], BF16, kind="Internal").ap()
    acsf_d = nc.dram_tensor("acsf_d", [1, 2048], F32, kind="Internal").ap()
    nacsf_d = nc.dram_tensor("nacsf_d", [1, 2048], F32, kind="Internal").ap()
    wout16h_h = nc.dram_tensor("wout16h", [DI, DM], BF16, kind="Internal").ap()
    wout16l_h = nc.dram_tensor("wout16l", [DI, DM], BF16, kind="Internal").ap()

    dbg_h = {}
    if dbg:
        for nm, shape in [("d_xnf", [128, 8 * T]), ("d_sz", [128, 16 * T]),
                          ("d_xbcc", [128, 18 * T]), ("d_dts", [16, 4 * T]),
                          ("d_ynw", [128, 16 * T]), ("d_att", [128, 2 * DM]),
                          ("d_carry", [128, NH * HD]),
                          ("d_mt", [128, 512]), ("d_cst", [128, 512]),
                          ("d_em", [128, 512]), ("d_er", [128, 512])]:
            dbg_h[nm] = nc.dram_tensor(nm, shape, F32, kind="ExternalOutput").ap()

    ident_h = nc.inline_tensor(np.eye(128, dtype=np.float32), name="identc").ap()
    onesc_h = nc.inline_tensor(np.ones((128, 1), dtype=np.float32), name="onescc").ap()
    onesr_h = nc.inline_tensor(np.ones((1, 128), dtype=np.float32), name="onesrc").ap()
    mt_np = np.where(np.arange(128)[None, :] >= np.arange(128)[:, None],
                     1.0, 0.0).astype(np.float32)
    masktri_h = nc.inline_tensor(mt_np, name="masktric").ap()
    tr_np = (np.arange(128)[:, None] <= np.arange(128)[None, :]).astype(np.float32)
    trilones_h = nc.inline_tensor(tr_np, name="trilonesc").ap()

    P = Prog()
    ctx = []

    def sb(name, p, cols, dt=F32):
        g = nc.sbuf_tensor(name, [p, cols], dt)
        ctx.append(g)
        return g.__enter__()

    ident = sb("ident", 128, 128)
    ones_col = sb("ones_col", 128, 1)
    ones_row = sb("ones_row", 1, 128)
    masktri = sb("masktri", 128, 128)
    trilones = sb("trilones", 128, 128)
    convw_sb = sb("convw_sb", 128, 18 * 4)
    convb_sb = sb("convb_sb", 128, 18)
    dtb_sb = sb("dtb_sb", NH, 1)
    c_one = sb("c_one", NH, 1)
    c_tiny = sb("c_tiny", NH, 1)
    alog_sb = sb("alog_sb", NH, 1)
    a_tmp = sb("a_tmp", NH, 1)
    a_col = sb("a_col", NH, 1)
    d_row = sb("d_row", 1, NH)
    db128 = sb("db128", 128, NH)
    nw_sb = sb("nw_sb", 128, NH)
    di_sb = sb("di_sb", 128, NH * HD)
    carry = sb("carry", 128, NH * HD)
    tail_sb = sb("tail_sb", 128, 18 * 3)

    u_sb = sb("u_sb", 128, 2 * DM)
    ln_s1 = sb("ln_s1", 128, 8)
    xn_f = sb("xn_f", 128, 8 * T)
    arenaA = sb("arenaA", 128, 18 * (T + 3))
    actB = sb("actB", 128, 8704)
    dts = sb("dts", NH, 4 * T)
    att_sb = sb("att_sb", 128, 2 * DM)
    out_sb = sb("out_sb", 128, 2 * DM)

    dAT = sb("dAT", 128, NH)
    acs_h = sb("acs_h", NH, CH)
    nacs_h = sb("nacs_h", NH, CH)
    dch = sb("dch", NH, CH)
    sc_h = sb("sc_h", NH, CH)
    scT = sb("scT", 128, NH)
    cdb128 = sb("cdb128", 128, NH)
    btok = sb("btok", 128, CH)
    bsc = sb("bsc", 128, 4 * CH)
    xtok = sb("xtok", 128, NH * CH)
    expmt4 = sb("expmt4", 128, 512)
    expr4 = sb("expr4", 128, 512)
    mt_sb = sb("mt_sb", 128, 4 * CH)
    cst = sb("cst", 128, 4 * CH)
    y1_sb = sb("y1_sb", 128, 4 * CH)
    ysq = sb("ysq", 128, 4 * CH)
    ssqrow = sb("ssqrow", 1, T)
    rstdrow = sb("rstdrow", 1, T)
    rstd_col = sb("rstd_col", 128, 2)
    ffacc = sb("ffacc", 128, DM)
    sg = sb("sg", 128, 2 * T)
    wbufW = sb("wbufW", 128, 16384, BF16)
    wbuf16 = sb("wbuf16", 128, 12288, BF16)
    xnh = sb("xnh", 128, 8 * T, BF16)
    xnl = sb("xnl", 128, 8 * T, BF16)
    hhb = sb("hhb", 128, 2 * T, BF16)
    hlb = sb("hlb", 128, 2 * T, BF16)

    convtmp = ffacc[:, 0:T]

    pbg = [nc.psum_tensor(f"pb{i}", [128, 512], F32) for i in range(8)]
    ctx.extend(pbg)
    pb = [g.__enter__() for g in pbg]

    def xbcpre(f):
        return arenaA[:, f * (T + 3):(f + 1) * (T + 3)]

    def ynw(h):
        return arenaA[:, h * T:(h + 1) * T]

    def szt(f):
        return actB[:, f * T:(f + 1) * T]

    def xbcc(f):
        return actB[:, 4096 + f * T:4096 + (f + 1) * T]

    def hffn(j):
        return actB[:, j * T:(j + 1) * T]

    def wslabW(i, lo):
        return wbufW[:, i * 8192 + lo * 4096:i * 8192 + lo * 4096 + 4096]

    def wqW(pos, lo):
        return wbufW[:, pos * 2048 + lo * 1024:pos * 2048 + (lo + 1) * 1024]

    def wslab16(i):
        return wbuf16[:, i * 4096:(i + 1) * 4096]

    def wq16(pos):
        return wbuf16[:, pos * 1024:(pos + 1) * 1024]

    def d(fn):
        return P.dma("y", "w", fn)

    d(lambda e: e.dma_start(out=ident[:], in_=ident_h[:, :]))
    d(lambda e: e.dma_start(out=ones_col[:], in_=onesc_h[:, :]))
    d(lambda e: e.dma_start(out=ones_row[:], in_=onesr_h[:, :]))
    d(lambda e: e.dma_start(out=masktri[:], in_=masktri_h[:, :]))
    d(lambda e: e.dma_start(out=trilones[:], in_=trilones_h[:, :]))
    d(lambda e: e.dma_start(out=convw_sb[:], in_=convw_h[:, :]))
    d(lambda e: e.dma_start(out=convb_sb[:], in_=convb_h[:, :]))
    d(lambda e: e.dma_start(out=dtb_sb[:], in_=dtb_h[:, :]))
    d(lambda e: e.dma_start(out=alog_sb[:], in_=alog_h[:, :]))
    d(lambda e: e.dma_start(out=d_row[:], in_=dvec_h[:, :]))
    d(lambda e: e.dma_start(out=nw_sb[:], in_=nw_h[:, :]))
    d(lambda e: e.dma_start(
        out=u_sb[:, 0:2 * DM],
        in_=u_h.rearrange("(a p) n -> p a n", p=128)[:, 0:2, :]))
    m_init = P.snap("w")

    m_prep = [P.dma("g", "g", lambda e, dst=dst, s_=s_: e.dma_start(
        out=dst[:, :], in_=s_[:, :]))
        for dst, s_ in [(wg16_h, wgT), (wu16_h, wuT), (wd16_h, wdT)]]
    P.emit("v", lambda e: e.memset(carry[:], 0.0), inc=SV)
    P.emit("v", lambda e: e.memset(tail_sb[:], 0.0), inc=SV)
    P.emit("v", lambda e: e.memset(c_one[:], 1.0), inc=SV)
    P.emit("v", lambda e: e.memset(c_tiny[:], 1e-30), inc=SV)
    P.emit("s", lambda e: e.activation(a_tmp[:], alog_sb[:], AF.Exp),
           waits=m_init, inc=SS)
    P.emit("v", lambda e: e.tensor_scalar(a_col[:], a_tmp[:], -1.0, None, ALU.mult),
           waits=[(SS, P.at(SS))], inc=SV)
    P.emit("t", lambda e: e.matmul(pb[0][0:128, 0:NH], ones_row[:], d_row[:],
                                   start=True, stop=True),
           waits=m_init, inc=ST)
    P.emit("v", lambda e: e.tensor_copy(db128[:], pb[0][0:128, 0:NH]),
           waits=[(ST, P.at(ST))], inc=SV)
    m_db = P.at(SV)
    for h in range(NH):
        P.emit("v", lambda e, h=h: e.tensor_scalar(
            di_sb[:, h * HD:(h + 1) * HD], ident[:], db128[:, h:h + 1], None, ALU.mult),
            waits=[(SV, m_db)] if h == 0 else (), inc=SV)

    xn_tok = arenaA
    lnsq_a, lnsq_b = expmt4, expr4

    def emit_LN_half(o, tsub):
        if True:
            ut = u_sb[:, tsub * DM:(tsub + 1) * DM]
            c0 = tsub * 4
            P.emit("v", lambda e, ut=ut, c0=c0: e.reduce_sum(
                ln_s1[:, c0:c0 + 1], ut, axis=AX.X),
                waits=[*m_u[o][tsub]], inc=SV)
            P.emit("s", lambda e, ut=ut: e.activation(
                lnsq_a[:], ut[:, 0:512], AF.Square),
                waits=[*m_u[o][tsub], (SV, P.at(SV))], inc=SS)
            P.emit("s", lambda e, ut=ut: e.activation(
                lnsq_b[:], ut[:, 512:1024], AF.Square), inc=SS)
            m_sq = P.at(SS)
            P.emit("v", lambda e, c0=c0: e.reduce_sum(
                ln_s1[:, c0 + 1:c0 + 2], lnsq_a[:], axis=AX.X),
                waits=[(SS, m_sq)], inc=SV)
            P.emit("v", lambda e, c0=c0: e.reduce_sum(
                ln_s1[:, c0 + 3:c0 + 4], lnsq_b[:], axis=AX.X), inc=SV)
            P.emit("v", lambda e, c0=c0: e.tensor_tensor(
                ln_s1[:, c0 + 1:c0 + 2], ln_s1[:, c0 + 1:c0 + 2],
                ln_s1[:, c0 + 3:c0 + 4], ALU.add), inc=SV)
            P.emit("v", lambda e, c0=c0: e.tensor_scalar(
                ln_s1[:, c0:c0 + 1], ln_s1[:, c0:c0 + 1], 1.0 / DM, None, ALU.mult),
                inc=SV)
            P.emit("v", lambda e, c0=c0: e.tensor_scalar(
                ln_s1[:, c0 + 1:c0 + 2], ln_s1[:, c0 + 1:c0 + 2], 1.0 / DM, EPS,
                ALU.mult, ALU.add), inc=SV)
            P.emit("v", lambda e, c0=c0: e.tensor_tensor(
                ln_s1[:, c0 + 2:c0 + 3], ln_s1[:, c0:c0 + 1], ln_s1[:, c0:c0 + 1],
                ALU.mult), inc=SV)
            P.emit("v", lambda e, c0=c0: e.tensor_tensor(
                ln_s1[:, c0 + 1:c0 + 2], ln_s1[:, c0 + 1:c0 + 2],
                ln_s1[:, c0 + 2:c0 + 3], ALU.subtract), inc=SV)
            P.emit("s", lambda e, c0=c0: e.activation(
                ln_s1[:, c0 + 3:c0 + 4], ln_s1[:, c0 + 1:c0 + 2], AF.Sqrt),
                waits=[(SV, P.at(SV))], inc=SS)
            P.emit("v", lambda e, c0=c0: e.reciprocal(
                ln_s1[:, c0 + 2:c0 + 3], ln_s1[:, c0 + 3:c0 + 4]),
                waits=[(SS, P.at(SS))], inc=SV)
            P.emit("v", lambda e, c0=c0: e.tensor_tensor(
                ln_s1[:, c0 + 3:c0 + 4], ln_s1[:, c0:c0 + 1],
                ln_s1[:, c0 + 2:c0 + 3], ALU.mult), inc=SV)
            P.emit("v", lambda e, ut=ut, tsub=tsub, c0=c0: e.tensor_scalar(
                xn_tok[:, tsub * DM:(tsub + 1) * DM], ut, ln_s1[:, c0 + 2:c0 + 3],
                ln_s1[:, c0 + 3:c0 + 4], ALU.mult, ALU.subtract),
                waits=[(ST, P.at(ST))] if tsub == 0 else (), inc=SV)
        return P.at(SV)

    def emit_LN_tp(m_xntok):
        for k in range(8):
            bk = pb[2 + (k % 2)]
            for tsub in range(2):
                P.emit("t", lambda e, k=k, tsub=tsub, bk=bk: e.transpose(
                    bk[0:128, tsub * 128:(tsub + 1) * 128],
                    xn_tok[:, tsub * DM + k * 128: tsub * DM + (k + 1) * 128],
                    ident[:]),
                    waits=[(SV, m_xntok), (SS, P.at(SS))] if (tsub == 0 and k < 2)
                    else ([(SV, P.at(SV))] if tsub == 0 else ()), inc=ST)
            m_tp = P.at(ST)
            P.emit("v", lambda e, k=k, bk=bk: e.tensor_copy(
                xn_f[:, k * T:(k + 1) * T], bk[0:128, 0:256]),
                waits=[(ST, m_tp)], inc=SV)
        return P.at(SV)

    NWG = 9
    PSCHED = [0, 0, 1, 1, 1, 1, 1, 1, 1, 1, 1, 1, 1, 1, 1, 1]
    m_prev_store = []
    m_prev_tensor_end = 0
    m_prev_gu_end = 0
    m_prev_ffev = 0
    m_prev_attmm = 0
    m_prev_guDE = 0
    m_prev_hv = 0
    m_prev_gu_ss = 0
    m_gu_start = 0

    m_u = [[m_init, m_init]] + [None] * n_outer

    GSEQ = [4, 5, 6, 7, 8, 0, 1, 2, 3]
    PCH = []
    for g_ in GSEQ:
        gw_ = 512 if g_ < 8 else JPROJ - 8 * 512
        PCH.append(("wi", g_, 0, 256))
        PCH.append(("wi", g_, 256, gw_ - 256))
    for k_ in range(16):
        PCH.append(("wo", k_, 0, 1024))
    NPCH = len(PCH)
    m_pcp = [0] * NPCH
    m_psb = [0] * NPCH
    m_pmm = [0] * NPCH
    pst_snap = [[] for _ in range(NPCH)]

    def pstag(c, coloff, w, lo):
        s3 = c % 3
        return wbuf16[:, s3 * 4096 + lo * 2048 + coloff:
                      s3 * 4096 + lo * 2048 + coloff + w]

    def emit_prep_chunk(i, ldwaits=()):
        kind, idx, j0, jw = PCH[i]
        ncols = 8 * jw if kind == "wi" else jw
        sf = att_sb if i % 2 == 0 else out_sb
        fv = sf[:, 0:ncols]
        hv = pstag(i, 0, ncols, 0)
        lv = pstag(i, 0, ncols, 1)
        if kind == "wi":
            j0a = idx * 512 + j0
            src = winT.rearrange("(k p) j -> p k j", p=128)[:, :, j0a:j0a + jw]
            dsth = win16h_h.rearrange("(k p) j -> p k j", p=128)[:, :, j0a:j0a + jw]
            dstl = win16l_h.rearrange("(k p) j -> p k j", p=128)[:, :, j0a:j0a + jw]
            fvr = fv.rearrange("p (k j) -> p k j", k=8)
            hvr = hv.rearrange("p (k j) -> p k j", k=8)
            lvr = lv.rearrange("p (k j) -> p k j", k=8)
        else:
            src = woutT.rearrange("(a p) n -> a p n", p=128)[idx]
            dsth = wout16h_h.rearrange("(a p) n -> a p n", p=128)[idx]
            dstl = wout16l_h.rearrange("(a p) n -> a p n", p=128)[idx]
            fvr, hvr, lvr = fv, hv, lv
        wld = list(ldwaits)
        if i >= 2:
            wld += [(SS, m_pcp[i - 2]), (SV, m_psb[i - 2])]
        ld = P.dma("y", "w", lambda e: e.dma_start(out=fvr, in_=src), waits=wld)
        wcp = [ld]
        if i >= 3:
            if m_pmm[i - 3] > 0:
                wcp.append((ST, m_pmm[i - 3]))
            wcp += pst_snap[i - 3]
        m_pcp[i] = P.emit("s", lambda e: e.copy(hv, fv), waits=wcp, inc=SS)
        m_psb[i] = P.emit("v", lambda e: e.tensor_tensor(
            lv, fv, hv, ALU.subtract), waits=[(SS, m_pcp[i])], inc=SV)
        s1 = P.dma("g", "p", lambda e: e.dma_start(out=dsth, in_=hvr),
                   waits=[(SS, m_pcp[i])])
        s2 = P.dma("g", "p", lambda e: e.dma_start(out=dstl, in_=lvr),
                   waits=[(SV, m_psb[i])])
        pst_snap[i] = [s1, s2]

    def emit_xnsplit(m_xnf_v, m_guend):
        for k in range(8):
            P.emit("s", lambda e, k=k: e.copy(
                xnh[:, k * T:(k + 1) * T], xn_f[:, k * T:(k + 1) * T]),
                waits=[(ST, m_guend), (SV, m_xnf_v)] if k == 0 else (),
                inc=SS)
            P.emit("v", lambda e, k=k: e.tensor_tensor(
                xnl[:, k * T:(k + 1) * T], xn_f[:, k * T:(k + 1) * T],
                xnh[:, k * T:(k + 1) * T], ALU.subtract),
                waits=[(SS, P.at(SS))], inc=SV)
        return P.at(SV)

    ln_state = {}
    xnsplit_state = {}
    for o in range(n_outer):
        ub = 0
        if o in xnsplit_state:
            m_xnsplit = xnsplit_state.pop(o)
            ln_state.pop(o, None)
        else:
            if o in ln_state:
                m_xnf = ln_state.pop(o)
            else:
                emit_LN_half(o, 0)
                m_xnf = emit_LN_tp(emit_LN_half(o, 1))
            m_xnsplit = emit_xnsplit(m_xnf, m_prev_gu_end)

        m_gu = [[None, None] for _ in range(6)]
        GUSLOTS = [wbuf16[:, 0:4096], wbuf16[:, 4096:8192],
                   wbufW[:, 0:4096], wbufW[:, 4096:8192],
                   wbufW[:, 8192:12288], wbufW[:, 12288:16384]]

        def emit_gu_dma(g, gu, waits):
            src = wg16_h if gu == 0 else wu16_h
            buf = GUSLOTS[(2 * g + gu) % 6]
            m_gu[g][gu] = [P.dma(
                "g", "g", lambda e, g=g, src=src, buf=buf: e.dma_start(
                    out=buf.rearrange("p (k j) -> p k j", k=8)[:, :, :],
                    in_=src.rearrange("(k p) j -> p k j", p=128)
                    [:, :, g * 512:(g + 1) * 512]),
                waits=waits)]

        if o > 0:
            emit_gu_dma(0, 0, [(ST, m_prev_gu_end)])
            emit_gu_dma(0, 1, [(ST, m_prev_gu_end)])

        m_wg = [None] * NWG
        m_csilu = [0] * 18

        def emit_win_dma(pos, waits):
            g = GSEQ[pos]
            gw = 512 if g < 8 else JPROJ - 8 * 512
            w2 = list(waits)
            if o == 1:
                w2 += pst_snap[2 * pos] + pst_snap[2 * pos + 1]
            dmas = []
            for lo, srcw in ((0, win16h_h), (1, win16l_h)):
                buf = wslabW(pos % 2, lo)
                dmas.append(P.dma(
                    "y", "w", lambda e, g=g, gw=gw, buf=buf, srcw=srcw: e.dma_start(
                        out=buf.rearrange("p (k j) -> p k j", k=8)[:, :, 0:gw],
                        in_=srcw.rearrange("(k p) j -> p k j", p=128)
                        [:, :, g * 512:g * 512 + gw]),
                    waits=w2))
            m_wg[g] = dmas

        def emit_conv(f):
            cacc = sg[:, (f % 2) * T:(f % 2) * T + T]
            P.emit("v", lambda e, f=f: e.tensor_copy(
                xbcpre(f)[:, 0:3], tail_sb[:, f * 3:(f + 1) * 3]), inc=SV)
            P.emit("v", lambda e, f=f, cacc=cacc: e.tensor_scalar(
                cacc, xbcpre(f)[:, 0:T],
                convw_sb[:, f * 4 + 0: f * 4 + 1], None, ALU.mult),
                waits=[(SS, m_csilu[f - 2])] if f >= 2 else (), inc=SV)
            for k in range(1, 4):
                P.emit("v", lambda e, f=f, k=k: e.tensor_scalar(
                    convtmp, xbcpre(f)[:, k:k + T],
                    convw_sb[:, f * 4 + k: f * 4 + k + 1], None, ALU.mult), inc=SV)
                P.emit("v", lambda e, cacc=cacc: e.tensor_tensor(
                    cacc, cacc, convtmp, ALU.add), inc=SV)
            m_cacc = P.at(SV)
            m_csilu[f] = P.emit("s", lambda e, f=f, cacc=cacc: e.activation(
                xbcc(f), cacc, AF.Silu, bias=convb_sb[:, f:f + 1]),
                waits=[(SV, m_cacc)], inc=SS)
            P.emit("v", lambda e, f=f: e.tensor_copy(
                tail_sb[:, f * 3:(f + 1) * 3], xbcpre(f)[:, T:T + 3]), inc=SV)

        ssm_units = []
        ssm_res = {}

        def build_prep_units(cc, svgate, ssgate):
            col = cc * CH
            tb0 = pb[2] if cc == 0 else pb[0]
            tb1 = pb[3] if cc == 0 else pb[1]
            res = {}
            U = []

            def u1():
                P.emit("t", lambda e: e.transpose(
                    tb0[0:128, 0:NH], dts[:, 2 * T + col:2 * T + col + CH],
                    ident[0:NH, 0:NH]),
                    waits=[(SV, svgate), (SS, P.at(SS))], inc=ST)
                res["t1"] = P.at(ST)
            U.append(u1)

            def u2():
                P.emit("v", lambda e: e.tensor_copy(dAT[:], tb0[0:128, 0:NH]),
                       waits=[(ST, res["t1"])], inc=SV)
                m_datv = P.at(SV)
                P.emit("t", lambda e: e.matmul(tb1[0:16, 0:CH], dAT[:, 0:NH],
                                               trilones[:],
                                               start=True, stop=True),
                       waits=[(SV, m_datv)], inc=ST)
                res["acs"] = P.at(ST)
            U.append(u2)

            def u3():
                P.emit("v", lambda e: e.tensor_copy(acs_h[:], tb1[0:16, 0:CH]),
                       waits=[(ST, res["acs"])], inc=SV)
                res["acshv"] = P.at(SV)
                P.emit("s", lambda e: e.activation(
                    dch[:], acs_h[:], AF.Exp, scale=-1.0,
                    bias=acs_h[:, CH - 1:CH]),
                    waits=[(SV, res["acshv"])], inc=SS)
                res["nacshv"] = P.emit("v", lambda e: e.tensor_tensor(
                    nacs_h[:], dts[:, 3 * T + col:3 * T + col + CH],
                    acs_h[:], ALU.subtract), inc=SV)
            U.append(u3)

            def u4():
                P.emit("v", lambda e: e.tensor_tensor(
                    sc_h[:], dch[:], dts[:, T + col:T + col + CH], ALU.mult),
                    waits=[(SS, P.at(SS))], inc=SV)
                m_sch = P.at(SV)
                P.emit("t", lambda e: e.transpose(tb1[0:128, NH:2 * NH],
                                                  sc_h[:], ident[0:NH, 0:NH]),
                       waits=[(SV, m_sch)], inc=ST)
                res["t2"] = P.at(ST)
            U.append(u4)

            def u5():
                P.emit("v", lambda e: e.tensor_copy(scT[:],
                                                    tb1[0:128, NH:2 * NH]),
                       waits=[(ST, res["t2"])], inc=SV)
            U.append(u5)

            def u6():
                w1 = [(SV, res["acshv"])]
                d1a = P.dma("y", "w", lambda e: e.dma_start(
                    out=acsf_d.rearrange("o (h q) -> (o h) q", q=CH),
                    in_=acs_h[:, :]), waits=w1)
                d2a = P.dma("y", "w", lambda e: e.dma_start(
                    out=nacsf_d.rearrange("o (h q) -> (o h) q", q=CH),
                    in_=nacs_h[:, :]), waits=[(SV, res["nacshv"])])
                w1b = [d1a]
                w2b = [d2a] + m_prev_store
                if o == 0 and cc == 0:
                    w1b += [(SS, m_pcp[16]), (SV, m_psb[16])]
                    w2b += [(SS, m_pcp[17]), (SV, m_psb[17])]
                d1b = P.dma("y", "w", lambda e: e.dma_start(
                    out=att_sb[0:1, 0:2048], in_=acsf_d[:, :]), waits=w1b)
                d2b = P.dma("y", "w", lambda e: e.dma_start(
                    out=out_sb[0:1, 0:2048], in_=nacsf_d[:, :]), waits=w2b)
                res["acsflat_w"] = [d1b, d2b]
            U.append(u6)

            def u10():
                P.emit("t", lambda e: e.matmul(
                    tb1[0:128, 2 * NH:3 * NH], ones_row[:],
                    att_sb[0:1, 0:2048].rearrange("o (h q) -> o h q", q=CH)
                    [:, :, CH - 1:CH], start=True, stop=True),
                    waits=list(res["acsflat_w"]), inc=ST)
                m_cdbt = P.at(ST)
                P.emit("s", lambda e: e.activation(
                    cdb128[:], tb1[0:128, 2 * NH:3 * NH], AF.Exp),
                    waits=[(ST, m_cdbt)], inc=SS)
                res["cdb"] = P.at(SS)
                P.emit("t", lambda e: e.transpose(
                    tb0[0:128, 128:256], xbcc(16)[:, col:col + CH], ident[:]),
                    waits=[(SS, ssgate), (SV, P.at(SV))], inc=ST)
                m_bt = P.at(ST)
                P.emit("v", lambda e: e.tensor_copy(btok[:],
                                                    tb0[0:128, 128:256]),
                       waits=[(ST, m_bt)], inc=SV)
            U.append(u10)

            def mk_xt(hs):
                def ux():
                    for h in hs:
                        bk = tb0 if h % 2 == 0 else tb1
                        P.emit("t", lambda e, h=h, bk=bk: e.transpose(
                            bk[0:128, 256:384], xbcc(h)[:, col:col + CH],
                            ident[:]), waits=[(SV, P.at(SV))], inc=ST)
                        m_xt = P.at(ST)
                        P.emit("v", lambda e, h=h, bk=bk: e.tensor_copy(
                            xtok[:, h * CH:(h + 1) * CH], bk[0:128, 256:384]),
                            waits=[(ST, m_xt)], inc=SV)
                return ux
            for hs in ([0, 1, 2, 3], [4, 5, 6, 7], [8, 9, 10, 11],
                       [12, 13, 14, 15]):
                U.append(mk_xt(hs))

            def u15():
                m_xtv = P.at(SV)
                P.emit("t", lambda e: e.matmul(
                    tb1[0:128, 384:512], xbcc(16)[:, col:col + CH],
                    xbcc(17)[:, col:col + CH], start=True, stop=True),
                    waits=[(SV, m_xtv), (SS, P.at(SS))], inc=ST)
                m_cbt = P.at(ST)
                P.emit("v", lambda e: e.tensor_tensor(
                    sg[:, 0:CH], tb1[0:128, 384:512], masktri[:], ALU.mult),
                    waits=[(ST, m_cbt)], inc=SV)
                res["prepv"] = P.at(SV)
            U.append(u15)
            return U, res

        if o == 0:
            for ci in range(3):
                emit_prep_chunk(ci)
        else:
            emit_win_dma(0, [(ST, m_prev_guDE)])
            emit_win_dma(1, [(ST, m_prev_gu_end)])

        bank_free = {i: (ST, 0) for i in range(4)}
        m_preS_sv = m_preS_ss = None
        zslot = 0
        for pos in range(NWG):
            g = GSEQ[pos]
            njt = 4 if g < 8 else 3
            for jj in range(njt):
                j = g * 4 + jj
                is_dt = (g == 8 and jj == 2)
                jw = 16 if is_dt else 128
                if pos == 0:
                    bank, bkey = pb[4 + jj], None
                elif pos >= 5:
                    bank, bkey = pb[jj % 2], jj % 2
                else:
                    bank, bkey = pb[j % 4], j % 4
                pview = bank[0:jw, 0:T]
                c = 2 * pos + (0 if jj < 2 else 1)
                if o == 0:
                    jwc = PCH[c][3]
                    joff = jj * 128 - PCH[c][2]

                    def wv(k, lo, c=c, jwc=jwc, joff=joff, jw=jw):
                        return pstag(c, k * jwc + joff, jw, lo)
                else:
                    def wv(k, lo, pos=pos, jj=jj, jw=jw):
                        return wslabW(pos % 2, lo)[:, k * 512 + jj * 128:
                                                   k * 512 + jj * 128 + jw]
                for k in range(8):
                    waits = []
                    if k == 0:
                        if o == 0:
                            waits += [(SS, m_pcp[c]), (SV, m_psb[c])]
                        else:
                            waits.extend(m_wg[g])
                        waits.append((SV, m_xnsplit))
                        if pos == 0:
                            if m_prev_hv > 0:
                                waits += [(SV, m_prev_hv), (SS, m_prev_gu_ss)]
                        else:
                            bf = bank_free[bkey]
                            if bf[1] > 0:
                                waits.append(bf)
                            elif m_prev_ffev > 0:
                                waits.append((SS, m_prev_ffev))
                    for pi, (lo, xs) in enumerate(((0, xnh), (0, xnl), (1, xnh))):
                        P.emit("t", lambda e, k=k, pi=pi, lo=lo, xs=xs, wv=wv,
                               pview=pview: e.matmul(
                            pview, wv(k, lo), xs[:, k * T:(k + 1) * T],
                            start=(k == 0 and pi == 0),
                            stop=(k == 7 and pi == 2)),
                            waits=waits if pi == 0 else (),
                            inc=(ST if (k == 7 and pi == 2) else None))
                        waits = []
                if o == 0:
                    m_pmm[c] = P.at(ST)
                m_mm = P.at(ST)
                if j < 16:
                    mev = P.emit("s", lambda e, j=j, pview=pview: e.activation(
                        szt(j), pview, AF.Silu), waits=[(ST, m_mm)], inc=SS)
                    bank_free[bkey] = (SS, mev)
                elif not is_dt:
                    f = j - 16
                    mev = P.emit("v", lambda e, f=f, pview=pview: e.tensor_copy(
                        xbcpre(f)[:, 3:3 + T], pview), waits=[(ST, m_mm)], inc=SV)
                    if pos > 0:
                        bank_free[bkey] = (SV, mev)
                    emit_conv(f)
                else:
                    mev = P.emit("v", lambda e, pview=pview: e.tensor_copy(
                        dts[:, 0:T], pview), waits=[(ST, m_mm)], inc=SV)
                    bank_free[bkey] = (SV, mev)
                if ssm_units:
                    if PSCHED[min(zslot, 15)]:
                        ssm_units.pop(0)()
                    zslot += 1
            if o == 0:
                for ci in (2 * pos + 3, 2 * pos + 4):
                    if ci < 18:
                        emit_prep_chunk(ci)
            elif pos + 2 < NWG:
                emit_win_dma(pos + 2, [(ST, P.at(ST))])
            if g == 8:
                m_dt_ev = P.at(SV)
                P.emit("s", lambda e: e.activation(dts[:, T:2 * T], dts[:, 0:T],
                                                   AF.Exp, bias=dtb_sb[:, 0:1]),
                       waits=[(SV, m_dt_ev)], inc=SS)
                P.emit("s", lambda e: e.activation(dts[:, T:2 * T],
                                                   dts[:, T:2 * T], AF.Ln,
                                                   bias=c_one[:, 0:1]), inc=SS)
                P.emit("s", lambda e: e.activation(dts[:, 3 * T:4 * T],
                                                   dts[:, T:2 * T], AF.Ln,
                                                   bias=c_tiny[:, 0:1]), inc=SS)
                m_dt_ss = P.at(SS)
                P.emit("v", lambda e: e.tensor_scalar(dts[:, 2 * T:3 * T],
                                                      dts[:, T:2 * T],
                                                      a_col[:, 0:1], None,
                                                      ALU.mult),
                       waits=[(SS, m_dt_ss)], inc=SV)
                m_preS_sv = P.at(SV)
                m_preS_ss = P.at(SS)
                if o > 0:
                    us0, r0 = build_prep_units(0, m_preS_sv, m_preS_ss)
                    ssm_units.extend(us0)
                    ssm_res[0] = r0
        while ssm_units:
            ssm_units.pop(0)()
        m_conv_s = m_preS_ss
        m_conv_v = m_preS_sv
        m_inproj_t = P.at(ST)

        dbg_gate = [[]]
        m_ssq_ev = 0
        m_prev_ssqfin = 0
        for cc in range(2):
            col = cc * CH

            if cc == 0 and o > 0:
                res = ssm_res[0]
            else:
                usx, res = build_prep_units(cc, max(m_preS_sv, m_ssq_ev),
                                            m_conv_s)
                for u_ in usx:
                    u_()
            m_acsflat_w = res["acsflat_w"]
            m_cdb = res["cdb"]
            m_prepv = res["prepv"]
            acs_flat = att_sb
            nacs_flat = out_sb

            NG = 4
            m_bankA = [None] * NG
            m_bankC = [None] * NG
            m_e4 = [None] * NG
            m_expmt4 = [None] * NG
            m_expr4 = [None] * NG
            m_mtc = [None] * NG
            m_bsc4 = [None] * NG
            m_Ylast = [None] * NG
            m_stlast = [None] * NG
            m_cmlast = [None] * NG
            m_y14 = [None] * NG
            m_ysq4 = [None] * NG
            m_ynw4 = [None] * NG
            m_ssqg = [None] * NG
            m_Y = [None] * NH

            mcbt_rep = bass.AP(sg, 0, [[512, 128], [0, 4], [1, 128]])
            c_rep = bass.AP(actB, 4096 + 17 * T + col, [[8704, 128], [0, 4], [1, 128]])

            def emit_bank(g):
                bankA = pb[2 + (g % 2)]
                ybank = pb[4 + (g % 2)]
                wA = list(m_acsflat_w)
                if g < 2:
                    wA.append((SV, m_prepv))
                if g >= 2:
                    wA.append((SS, m_expr4[g - 2]))
                m_bankA[g] = P.emit("t", lambda e, g=g, bankA=bankA: e.matmul(
                    bankA[0:128, 0:512], ones_row[:],
                    acs_flat[0:1, g * 512:(g + 1) * 512],
                    start=True, stop=True), waits=wA, inc=ST)
                wC = [(SV, m_y14[g - 2] if g >= 2 else P.at(SV))]
                P.emit("t", lambda e, g=g, ybank=ybank: e.matmul(
                    ybank[0:128, 0:512], ones_row[:],
                    acs_flat[0:1, g * 512:(g + 1) * 512],
                    start=True, stop=False), waits=wC)
                for hh in range(4):
                    h = 4 * g + hh
                    P.emit("t", lambda e, h=h, hh=hh, ybank=ybank: e.matmul(
                        ybank[0:128, hh * CH:(hh + 1) * CH],
                        nacs_flat[0:1, h * CH:(h + 1) * CH], ones_row[:],
                        start=False, stop=(hh == 3)),
                        inc=(ST if hh == 3 else None))
                m_bankC[g] = P.at(ST)

            def emit_prep_group(g):
                ybank = pb[4 + (g % 2)]
                bankA = pb[2 + (g % 2)]
                w = [(ST, m_bankC[g])]
                if g >= 1:
                    w.append((SS, m_expmt4[g - 1]))
                if dbg_gate[0] and g == 2:
                    w.extend(dbg_gate[0]); dbg_gate[0] = []
                m_e4[g] = P.emit("v", lambda e, ybank=ybank: e.tensor_scalar(
                    expmt4[:], ybank[0:128, 0:512], 80.0, None, ALU.min),
                    waits=w, inc=SV)
                m_expmt4[g] = P.emit("s", lambda e: e.activation(
                    expmt4[:], expmt4[:], AF.Exp), waits=[(SV, m_e4[g])], inc=SS)
                m_expr4[g] = P.emit("s", lambda e, bankA=bankA: e.activation(
                    expr4[:], bankA[0:128, 0:512], AF.Exp),
                    waits=[(SV, m_e4[g])], inc=SS)
                for hh in range(4):
                    h = 4 * g + hh
                    P.emit("s", lambda e, h=h, hh=hh: e.mul(
                        bsc[:, hh * CH:(hh + 1) * CH], btok[:], scT[:, h:h + 1]),
                        waits=[(SV, P.at(SV)),
                               (ST, m_stlast[g - 1] if g >= 1 else P.at(ST))]
                        if hh == 0 else (), inc=SS)
                m_bsc4[g] = P.at(SS)
                mw = [(SS, m_expmt4[g])]
                if g >= 1:
                    mw.append((ST, m_Ylast[g - 1]))
                P.emit("v", lambda e: e.tensor_tensor(
                    mt_sb[:], mcbt_rep, expmt4[:], ALU.mult), waits=mw, inc=SV)
                P.emit("v", lambda e, g=g: e.tensor_tensor(
                    mt_sb[:], mt_sb[:], di_sb[:, g * 512:(g + 1) * 512], ALU.add),
                    inc=SV)
                cw = [(SS, m_expr4[g])]
                if g >= 1:
                    cw.append((ST, m_Ylast[g - 1]))
                P.emit("v", lambda e, c_rep=c_rep: e.tensor_tensor(
                    cst[:], c_rep, expr4[:], ALU.mult), waits=cw, inc=SV)
                m_mtc[g] = P.at(SV)
                if dbg and o == n_outer - 1 and cc == 0 and g == 1:
                    dbg_gate[0] = [
                        P.dma("y", "o", lambda e, nm=nm, srcb=srcb: e.dma_start(
                            out=dbg_h[nm][:, :], in_=srcb[:]),
                            waits=[(SV, m_mtc[g]), (SS, m_expr4[g])])
                        for nm, srcb in [("d_mt", mt_sb), ("d_cst", cst),
                                         ("d_em", expmt4), ("d_er", expr4)]]

            emit_bank(0)
            emit_prep_group(0)
            for g in range(NG):
                ybank = pb[4 + (g % 2)]
                sbank = pb[6 + (g % 2)]
                if g + 1 < NG:
                    emit_bank(g + 1)
                for hh in range(4):
                    h = 4 * g + hh
                    P.emit("t", lambda e, h=h, hh=hh, ybank=ybank: e.matmul(
                        ybank[0:128, hh * CH:(hh + 1) * CH],
                        xtok[:, h * CH:(h + 1) * CH],
                        mt_sb[:, hh * CH:(hh + 1) * CH], start=True, stop=False),
                        waits=[(SV, m_mtc[g])] if hh == 0 else ())
                    m_Y[h] = P.emit("t", lambda e, h=h, hh=hh, ybank=ybank: e.matmul(
                        ybank[0:128, hh * CH:(hh + 1) * CH],
                        carry[:, h * HD:(h + 1) * HD],
                        cst[:, hh * CH:(hh + 1) * CH], start=False, stop=True),
                        inc=ST)
                m_Ylast[g] = m_Y[4 * g + 3]
                for hh in range(4):
                    h = 4 * g + hh
                    P.emit("t", lambda e, h=h, hh=hh, sbank=sbank: e.matmul(
                        sbank[0:128, hh * CH:(hh + 1) * CH],
                        bsc[:, hh * CH:(hh + 1) * CH],
                        xtok[:, h * CH:(h + 1) * CH], start=(hh == 0),
                        stop=(hh == 3)),
                        waits=[(SS, m_bsc4[g]), (SV, P.at(SV))] if hh == 0 else (),
                        inc=(ST if hh == 3 else None))
                m_stlast[g] = P.at(ST)
                if g >= 1:
                    for hh in range(4):
                        hq = 4 * (g - 1) + hh
                        P.emit("t", lambda e, g=g, hh=hh: e.matmul(
                            pb[0][0:1, 384:384 + CH], ones_col[:],
                            ysq[:, hh * CH:(hh + 1) * CH],
                            start=(g == 1 and hh == 0), stop=False),
                            waits=[(SV, m_ysq4[g - 1])] if hh == 0 else (),
                            inc=(ST if hh == 3 else None))
                    m_ssqg[g - 1] = P.at(ST)
                if g + 1 < NG:
                    emit_prep_group(g + 1)
                for hh in range(4):
                    h = 4 * g + hh
                    P.emit("s", lambda e, h=h: e.mul(
                        carry[:, h * HD:(h + 1) * HD],
                        carry[:, h * HD:(h + 1) * HD], cdb128[:, h:h + 1]),
                        waits=[(ST, m_Y[h]), (SS, m_cdb)] if hh == 0 else
                              [(ST, m_Y[h])], inc=SS)
                m_cmlast[g] = P.at(SS)
                sz_ap = bass.AP(actB, 4 * g * T + col, [[8704, 128], [T, 4], [1, 128]])
                m_y14[g] = P.emit("v", lambda e, g=g, ybank=ybank, sz_ap=sz_ap:
                                  e.tensor_tensor(
                    y1_sb[:], ybank[0:128, 0:512], sz_ap, ALU.mult),
                    waits=[(ST, m_Ylast[g]),
                           (SS, m_ynw4[g - 1] if g >= 1 else 0)], inc=SV)
                m_ysq4[g] = P.emit("v", lambda e: e.tensor_tensor(
                    ysq[:], y1_sb[:], y1_sb[:], ALU.mult),
                    waits=[(ST, m_ssqg[g - 1] if g >= 1 else m_prev_ssqfin)],
                    inc=SV)
                P.emit("v", lambda e, g=g, sbank=sbank: e.tensor_tensor(
                    carry[:, g * 512:(g + 1) * 512],
                    carry[:, g * 512:(g + 1) * 512],
                    sbank[0:128, 0:512], ALU.add),
                    waits=[(ST, m_stlast[g]), (SS, m_cmlast[g])], inc=SV)
                for hh in range(4):
                    h = 4 * g + hh
                    P.emit("s", lambda e, h=h, hh=hh, col=col: e.mul(
                        ynw(h)[:, col:col + CH], y1_sb[:, hh * CH:(hh + 1) * CH],
                        nw_sb[:, h:h + 1]),
                        waits=[(SV, m_y14[g])] if hh == 0 else (), inc=SS)
                m_ynw4[g] = P.at(SS)
            for hh in range(4):
                P.emit("t", lambda e, hh=hh: e.matmul(
                    pb[0][0:1, 384:384 + CH], ones_col[:],
                    ysq[:, hh * CH:(hh + 1) * CH],
                    start=False, stop=(hh == 3)),
                    waits=[(SV, m_ysq4[NG - 1])] if hh == 0 else (),
                    inc=(ST if hh == 3 else None))
            m_ssqfin = P.at(ST)
            m_prev_ssqfin = m_ssqfin
            m_ssq_ev = P.emit("v", lambda e, col=col: e.tensor_copy(
                ssqrow[0:1, col:col + CH], pb[0][0:1, 384:384 + CH]),
                waits=[(ST, m_ssqfin)], inc=SV)
        m_ssm_v = P.at(SV)
        m_ssm_s = P.at(SS)

        if dbg and o == n_outer - 1:
            P.dma("s", "o", lambda e: e.dma_start(
                out=dbg_h["d_carry"][:, :], in_=carry[:]),
                waits=[(SV, m_ssm_v)])

        P.emit("v", lambda e: e.tensor_scalar(rstdrow[0:1, :], ssqrow[0:1, :],
                                              1.0 / DI, EPS, ALU.mult, ALU.add),
               waits=[(SV, m_ssm_v)], inc=SV)
        P.emit("s", lambda e: e.activation(rstdrow[0:1, :], rstdrow[0:1, :], AF.Sqrt),
               waits=[(SV, P.at(SV))], inc=SS)
        P.emit("v", lambda e: e.reciprocal(rstdrow[0:1, :], rstdrow[0:1, :]),
               waits=[(SS, P.at(SS))], inc=SV)
        m_rstdr = P.at(SV)

        m_ssm_t = P.at(ST)
        m_wo = [None] * 16

        def emit_wo_dma(k, waits):
            w2 = list(waits)
            if o == 1:
                w2 += pst_snap[18 + k]
            m_wo[k] = [P.dma(
                "y", "w", lambda e, k=k, lo=lo, srcw=srcw: e.dma_start(
                    out=wqW(k % 8, lo),
                    in_=srcw.rearrange("(a p) n -> a p n", p=128)[k, :, :]),
                waits=w2)
                for lo, srcw in ((0, wout16h_h), (1, wout16l_h))]

        m_ysp = [0] * 16

        def emit_ynw_split(k, extra=()):
            r = (k % 2) * T
            P.emit("v", lambda e, k=k, r=r: e.tensor_copy(
                hhb[:, r:r + T], ynw(k)), waits=extra, inc=SV)
            m_ysp[k] = P.emit("v", lambda e, k=k, r=r: e.tensor_tensor(
                hlb[:, r:r + T], ynw(k), hhb[:, r:r + T], ALU.subtract),
                inc=SV)

        if o == 0:
            emit_prep_chunk(18, ldwaits=[(ST, m_ssm_t)])
            emit_prep_chunk(19, ldwaits=[(ST, m_ssm_t)])
        else:
            for k in range(8):
                emit_wo_dma(k, [(ST, m_inproj_t)] if k < 2 else ())
        emit_ynw_split(0, extra=[(SS, m_ssm_s), (ST, m_ssm_t)])
        emit_ynw_split(1)
        for k in range(16):
            if o == 0:
                waits = [(SS, m_pcp[18 + k]), (SV, m_psb[18 + k])]
                s3w = (18 + k) % 3
                wqh = wbuf16[:, s3w * 4096:s3w * 4096 + 1024]
                wql = wbuf16[:, s3w * 4096 + 2048:s3w * 4096 + 3072]
            else:
                waits = list(m_wo[k])
                wqh = wqW(k % 8, 0)
                wql = wqW(k % 8, 1)
            waits.append((SV, m_ysp[k]))
            if k == 0:
                waits += [(SV, m_ssm_v), (SS, m_ssm_s)]
            for tsub in range(2):
                bks = (pb[0], pb[1]) if tsub == 0 else (pb[2], pb[3])
                r = (k % 2) * T + tsub * 128
                yh = hhb[:, r:r + 128]
                yl = hlb[:, r:r + 128]
                seq = [(yh, wqh, 0), (yh, wqh, 1), (yh, wql, 0),
                       (yh, wql, 1), (yl, wqh, 0), (yl, wqh, 1)]
                for si, (ys, ws, nn_) in enumerate(seq):
                    P.emit("t", lambda e, k=k, si=si, ys=ys, ws=ws, nn_=nn_,
                           bks=bks: e.matmul(
                        bks[nn_][0:128, 0:512], ys,
                        ws[:, nn_ * 512:(nn_ + 1) * 512],
                        start=(k == 0 and si < 2),
                        stop=(k == 15 and si >= 4)),
                        waits=waits if (tsub == 0 and si == 0) else (),
                        inc=(ST if (tsub == 1 and si == 5) else None))
                    waits = []
            if o == 0:
                m_pmm[18 + k] = P.at(ST)
                if 20 + k < NPCH:
                    emit_prep_chunk(20 + k)
            elif k + 8 < 16:
                emit_wo_dma(k + 8, [(ST, P.at(ST))])
            if k + 2 < 16:
                emit_ynw_split(k + 2, extra=[(ST, P.at(ST))])
            if k == 3:
                for tsub in range(2):
                    P.emit("t", lambda e, tsub=tsub: e.transpose(
                        pb[4][0:128, tsub:tsub + 1],
                        rstdrow[0:1, tsub * 128:(tsub + 1) * 128], ident[0:1, 0:1]),
                        waits=[(SV, m_rstdr)] if tsub == 0 else (), inc=ST)
                m_rstdt = P.at(ST)
                P.emit("v", lambda e: e.tensor_copy(
                    rstd_col[:, 0:2], pb[4][0:128, 0:2]),
                    waits=[(ST, m_rstdt)], inc=SV)
                m_rstdc = P.at(SV)
        m_att_mm = P.at(ST)
        m_prev_attmm = m_att_mm
        for tsub in range(2):
            bks = (pb[0], pb[1]) if tsub == 0 else (pb[2], pb[3])
            wev = [(ST, m_att_mm)] if tsub == 0 else []
            if o == 0 and tsub == 0:
                wev += [(SS, m_pcp[32]), (SV, m_psb[32])]
            for nn_ in range(2):
                P.emit("v", lambda e, tsub=tsub, bks=bks, nn_=nn_, wev=wev:
                       e.scalar_tensor_tensor(
                    att_sb[:, tsub * DM + nn_ * 512:tsub * DM + nn_ * 512 + 512],
                    bks[nn_][0:128, 0:512], rstd_col[:, tsub:tsub + 1],
                    u_sb[:, tsub * DM + nn_ * 512:tsub * DM + nn_ * 512 + 512],
                    ALU.mult, ALU.add),
                    waits=wev if nn_ == 0 else (), inc=SV)
                wev = []
        m_att_v = P.at(SV)
        m_u_done = P.at(SV)
        if o + 1 < n_outer:
            m_u[o + 1] = [[P.dma("y", "w", lambda e, o=o, tsub=tsub: e.dma_start(
                out=u_sb[:, tsub * DM:(tsub + 1) * DM],
                in_=u_h.rearrange("(a p) n -> p a n", p=128)
                [:, 2 * (o + 1) + tsub, :]),
                waits=[(SV, m_u_done)])] for tsub in range(2)]
            emit_LN_half(o + 1, 0)
            ln_half_pending = True
        else:
            ln_half_pending = False
        ln_xntok_next = None

        m_dbgst = []
        if dbg and o == n_outer - 1:
            for nm, src in [("d_xnf", xn_f[:]), ("d_sz", actB[:, 0:4096]),
                            ("d_xbcc", actB[:, 4096:8704]), ("d_dts", dts[:]),
                            ("d_ynw", arenaA[:, 0:4096]), ("d_att", att_sb[:])]:
                P.dma("s", "o", lambda e, nm=nm, src=src: e.dma_start(
                    out=dbg_h[nm][:, :], in_=src),
                    waits=[(SV, m_att_v)])
            m_dbgst = P.snap("o")

        gu0_extra = (m_prep + P.snap("p") + [(ST, P.at(ST))]) if o == 0 else []
        if o == 0:
            emit_gu_dma(0, 0, gu0_extra)
            emit_gu_dma(0, 1, gu0_extra)
        emit_gu_dma(1, 0, [(ST, P.at(ST))] + gu0_extra)
        emit_gu_dma(1, 1, [(ST, P.at(ST))] + gu0_extra)

        m_silu = [0] * 24
        m_hev = [0] * 24
        m_lntp_cur = 0

        def sgslot(j):
            return (sg[:, (j % 4) * T:(j % 4) * T + T] if j % 4 < 2
                    else ffacc[:, (j % 4 - 2) * T:(j % 4 - 1) * T])

        for g in range(6):
            for jj in range(4):
                j = g * 4 + jj
                for gu in range(2):
                    buf = GUSLOTS[(2 * g + gu) % 6]
                    bank = pb[j % 4] if gu == 0 else pb[4 + (j % 4)]
                    for k in range(8):
                        waits = []
                        if k == 0:
                            waits.extend(m_gu[g][gu])
                            if gu == 0:
                                waits.append((SS, m_silu[j - 4]) if j >= 4
                                             else (SV, m_att_v))
                            else:
                                waits.append((SV, m_hev[j - 4]) if j >= 4
                                             else (SV, m_att_v))
                            if g == 3 and gu == 0 and m_lntp_cur > 0:
                                waits.append((SV, m_lntp_cur))
                        for xsrc in (xnh, xnl):
                            P.emit("t", lambda e, buf=buf, k=k, jj=jj, bank=bank,
                                   xsrc=xsrc: e.matmul(
                                bank[0:128, 0:T],
                                buf[:, k * 512 + jj * 128:k * 512 + (jj + 1) * 128],
                                xsrc[:, k * T:(k + 1) * T],
                                start=(k == 0 and xsrc is xnh),
                                stop=(k == 7 and xsrc is xnl)),
                                waits=waits if xsrc is xnh else (),
                                inc=(ST if (k == 7 and xsrc is xnl) else None))
                            waits = []
                    m_mm = P.at(ST)
                    if gu == 0:
                        w2 = [(ST, m_mm)]
                        if j >= 4:
                            w2.append((SV, m_hev[j - 4]))
                        m_silu[j] = P.emit("s", lambda e, j=j, bank=bank: e.activation(
                            sgslot(j), bank[0:128, 0:T],
                            AF.Silu, scale=QS), waits=w2, inc=SS)
                    else:
                        m_hev[j] = P.emit("v", lambda e, j=j, bank=bank: e.tensor_tensor(
                            hffn(j), bank[0:128, 0:T],
                            sgslot(j), ALU.mult),
                            waits=[(ST, m_mm), (SS, m_silu[j])]
                            + (m_dbgst if j == 0 else []), inc=SV)
            for pn in (2 * g + 4, 2 * g + 5):
                if pn < 12:
                    emit_gu_dma(pn // 2, pn % 2, [(ST, P.at(ST))])
            if g == 1 and ln_half_pending:
                ln_xntok_next = emit_LN_half(o + 1, 1)
            if g == 2 and ln_xntok_next is not None:
                ln_state[o + 1] = emit_LN_tp(ln_xntok_next)
                m_lntp_cur = ln_state[o + 1]
            if g == 4:
                m_guDE = P.at(ST)
        m_h_v = P.at(SV)
        m_prev_hv = m_h_v
        m_prev_gu_ss = P.at(SS)
        m_prev_guDE = m_guDE
        m_prev_gu_end = P.at(ST)
        if o + 1 < n_outer and (o + 1) in ln_state:
            xnsplit_state[o + 1] = emit_xnsplit(ln_state[o + 1],
                                                m_prev_gu_end)

        m_wd = [None] * 24

        def emit_wd_dma(k, waits):
            m_wd[k] = [P.dma("g", "g", lambda e, k=k: e.dma_start(
                out=wbuf16[:, 8192 + (k % 4) * 1024:8192 + (k % 4) * 1024 + 1024],
                in_=wd16_h.rearrange("(a p) n -> a p n", p=128)[k, :, :]),
                waits=waits)]

        for k in range(4):
            emit_wd_dma(k, ([(ST, m_gu_start)]
                            + ((m_prep + P.snap("p")) if o == 0 else []))
                        if k == 0 else ())
        m_hsp = [0] * 24
        m_dnk = [0] * 24
        for k in range(24):
            hc = (k % 2) * T
            wsp = [(SV, m_hev[k])] if False else []
            if k >= 2:
                wsp.append((ST, m_dnk[k - 2]))
            P.emit("v", lambda e, k=k, hc=hc: e.tensor_copy(
                hhb[:, hc:hc + T], hffn(k)), waits=wsp, inc=SV)
            m_hsp[k] = P.emit("v", lambda e, k=k, hc=hc: e.tensor_tensor(
                hlb[:, hc:hc + T], hffn(k), hhb[:, hc:hc + T], ALU.subtract),
                inc=SV)
            waits = list(m_wd[k]) + [(SV, m_hsp[k])]
            if k == 0:
                waits += [(SV, m_h_v), (SV, m_att_v)]
            for tsub in range(2):
                bks = (pb[0], pb[1]) if tsub == 0 else (pb[2], pb[3])
                for nn_ in range(2):
                    for hsrc in (hhb, hlb):
                        P.emit("t", lambda e, k=k, hc=hc, tsub=tsub, nn_=nn_,
                               bks=bks, hsrc=hsrc: e.matmul(
                            bks[nn_][0:128, 0:512],
                            hsrc[:, hc + tsub * 128:hc + (tsub + 1) * 128],
                            wbuf16[:, 8192 + (k % 4) * 1024 + nn_ * 512:
                                   8192 + (k % 4) * 1024 + (nn_ + 1) * 512],
                            start=(k == 0 and hsrc is hhb),
                            stop=(k == 23 and hsrc is hlb)),
                            waits=waits if (tsub == 0 and nn_ == 0 and hsrc is hhb)
                            else (),
                            inc=(ST if (tsub == 1 and nn_ == 1 and hsrc is hlb)
                                 else None))
            m_dnk[k] = P.at(ST)
            if k + 4 < 24:
                emit_wd_dma(k + 4, [(ST, P.at(ST))])
        m_dn = P.at(ST)
        for tsub in range(2):
            bks = (pb[0], pb[1]) if tsub == 0 else (pb[2], pb[3])
            w1 = [(ST, m_dn)]
            if tsub == 1:
                w1.append((SV, P.at(SV)))
            P.emit("s", lambda e, bks=bks: e.mul(
                ffacc[:, 0:512], bks[0][0:128, 0:512], QS * QS), waits=w1, inc=SS)
            P.emit("s", lambda e, bks=bks: e.mul(
                ffacc[:, 512:1024], bks[1][0:128, 0:512], QS * QS), inc=SS)
            m_ff = P.at(SS)
            wout_w = [(SS, m_ff)] + (m_prev_store if tsub == 0 else [])
            if o == 0 and tsub == 0:
                wout_w += [(SS, m_pcp[33]), (SV, m_psb[33])]
            P.emit("v", lambda e, tsub=tsub, wout_w=wout_w: e.tensor_tensor(
                out_sb[:, tsub * DM:(tsub + 1) * DM], ffacc[:],
                att_sb[:, tsub * DM:(tsub + 1) * DM], ALU.add),
                waits=wout_w, inc=SV)
            m_out_v = P.at(SV)
            P.dma("s", "o", lambda e, o=o, tsub=tsub: e.dma_start(
                out=out_h.rearrange("(a p) n -> p a n", p=128)[:, 2 * o + tsub, :],
                in_=out_sb[:, tsub * DM:(tsub + 1) * DM]),
                waits=[(SV, m_out_v)])
        m_prev_store = P.snap("o")
        m_prev_tensor_end = P.at(ST)
        m_gu_start = m_prev_tensor_end
        m_prev_ffev = P.at(SS)

    final_waits = [(k, v) for k, v in P.cnt.items()
                   if v > 0 and k not in (ST, SV, SS)]

    import contextlib
    with contextlib.ExitStack() as stack:
        sems = {k: stack.enter_context(nc.semaphore(f"sem_{k}"))
                for k in P.cnt}
        block = stack.enter_context(nc.Block())

        own_sem = {"v": SV, "s": SS, "g": SG}

        def runner(eng_key):
            def _run(eng):
                own = own_sem.get(eng_key)
                prev = 0
                for waits, fn, inc, inc_by in P.q[eng_key]:
                    if own is not None and prev > 0:
                        eng.wait_ge(sems[own], prev)
                    for sem_key, val in waits:
                        if val and val > 0:
                            eng.wait_ge(sems[sem_key], val)
                    ins = fn(eng)
                    if inc is not None and ins is not None:
                        ins.then_inc(sems[inc], inc_by)
                    if inc is not None and inc == own:
                        prev += inc_by
                if eng_key == "y":
                    for sem_key, val in final_waits:
                        eng.wait_ge(sems[sem_key], val)
            return _run

        block.tensor(runner("t"))
        block.vector(runner("v"))
        block.scalar(runner("s"))
        block.sync(runner("y"))
        block.gpsimd(runner("g"))

    for g in reversed(ctx):
        g.__exit__(None, None, None)
    return nc


def make_in_maps(inputs, n_cores=8):
    c = np.ascontiguousarray
    base = {
        "winT": c(np.asarray(inputs["L0_attn_W_in"], dtype=np.float32).T),
        "convw": c(np.asarray(inputs["L0_attn_conv_w"], dtype=np.float32)
                   .reshape(18, 128, 4).transpose(1, 0, 2).reshape(128, 72)),
        "convb": c(np.asarray(inputs["L0_attn_conv_b"], dtype=np.float32)
                   .reshape(18, 128).T),
        "dtb": c(np.asarray(inputs["L0_attn_dt_bias"], dtype=np.float32)
                 .reshape(NH, 1)),
        "alog": c(np.asarray(inputs["L0_attn_A_log"], dtype=np.float32)
                  .reshape(NH, 1)),
        "dvec": c(np.asarray(inputs["L0_attn_D"], dtype=np.float32).reshape(1, NH)),
        "nw": c(np.asarray(inputs["L0_attn_norm_weight"], dtype=np.float32)
                .reshape(NH, 128).T),
        "woutT": c(np.asarray(inputs["L0_attn_W_out"], dtype=np.float32).T),
        "wgT": c(np.asarray(inputs["L0_ffn_w_gate"], dtype=np.int32).T),
        "wuT": c(np.asarray(inputs["L0_ffn_w_up"], dtype=np.int32).T),
        "wdT": c(np.asarray(inputs["L0_ffn_w_down"], dtype=np.int32).T),
    }
    u = np.asarray(inputs["L0_attn_u"], dtype=np.float32)
    maps = []
    for i in range(n_cores):
        m = dict(base)
        m["u"] = c(u[i])
        maps.append(m)
    return maps


def kernel(**inputs):
    nc = build_nc(n_outer=8)
    in_maps = make_in_maps(inputs, n_cores=8)
    res = run_bass_kernel_spmd(nc, in_maps, core_ids=list(range(8)))
    return np.stack([np.asarray(r["out"], dtype=np.float32) for r in res.results],
                    axis=0)


# version 32
# speedup vs baseline: 1.0293x; 1.0007x over previous
import numpy as np
import concourse.bass as bass
import concourse.mybir as mybir
from concourse.bass_utils import run_bass_kernel_spmd

F32 = mybir.dt.float32
BF16 = mybir.dt.bfloat16
I32 = mybir.dt.int32
AF = mybir.ActivationFunctionType
ALU = mybir.AluOpType
AX = mybir.AxisListType

DM = 1024
DI = 2048
NH = 16
HD = 128
CONVD = 2304
JPROJ = 4368
HID = 3072
CH = 128
T = 256
EPS = 1e-5
QS = 1.0 / 127.0
NEG = -1e9

SW, GW, ST, SV, SS, SO = "sw", "gw", "st", "sv", "ss", "so"
SG = "sg"


class Prog:
    ENGS = ("t", "v", "s", "y", "g")
    POOLS = {"w": ["w0", "w1", "w2", "w3"],
             "g": ["g0", "g1", "g2", "g3", "g4", "g5", "g6", "g7"],
             "o": ["o0", "o1"], "p": ["p0", "p1", "p2", "p3"]}

    def __init__(self):
        self.q = {e: [] for e in self.ENGS}
        keys = [ST, SV, SS, SG] + [s for p in self.POOLS.values() for s in p]
        self.cnt = {k: 0 for k in keys}
        self.rr = {p: 0 for p in self.POOLS}

    def emit(self, eng, fn, waits=(), inc=None, inc_by=1):
        self.q[eng].append((tuple(waits), fn, inc, inc_by))
        if inc is not None:
            self.cnt[inc] += inc_by
            return self.cnt[inc]
        return None

    def at(self, sem):
        return self.cnt[sem]

    def dma(self, eng, pool, fn, waits=()):
        sems = self.POOLS[pool]
        s = sems[self.rr[pool] % len(sems)]
        self.rr[pool] += 1
        w = list(waits)
        if self.cnt[s] > 0:
            w.append((s, self.cnt[s]))
        v = self.emit(eng, fn, waits=w, inc=s, inc_by=16)
        return (s, v)

    def snap(self, pool):
        return [(s, self.cnt[s]) for s in self.POOLS[pool] if self.cnt[s] > 0]


def build_nc(n_outer=8, dbg=False):
    L = n_outer * T
    nc = bass.Bass("TRN2", target_bir_lowering=False)

    u_h = nc.dram_tensor("u", [L, DM], F32, kind="ExternalInput").ap()
    winT = nc.dram_tensor("winT", [DM, JPROJ], F32, kind="ExternalInput").ap()
    convw_h = nc.dram_tensor("convw", [128, 18 * 4], F32, kind="ExternalInput").ap()
    convb_h = nc.dram_tensor("convb", [128, 18], F32, kind="ExternalInput").ap()
    dtb_h = nc.dram_tensor("dtb", [NH, 1], F32, kind="ExternalInput").ap()
    alog_h = nc.dram_tensor("alog", [NH, 1], F32, kind="ExternalInput").ap()
    dvec_h = nc.dram_tensor("dvec", [1, NH], F32, kind="ExternalInput").ap()
    nw_h = nc.dram_tensor("nw", [128, NH], F32, kind="ExternalInput").ap()
    woutT = nc.dram_tensor("woutT", [DI, DM], F32, kind="ExternalInput").ap()
    wgT = nc.dram_tensor("wgT", [DM, HID], I32, kind="ExternalInput").ap()
    wuT = nc.dram_tensor("wuT", [DM, HID], I32, kind="ExternalInput").ap()
    wdT = nc.dram_tensor("wdT", [HID, DM], I32, kind="ExternalInput").ap()
    out_h = nc.dram_tensor("out", [L, DM], F32, kind="ExternalOutput").ap()
    wg16_h = nc.dram_tensor("wg16", [DM, HID], BF16, kind="Internal").ap()
    wu16_h = nc.dram_tensor("wu16", [DM, HID], BF16, kind="Internal").ap()
    wd16_h = nc.dram_tensor("wd16", [HID, DM], BF16, kind="Internal").ap()
    win16h_h = nc.dram_tensor("win16h", [DM, JPROJ], BF16, kind="Internal").ap()
    win16l_h = nc.dram_tensor("win16l", [DM, JPROJ], BF16, kind="Internal").ap()
    acsf_d = nc.dram_tensor("acsf_d", [1, 2048], F32, kind="Internal").ap()
    nacsf_d = nc.dram_tensor("nacsf_d", [1, 2048], F32, kind="Internal").ap()
    wout16h_h = nc.dram_tensor("wout16h", [DI, DM], BF16, kind="Internal").ap()
    wout16l_h = nc.dram_tensor("wout16l", [DI, DM], BF16, kind="Internal").ap()

    dbg_h = {}
    if dbg:
        for nm, shape in [("d_xnf", [128, 8 * T]), ("d_sz", [128, 16 * T]),
                          ("d_xbcc", [128, 18 * T]), ("d_dts", [16, 4 * T]),
                          ("d_ynw", [128, 16 * T]), ("d_att", [128, 2 * DM]),
                          ("d_carry", [128, NH * HD]),
                          ("d_mt", [128, 512]), ("d_cst", [128, 512]),
                          ("d_em", [128, 512]), ("d_er", [128, 512])]:
            dbg_h[nm] = nc.dram_tensor(nm, shape, F32, kind="ExternalOutput").ap()

    ident_h = nc.inline_tensor(np.eye(128, dtype=np.float32), name="identc").ap()
    onesc_h = nc.inline_tensor(np.ones((128, 1), dtype=np.float32), name="onescc").ap()
    onesr_h = nc.inline_tensor(np.ones((1, 128), dtype=np.float32), name="onesrc").ap()
    mt_np = np.where(np.arange(128)[None, :] >= np.arange(128)[:, None],
                     1.0, 0.0).astype(np.float32)
    masktri_h = nc.inline_tensor(mt_np, name="masktric").ap()
    tr_np = (np.arange(128)[:, None] <= np.arange(128)[None, :]).astype(np.float32)
    trilones_h = nc.inline_tensor(tr_np, name="trilonesc").ap()

    P = Prog()
    ctx = []

    def sb(name, p, cols, dt=F32):
        g = nc.sbuf_tensor(name, [p, cols], dt)
        ctx.append(g)
        return g.__enter__()

    ident = sb("ident", 128, 128)
    ones_col = sb("ones_col", 128, 1)
    ones_row = sb("ones_row", 1, 128)
    masktri = sb("masktri", 128, 128)
    trilones = sb("trilones", 128, 128)
    convw_sb = sb("convw_sb", 128, 18 * 4)
    convb_sb = sb("convb_sb", 128, 18)
    dtb_sb = sb("dtb_sb", NH, 1)
    c_one = sb("c_one", NH, 1)
    c_tiny = sb("c_tiny", NH, 1)
    alog_sb = sb("alog_sb", NH, 1)
    a_tmp = sb("a_tmp", NH, 1)
    a_col = sb("a_col", NH, 1)
    d_row = sb("d_row", 1, NH)
    db128 = sb("db128", 128, NH)
    nw_sb = sb("nw_sb", 128, NH)
    di_sb = sb("di_sb", 128, NH * HD)
    carry = sb("carry", 128, NH * HD)
    tail_sb = sb("tail_sb", 128, 18 * 3)

    u_sb = sb("u_sb", 128, 2 * DM)
    ln_s1 = sb("ln_s1", 128, 8)
    xn_f = sb("xn_f", 128, 8 * T)
    arenaA = sb("arenaA", 128, 18 * (T + 3))
    actB = sb("actB", 128, 8704)
    dts = sb("dts", NH, 4 * T)
    att_sb = sb("att_sb", 128, 2 * DM)
    out_sb = sb("out_sb", 128, 2 * DM)

    dAT = sb("dAT", 128, NH)
    acs_h = sb("acs_h", NH, CH)
    nacs_h = sb("nacs_h", NH, CH)
    dch = sb("dch", NH, CH)
    sc_h = sb("sc_h", NH, CH)
    scT = sb("scT", 128, NH)
    cdb128 = sb("cdb128", 128, NH)
    btok = sb("btok", 128, CH)
    bsc = sb("bsc", 128, 4 * CH)
    xtok = sb("xtok", 128, NH * CH)
    expmt4 = sb("expmt4", 128, 512)
    expr4 = sb("expr4", 128, 512)
    mt_sb = sb("mt_sb", 128, 4 * CH)
    cst = sb("cst", 128, 4 * CH)
    y1_sb = sb("y1_sb", 128, 4 * CH)
    ysq = sb("ysq", 128, 4 * CH)
    ssqrow = sb("ssqrow", 1, T)
    rstdrow = sb("rstdrow", 1, T)
    rstd_col = sb("rstd_col", 128, 2)
    ffacc = sb("ffacc", 128, DM)
    sg = sb("sg", 128, 2 * T)
    wbufW = sb("wbufW", 128, 16384, BF16)
    wbuf16 = sb("wbuf16", 128, 12288, BF16)
    xnh = sb("xnh", 128, 8 * T, BF16)
    xnl = sb("xnl", 128, 8 * T, BF16)
    hhb = sb("hhb", 128, 2 * T, BF16)
    hlb = sb("hlb", 128, 2 * T, BF16)

    convtmp = ffacc[:, 0:T]

    pbg = [nc.psum_tensor(f"pb{i}", [128, 512], F32) for i in range(8)]
    ctx.extend(pbg)
    pb = [g.__enter__() for g in pbg]

    def xbcpre(f):
        return arenaA[:, f * (T + 3):(f + 1) * (T + 3)]

    def ynw(h):
        return arenaA[:, h * T:(h + 1) * T]

    def szt(f):
        return actB[:, f * T:(f + 1) * T]

    def xbcc(f):
        return actB[:, 4096 + f * T:4096 + (f + 1) * T]

    def hffn(j):
        return actB[:, j * T:(j + 1) * T]

    def wslabW(i, lo):
        return wbufW[:, i * 8192 + lo * 4096:i * 8192 + lo * 4096 + 4096]

    def wqW(pos, lo):
        return wbufW[:, pos * 2048 + lo * 1024:pos * 2048 + (lo + 1) * 1024]

    def wslab16(i):
        return wbuf16[:, i * 4096:(i + 1) * 4096]

    def wq16(pos):
        return wbuf16[:, pos * 1024:(pos + 1) * 1024]

    def d(fn):
        return P.dma("y", "w", fn)

    d(lambda e: e.dma_start(out=ident[:], in_=ident_h[:, :]))
    d(lambda e: e.dma_start(out=ones_col[:], in_=onesc_h[:, :]))
    d(lambda e: e.dma_start(out=ones_row[:], in_=onesr_h[:, :]))
    d(lambda e: e.dma_start(out=masktri[:], in_=masktri_h[:, :]))
    d(lambda e: e.dma_start(out=trilones[:], in_=trilones_h[:, :]))
    d(lambda e: e.dma_start(out=convw_sb[:], in_=convw_h[:, :]))
    d(lambda e: e.dma_start(out=convb_sb[:], in_=convb_h[:, :]))
    d(lambda e: e.dma_start(out=dtb_sb[:], in_=dtb_h[:, :]))
    d(lambda e: e.dma_start(out=alog_sb[:], in_=alog_h[:, :]))
    d(lambda e: e.dma_start(out=d_row[:], in_=dvec_h[:, :]))
    d(lambda e: e.dma_start(out=nw_sb[:], in_=nw_h[:, :]))
    d(lambda e: e.dma_start(
        out=u_sb[:, 0:2 * DM],
        in_=u_h.rearrange("(a p) n -> p a n", p=128)[:, 0:2, :]))
    m_init = P.snap("w")

    m_prep = [P.dma("g", "g", lambda e, dst=dst, s_=s_: e.dma_start(
        out=dst[:, :], in_=s_[:, :]))
        for dst, s_ in [(wg16_h, wgT), (wu16_h, wuT), (wd16_h, wdT)]]
    P.emit("v", lambda e: e.memset(carry[:], 0.0), inc=SV)
    P.emit("v", lambda e: e.memset(tail_sb[:], 0.0), inc=SV)
    P.emit("v", lambda e: e.memset(c_one[:], 1.0), inc=SV)
    P.emit("v", lambda e: e.memset(c_tiny[:], 1e-30), inc=SV)
    P.emit("s", lambda e: e.activation(a_tmp[:], alog_sb[:], AF.Exp),
           waits=m_init, inc=SS)
    P.emit("v", lambda e: e.tensor_scalar(a_col[:], a_tmp[:], -1.0, None, ALU.mult),
           waits=[(SS, P.at(SS))], inc=SV)
    P.emit("t", lambda e: e.matmul(pb[0][0:128, 0:NH], ones_row[:], d_row[:],
                                   start=True, stop=True),
           waits=m_init, inc=ST)
    P.emit("v", lambda e: e.tensor_copy(db128[:], pb[0][0:128, 0:NH]),
           waits=[(ST, P.at(ST))], inc=SV)
    m_db = P.at(SV)
    for h in range(NH):
        P.emit("v", lambda e, h=h: e.tensor_scalar(
            di_sb[:, h * HD:(h + 1) * HD], ident[:], db128[:, h:h + 1], None, ALU.mult),
            waits=[(SV, m_db)] if h == 0 else (), inc=SV)

    xn_tok = arenaA
    lnsq_a, lnsq_b = expmt4, expr4

    def emit_LN_half(o, tsub):
        if True:
            ut = u_sb[:, tsub * DM:(tsub + 1) * DM]
            c0 = tsub * 4
            P.emit("v", lambda e, ut=ut, c0=c0: e.reduce_sum(
                ln_s1[:, c0:c0 + 1], ut, axis=AX.X),
                waits=[*m_u[o][tsub]], inc=SV)
            P.emit("s", lambda e, ut=ut: e.activation(
                lnsq_a[:], ut[:, 0:512], AF.Square),
                waits=[*m_u[o][tsub], (SV, P.at(SV))], inc=SS)
            P.emit("s", lambda e, ut=ut: e.activation(
                lnsq_b[:], ut[:, 512:1024], AF.Square), inc=SS)
            m_sq = P.at(SS)
            P.emit("v", lambda e, c0=c0: e.reduce_sum(
                ln_s1[:, c0 + 1:c0 + 2], lnsq_a[:], axis=AX.X),
                waits=[(SS, m_sq)], inc=SV)
            P.emit("v", lambda e, c0=c0: e.reduce_sum(
                ln_s1[:, c0 + 3:c0 + 4], lnsq_b[:], axis=AX.X), inc=SV)
            P.emit("v", lambda e, c0=c0: e.tensor_tensor(
                ln_s1[:, c0 + 1:c0 + 2], ln_s1[:, c0 + 1:c0 + 2],
                ln_s1[:, c0 + 3:c0 + 4], ALU.add), inc=SV)
            P.emit("v", lambda e, c0=c0: e.tensor_scalar(
                ln_s1[:, c0:c0 + 1], ln_s1[:, c0:c0 + 1], 1.0 / DM, None, ALU.mult),
                inc=SV)
            P.emit("v", lambda e, c0=c0: e.tensor_scalar(
                ln_s1[:, c0 + 1:c0 + 2], ln_s1[:, c0 + 1:c0 + 2], 1.0 / DM, EPS,
                ALU.mult, ALU.add), inc=SV)
            P.emit("v", lambda e, c0=c0: e.tensor_tensor(
                ln_s1[:, c0 + 2:c0 + 3], ln_s1[:, c0:c0 + 1], ln_s1[:, c0:c0 + 1],
                ALU.mult), inc=SV)
            P.emit("v", lambda e, c0=c0: e.tensor_tensor(
                ln_s1[:, c0 + 1:c0 + 2], ln_s1[:, c0 + 1:c0 + 2],
                ln_s1[:, c0 + 2:c0 + 3], ALU.subtract), inc=SV)
            P.emit("s", lambda e, c0=c0: e.activation(
                ln_s1[:, c0 + 3:c0 + 4], ln_s1[:, c0 + 1:c0 + 2], AF.Sqrt),
                waits=[(SV, P.at(SV))], inc=SS)
            P.emit("v", lambda e, c0=c0: e.reciprocal(
                ln_s1[:, c0 + 2:c0 + 3], ln_s1[:, c0 + 3:c0 + 4]),
                waits=[(SS, P.at(SS))], inc=SV)
            P.emit("v", lambda e, c0=c0: e.tensor_tensor(
                ln_s1[:, c0 + 3:c0 + 4], ln_s1[:, c0:c0 + 1],
                ln_s1[:, c0 + 2:c0 + 3], ALU.mult), inc=SV)
            P.emit("v", lambda e, ut=ut, tsub=tsub, c0=c0: e.tensor_scalar(
                xn_tok[:, tsub * DM:(tsub + 1) * DM], ut, ln_s1[:, c0 + 2:c0 + 3],
                ln_s1[:, c0 + 3:c0 + 4], ALU.mult, ALU.subtract),
                waits=[(ST, P.at(ST))] if tsub == 0 else (), inc=SV)
        return P.at(SV)

    def emit_LN_tp(m_xntok):
        for k in range(8):
            bk = pb[2 + (k % 2)]
            for tsub in range(2):
                P.emit("t", lambda e, k=k, tsub=tsub, bk=bk: e.transpose(
                    bk[0:128, tsub * 128:(tsub + 1) * 128],
                    xn_tok[:, tsub * DM + k * 128: tsub * DM + (k + 1) * 128],
                    ident[:]),
                    waits=[(SV, m_xntok), (SS, P.at(SS))] if (tsub == 0 and k < 2)
                    else ([(SV, P.at(SV))] if tsub == 0 else ()), inc=ST)
            m_tp = P.at(ST)
            P.emit("v", lambda e, k=k, bk=bk: e.tensor_copy(
                xn_f[:, k * T:(k + 1) * T], bk[0:128, 0:256]),
                waits=[(ST, m_tp)], inc=SV)
        return P.at(SV)

    NWG = 9
    PSCHED = [0, 0, 1, 1, 1, 1, 1, 1, 1, 1, 1, 1, 1, 1, 1, 1]
    m_prev_store = []
    m_prev_tensor_end = 0
    m_prev_gu_end = 0
    m_prev_ffev = 0
    m_prev_attmm = 0
    m_prev_guDE = 0
    m_prev_hv = 0
    m_prev_gu_ss = 0
    m_gu_start = 0

    m_u = [[m_init, m_init]] + [None] * n_outer

    GSEQ = [4, 5, 6, 7, 8, 0, 1, 2, 3]
    PCH = []
    for g_ in GSEQ:
        gw_ = 512 if g_ < 8 else JPROJ - 8 * 512
        PCH.append(("wi", g_, 0, 256))
        PCH.append(("wi", g_, 256, gw_ - 256))
    for k_ in range(16):
        PCH.append(("wo", k_, 0, 1024))
    NPCH = len(PCH)
    m_pcp = [0] * NPCH
    m_psb = [0] * NPCH
    m_pmm = [0] * NPCH
    pst_snap = [[] for _ in range(NPCH)]

    def pstag(c, coloff, w, lo):
        s3 = c % 3
        return wbuf16[:, s3 * 4096 + lo * 2048 + coloff:
                      s3 * 4096 + lo * 2048 + coloff + w]

    def emit_prep_chunk(i, ldwaits=()):
        kind, idx, j0, jw = PCH[i]
        ncols = 8 * jw if kind == "wi" else jw
        sf = att_sb if i % 2 == 0 else out_sb
        fv = sf[:, 0:ncols]
        hv = pstag(i, 0, ncols, 0)
        lv = pstag(i, 0, ncols, 1)
        if kind == "wi":
            j0a = idx * 512 + j0
            src = winT.rearrange("(k p) j -> p k j", p=128)[:, :, j0a:j0a + jw]
            dsth = win16h_h.rearrange("(k p) j -> p k j", p=128)[:, :, j0a:j0a + jw]
            dstl = win16l_h.rearrange("(k p) j -> p k j", p=128)[:, :, j0a:j0a + jw]
            fvr = fv.rearrange("p (k j) -> p k j", k=8)
            hvr = hv.rearrange("p (k j) -> p k j", k=8)
            lvr = lv.rearrange("p (k j) -> p k j", k=8)
        else:
            src = woutT.rearrange("(a p) n -> a p n", p=128)[idx]
            dsth = wout16h_h.rearrange("(a p) n -> a p n", p=128)[idx]
            dstl = wout16l_h.rearrange("(a p) n -> a p n", p=128)[idx]
            fvr, hvr, lvr = fv, hv, lv
        wld = list(ldwaits)
        if i >= 2:
            wld += [(SS, m_pcp[i - 2]), (SV, m_psb[i - 2])]
        ld = P.dma("y", "w", lambda e: e.dma_start(out=fvr, in_=src), waits=wld)
        wcp = [ld]
        if i >= 3:
            if m_pmm[i - 3] > 0:
                wcp.append((ST, m_pmm[i - 3]))
            wcp += pst_snap[i - 3]
        m_pcp[i] = P.emit("s", lambda e: e.copy(hv, fv), waits=wcp, inc=SS)
        m_psb[i] = P.emit("v", lambda e: e.tensor_tensor(
            lv, fv, hv, ALU.subtract), waits=[(SS, m_pcp[i])], inc=SV)
        s1 = P.dma("g", "p", lambda e: e.dma_start(out=dsth, in_=hvr),
                   waits=[(SS, m_pcp[i])])
        s2 = P.dma("g", "p", lambda e: e.dma_start(out=dstl, in_=lvr),
                   waits=[(SV, m_psb[i])])
        pst_snap[i] = [s1, s2]

    def emit_xnsplit(m_xnf_v, m_guend):
        for k in range(8):
            P.emit("s", lambda e, k=k: e.copy(
                xnh[:, k * T:(k + 1) * T], xn_f[:, k * T:(k + 1) * T]),
                waits=[(ST, m_guend), (SV, m_xnf_v)] if k == 0 else (),
                inc=SS)
            P.emit("v", lambda e, k=k: e.tensor_tensor(
                xnl[:, k * T:(k + 1) * T], xn_f[:, k * T:(k + 1) * T],
                xnh[:, k * T:(k + 1) * T], ALU.subtract),
                waits=[(SS, P.at(SS))], inc=SV)
        return P.at(SV)

    ln_state = {}
    xnsplit_state = {}
    for o in range(n_outer):
        ub = 0
        if o in xnsplit_state:
            m_xnsplit = xnsplit_state.pop(o)
            ln_state.pop(o, None)
        else:
            if o in ln_state:
                m_xnf = ln_state.pop(o)
            else:
                emit_LN_half(o, 0)
                m_xnf = emit_LN_tp(emit_LN_half(o, 1))
            m_xnsplit = emit_xnsplit(m_xnf, m_prev_gu_end)

        m_gu = [[None, None] for _ in range(6)]
        GUSLOTS = [wbuf16[:, 0:4096], wbuf16[:, 4096:8192],
                   wbufW[:, 0:4096], wbufW[:, 4096:8192],
                   wbufW[:, 8192:12288], wbufW[:, 12288:16384]]

        def emit_gu_dma(g, gu, waits):
            src = wg16_h if gu == 0 else wu16_h
            buf = GUSLOTS[(2 * g + gu) % 6]
            m_gu[g][gu] = [P.dma(
                "g", "g", lambda e, g=g, src=src, buf=buf: e.dma_start(
                    out=buf.rearrange("p (k j) -> p k j", k=8)[:, :, :],
                    in_=src.rearrange("(k p) j -> p k j", p=128)
                    [:, :, g * 512:(g + 1) * 512]),
                waits=waits)]

        if o > 0:
            emit_gu_dma(0, 0, [(ST, m_prev_gu_end)])
            emit_gu_dma(0, 1, [(ST, m_prev_gu_end)])

        m_wg = [None] * NWG
        m_csilu = [0] * 18

        def emit_win_dma(pos, waits):
            g = GSEQ[pos]
            gw = 512 if g < 8 else JPROJ - 8 * 512
            w2 = list(waits)
            if o == 1:
                w2 += pst_snap[2 * pos] + pst_snap[2 * pos + 1]
            dmas = []
            for lo, srcw in ((0, win16h_h), (1, win16l_h)):
                buf = wslabW(pos % 2, lo)
                dmas.append(P.dma(
                    "y", "w", lambda e, g=g, gw=gw, buf=buf, srcw=srcw: e.dma_start(
                        out=buf.rearrange("p (k j) -> p k j", k=8)[:, :, 0:gw],
                        in_=srcw.rearrange("(k p) j -> p k j", p=128)
                        [:, :, g * 512:g * 512 + gw]),
                    waits=w2))
            m_wg[g] = dmas

        def emit_conv(f):
            cacc = sg[:, (f % 2) * T:(f % 2) * T + T]
            P.emit("v", lambda e, f=f: e.tensor_copy(
                xbcpre(f)[:, 0:3], tail_sb[:, f * 3:(f + 1) * 3]), inc=SV)
            P.emit("v", lambda e, f=f, cacc=cacc: e.tensor_scalar(
                cacc, xbcpre(f)[:, 0:T],
                convw_sb[:, f * 4 + 0: f * 4 + 1], None, ALU.mult),
                waits=[(SS, m_csilu[f - 2])] if f >= 2 else (), inc=SV)
            for k in range(1, 4):
                P.emit("v", lambda e, f=f, k=k: e.tensor_scalar(
                    convtmp, xbcpre(f)[:, k:k + T],
                    convw_sb[:, f * 4 + k: f * 4 + k + 1], None, ALU.mult), inc=SV)
                P.emit("v", lambda e, cacc=cacc: e.tensor_tensor(
                    cacc, cacc, convtmp, ALU.add), inc=SV)
            m_cacc = P.at(SV)
            m_csilu[f] = P.emit("s", lambda e, f=f, cacc=cacc: e.activation(
                xbcc(f), cacc, AF.Silu, bias=convb_sb[:, f:f + 1]),
                waits=[(SV, m_cacc)], inc=SS)
            P.emit("v", lambda e, f=f: e.tensor_copy(
                tail_sb[:, f * 3:(f + 1) * 3], xbcpre(f)[:, T:T + 3]), inc=SV)

        ssm_units = []
        ssm_res = {}

        def build_prep_units(cc, svgate, ssgate):
            col = cc * CH
            tb0 = pb[2] if cc == 0 else pb[0]
            tb1 = pb[3] if cc == 0 else pb[1]
            res = {}
            U = []

            def u1():
                P.emit("t", lambda e: e.transpose(
                    tb0[0:128, 0:NH], dts[:, 2 * T + col:2 * T + col + CH],
                    ident[0:NH, 0:NH]),
                    waits=[(SV, svgate), (SS, P.at(SS))], inc=ST)
                res["t1"] = P.at(ST)
            U.append(u1)

            def u2():
                P.emit("v", lambda e: e.tensor_copy(dAT[:], tb0[0:128, 0:NH]),
                       waits=[(ST, res["t1"])], inc=SV)
                m_datv = P.at(SV)
                P.emit("t", lambda e: e.matmul(tb1[0:16, 0:CH], dAT[:, 0:NH],
                                               trilones[:],
                                               start=True, stop=True),
                       waits=[(SV, m_datv)], inc=ST)
                res["acs"] = P.at(ST)
            U.append(u2)

            def u3():
                P.emit("v", lambda e: e.tensor_copy(acs_h[:], tb1[0:16, 0:CH]),
                       waits=[(ST, res["acs"])], inc=SV)
                res["acshv"] = P.at(SV)
                P.emit("s", lambda e: e.activation(
                    dch[:], acs_h[:], AF.Exp, scale=-1.0,
                    bias=acs_h[:, CH - 1:CH]),
                    waits=[(SV, res["acshv"])], inc=SS)
                res["nacshv"] = P.emit("v", lambda e: e.tensor_tensor(
                    nacs_h[:], dts[:, 3 * T + col:3 * T + col + CH],
                    acs_h[:], ALU.subtract), inc=SV)
            U.append(u3)

            def u4():
                P.emit("v", lambda e: e.tensor_tensor(
                    sc_h[:], dch[:], dts[:, T + col:T + col + CH], ALU.mult),
                    waits=[(SS, P.at(SS))], inc=SV)
                m_sch = P.at(SV)
                P.emit("t", lambda e: e.transpose(tb1[0:128, NH:2 * NH],
                                                  sc_h[:], ident[0:NH, 0:NH]),
                       waits=[(SV, m_sch)], inc=ST)
                res["t2"] = P.at(ST)
            U.append(u4)

            def u5():
                P.emit("v", lambda e: e.tensor_copy(scT[:],
                                                    tb1[0:128, NH:2 * NH]),
                       waits=[(ST, res["t2"])], inc=SV)
            U.append(u5)

            def u6():
                w1 = [(SV, res["acshv"])]
                d1a = P.dma("y", "w", lambda e: e.dma_start(
                    out=acsf_d.rearrange("o (h q) -> (o h) q", q=CH),
                    in_=acs_h[:, :]), waits=w1)
                d2a = P.dma("y", "w", lambda e: e.dma_start(
                    out=nacsf_d.rearrange("o (h q) -> (o h) q", q=CH),
                    in_=nacs_h[:, :]), waits=[(SV, res["nacshv"])])
                w1b = [d1a]
                w2b = [d2a] + m_prev_store
                if o == 0 and cc == 0:
                    w1b += [(SS, m_pcp[16]), (SV, m_psb[16])]
                    w2b += [(SS, m_pcp[17]), (SV, m_psb[17])]
                d1b = P.dma("y", "w", lambda e: e.dma_start(
                    out=att_sb[0:1, 0:2048], in_=acsf_d[:, :]), waits=w1b)
                d2b = P.dma("y", "w", lambda e: e.dma_start(
                    out=out_sb[0:1, 0:2048], in_=nacsf_d[:, :]), waits=w2b)
                res["acsflat_w"] = [d1b, d2b]
            U.append(u6)

            def u10():
                P.emit("t", lambda e: e.matmul(
                    tb1[0:128, 2 * NH:3 * NH], ones_row[:],
                    att_sb[0:1, 0:2048].rearrange("o (h q) -> o h q", q=CH)
                    [:, :, CH - 1:CH], start=True, stop=True),
                    waits=list(res["acsflat_w"]), inc=ST)
                m_cdbt = P.at(ST)
                P.emit("s", lambda e: e.activation(
                    cdb128[:], tb1[0:128, 2 * NH:3 * NH], AF.Exp),
                    waits=[(ST, m_cdbt)], inc=SS)
                res["cdb"] = P.at(SS)
                P.emit("t", lambda e: e.transpose(
                    tb0[0:128, 128:256], xbcc(16)[:, col:col + CH], ident[:]),
                    waits=[(SS, ssgate), (SV, P.at(SV))], inc=ST)
                m_bt = P.at(ST)
                P.emit("v", lambda e: e.tensor_copy(btok[:],
                                                    tb0[0:128, 128:256]),
                       waits=[(ST, m_bt)], inc=SV)
            U.append(u10)

            def mk_xt(hs):
                def ux():
                    for h in hs:
                        bk = tb0 if h % 2 == 0 else tb1
                        P.emit("t", lambda e, h=h, bk=bk: e.transpose(
                            bk[0:128, 256:384], xbcc(h)[:, col:col + CH],
                            ident[:]), waits=[(SV, P.at(SV))], inc=ST)
                        m_xt = P.at(ST)
                        P.emit("v", lambda e, h=h, bk=bk: e.tensor_copy(
                            xtok[:, h * CH:(h + 1) * CH], bk[0:128, 256:384]),
                            waits=[(ST, m_xt)], inc=SV)
                return ux
            for hs in ([0, 1, 2, 3], [4, 5, 6, 7], [8, 9, 10, 11],
                       [12, 13, 14, 15]):
                U.append(mk_xt(hs))

            def u15():
                m_xtv = P.at(SV)
                P.emit("t", lambda e: e.matmul(
                    tb1[0:128, 384:512], xbcc(16)[:, col:col + CH],
                    xbcc(17)[:, col:col + CH], start=True, stop=True),
                    waits=[(SV, m_xtv), (SS, P.at(SS))], inc=ST)
                m_cbt = P.at(ST)
                P.emit("v", lambda e: e.tensor_tensor(
                    sg[:, 0:CH], tb1[0:128, 384:512], masktri[:], ALU.mult),
                    waits=[(ST, m_cbt)], inc=SV)
                res["prepv"] = P.at(SV)
            U.append(u15)
            return U, res

        if o == 0:
            for ci in range(3):
                emit_prep_chunk(ci)
        else:
            emit_win_dma(0, [(ST, m_prev_guDE)])
            emit_win_dma(1, [(ST, m_prev_gu_end)])

        bank_free = {i: (ST, 0) for i in range(4)}
        m_preS_sv = m_preS_ss = None
        zslot = 0
        for pos in range(NWG):
            g = GSEQ[pos]
            njt = 4 if g < 8 else 3
            for jj in range(njt):
                j = g * 4 + jj
                is_dt = (g == 8 and jj == 2)
                jw = 16 if is_dt else 128
                if pos == 0:
                    bank, bkey = pb[4 + jj], None
                elif pos >= 5:
                    bank, bkey = pb[jj % 2], jj % 2
                else:
                    bank, bkey = pb[j % 4], j % 4
                pview = bank[0:jw, 0:T]
                c = 2 * pos + (0 if jj < 2 else 1)
                if o == 0:
                    jwc = PCH[c][3]
                    joff = jj * 128 - PCH[c][2]

                    def wv(k, lo, c=c, jwc=jwc, joff=joff, jw=jw):
                        return pstag(c, k * jwc + joff, jw, lo)
                else:
                    def wv(k, lo, pos=pos, jj=jj, jw=jw):
                        return wslabW(pos % 2, lo)[:, k * 512 + jj * 128:
                                                   k * 512 + jj * 128 + jw]
                for k in range(8):
                    waits = []
                    if k == 0:
                        if o == 0:
                            waits += [(SS, m_pcp[c]), (SV, m_psb[c])]
                        else:
                            waits.extend(m_wg[g])
                        waits.append((SV, m_xnsplit))
                        if pos == 0:
                            if m_prev_hv > 0:
                                waits += [(SV, m_prev_hv), (SS, m_prev_gu_ss)]
                        else:
                            bf = bank_free[bkey]
                            if bf[1] > 0:
                                waits.append(bf)
                            elif m_prev_ffev > 0:
                                waits.append((SS, m_prev_ffev))
                    for pi, (lo, xs) in enumerate(((0, xnh), (0, xnl), (1, xnh))):
                        P.emit("t", lambda e, k=k, pi=pi, lo=lo, xs=xs, wv=wv,
                               pview=pview: e.matmul(
                            pview, wv(k, lo), xs[:, k * T:(k + 1) * T],
                            start=(k == 0 and pi == 0),
                            stop=(k == 7 and pi == 2)),
                            waits=waits if pi == 0 else (),
                            inc=(ST if (k == 7 and pi == 2) else None))
                        waits = []
                if o == 0:
                    m_pmm[c] = P.at(ST)
                m_mm = P.at(ST)
                if j < 16:
                    mev = P.emit("s", lambda e, j=j, pview=pview: e.activation(
                        szt(j), pview, AF.Silu), waits=[(ST, m_mm)], inc=SS)
                    bank_free[bkey] = (SS, mev)
                elif not is_dt:
                    f = j - 16
                    mev = P.emit("v", lambda e, f=f, pview=pview: e.tensor_copy(
                        xbcpre(f)[:, 3:3 + T], pview), waits=[(ST, m_mm)], inc=SV)
                    if pos > 0:
                        bank_free[bkey] = (SV, mev)
                    emit_conv(f)
                else:
                    mev = P.emit("v", lambda e, pview=pview: e.tensor_copy(
                        dts[:, 0:T], pview), waits=[(ST, m_mm)], inc=SV)
                    bank_free[bkey] = (SV, mev)
                if ssm_units:
                    if PSCHED[min(zslot, 15)]:
                        ssm_units.pop(0)()
                    zslot += 1
            if o == 0:
                for ci in (2 * pos + 3, 2 * pos + 4):
                    if ci < 18:
                        emit_prep_chunk(ci)
            elif pos + 2 < NWG:
                emit_win_dma(pos + 2, [(ST, P.at(ST))])
            if g == 8:
                m_dt_ev = P.at(SV)
                P.emit("s", lambda e: e.activation(dts[:, T:2 * T], dts[:, 0:T],
                                                   AF.Exp, bias=dtb_sb[:, 0:1]),
                       waits=[(SV, m_dt_ev)], inc=SS)
                P.emit("s", lambda e: e.activation(dts[:, T:2 * T],
                                                   dts[:, T:2 * T], AF.Ln,
                                                   bias=c_one[:, 0:1]), inc=SS)
                P.emit("s", lambda e: e.activation(dts[:, 3 * T:4 * T],
                                                   dts[:, T:2 * T], AF.Ln,
                                                   bias=c_tiny[:, 0:1]), inc=SS)
                m_dt_ss = P.at(SS)
                P.emit("v", lambda e: e.tensor_scalar(dts[:, 2 * T:3 * T],
                                                      dts[:, T:2 * T],
                                                      a_col[:, 0:1], None,
                                                      ALU.mult),
                       waits=[(SS, m_dt_ss)], inc=SV)
                m_preS_sv = P.at(SV)
                m_preS_ss = P.at(SS)
                if o > 0:
                    us0, r0 = build_prep_units(0, m_preS_sv, m_preS_ss)
                    ssm_units.extend(us0)
                    ssm_res[0] = r0
        while ssm_units:
            ssm_units.pop(0)()
        m_conv_s = m_preS_ss
        m_conv_v = m_preS_sv
        m_inproj_t = P.at(ST)

        dbg_gate = [[]]
        m_ssq_ev = 0
        m_prev_ssqfin = 0
        for cc in range(2):
            col = cc * CH

            if cc == 0 and o > 0:
                res = ssm_res[0]
            else:
                usx, res = build_prep_units(cc, max(m_preS_sv, m_ssq_ev),
                                            m_conv_s)
                for u_ in usx:
                    u_()
            m_acsflat_w = res["acsflat_w"]
            m_cdb = res["cdb"]
            m_prepv = res["prepv"]
            acs_flat = att_sb
            nacs_flat = out_sb

            NG = 4
            m_bankA = [None] * NG
            m_bankC = [None] * NG
            m_e4 = [None] * NG
            m_expmt4 = [None] * NG
            m_expr4 = [None] * NG
            m_mtc = [None] * NG
            m_bsc4 = [None] * NG
            m_Ylast = [None] * NG
            m_stlast = [None] * NG
            m_cmlast = [None] * NG
            m_y14 = [None] * NG
            m_ysq4 = [None] * NG
            m_ynw4 = [None] * NG
            m_ssqg = [None] * NG
            m_Y = [None] * NH

            mcbt_rep = bass.AP(sg, 0, [[512, 128], [0, 4], [1, 128]])
            c_rep = bass.AP(actB, 4096 + 17 * T + col, [[8704, 128], [0, 4], [1, 128]])

            def emit_bank(g):
                bankA = pb[2 + (g % 2)]
                ybank = pb[4 + (g % 2)]
                wA = list(m_acsflat_w)
                if g < 2:
                    wA.append((SV, m_prepv))
                if g >= 2:
                    wA.append((SS, m_expr4[g - 2]))
                m_bankA[g] = P.emit("t", lambda e, g=g, bankA=bankA: e.matmul(
                    bankA[0:128, 0:512], ones_row[:],
                    acs_flat[0:1, g * 512:(g + 1) * 512],
                    start=True, stop=True), waits=wA, inc=ST)
                wC = [(SV, m_y14[g - 2] if g >= 2 else P.at(SV))]
                P.emit("t", lambda e, g=g, ybank=ybank: e.matmul(
                    ybank[0:128, 0:512], ones_row[:],
                    acs_flat[0:1, g * 512:(g + 1) * 512],
                    start=True, stop=False), waits=wC)
                for hh in range(4):
                    h = 4 * g + hh
                    P.emit("t", lambda e, h=h, hh=hh, ybank=ybank: e.matmul(
                        ybank[0:128, hh * CH:(hh + 1) * CH],
                        nacs_flat[0:1, h * CH:(h + 1) * CH], ones_row[:],
                        start=False, stop=(hh == 3)),
                        inc=(ST if hh == 3 else None))
                m_bankC[g] = P.at(ST)

            def emit_prep_group(g):
                ybank = pb[4 + (g % 2)]
                bankA = pb[2 + (g % 2)]
                w = [(ST, m_bankC[g])]
                if g >= 1:
                    w.append((SS, m_expmt4[g - 1]))
                if dbg_gate[0] and g == 2:
                    w.extend(dbg_gate[0]); dbg_gate[0] = []
                m_e4[g] = P.emit("v", lambda e, ybank=ybank: e.tensor_scalar(
                    expmt4[:], ybank[0:128, 0:512], 80.0, None, ALU.min),
                    waits=w, inc=SV)
                m_expmt4[g] = P.emit("s", lambda e: e.activation(
                    expmt4[:], expmt4[:], AF.Exp), waits=[(SV, m_e4[g])], inc=SS)
                m_expr4[g] = P.emit("s", lambda e, bankA=bankA: e.activation(
                    expr4[:], bankA[0:128, 0:512], AF.Exp),
                    waits=[(SV, m_e4[g])], inc=SS)
                for hh in range(4):
                    h = 4 * g + hh
                    P.emit("s", lambda e, h=h, hh=hh: e.mul(
                        bsc[:, hh * CH:(hh + 1) * CH], btok[:], scT[:, h:h + 1]),
                        waits=[(SV, P.at(SV)),
                               (ST, m_stlast[g - 1] if g >= 1 else P.at(ST))]
                        if hh == 0 else (), inc=SS)
                m_bsc4[g] = P.at(SS)
                mw = [(SS, m_expmt4[g])]
                if g >= 1:
                    mw.append((ST, m_Ylast[g - 1]))
                P.emit("v", lambda e: e.tensor_tensor(
                    mt_sb[:], mcbt_rep, expmt4[:], ALU.mult), waits=mw, inc=SV)
                P.emit("v", lambda e, g=g: e.tensor_tensor(
                    mt_sb[:], mt_sb[:], di_sb[:, g * 512:(g + 1) * 512], ALU.add),
                    inc=SV)
                cw = [(SS, m_expr4[g])]
                if g >= 1:
                    cw.append((ST, m_Ylast[g - 1]))
                P.emit("v", lambda e, c_rep=c_rep: e.tensor_tensor(
                    cst[:], c_rep, expr4[:], ALU.mult), waits=cw, inc=SV)
                m_mtc[g] = P.at(SV)
                if dbg and o == n_outer - 1 and cc == 0 and g == 1:
                    dbg_gate[0] = [
                        P.dma("y", "o", lambda e, nm=nm, srcb=srcb: e.dma_start(
                            out=dbg_h[nm][:, :], in_=srcb[:]),
                            waits=[(SV, m_mtc[g]), (SS, m_expr4[g])])
                        for nm, srcb in [("d_mt", mt_sb), ("d_cst", cst),
                                         ("d_em", expmt4), ("d_er", expr4)]]

            emit_bank(0)
            emit_prep_group(0)
            for g in range(NG):
                ybank = pb[4 + (g % 2)]
                sbank = pb[6 + (g % 2)]
                if g + 1 < NG:
                    emit_bank(g + 1)
                for hh in range(4):
                    h = 4 * g + hh
                    P.emit("t", lambda e, h=h, hh=hh, ybank=ybank: e.matmul(
                        ybank[0:128, hh * CH:(hh + 1) * CH],
                        xtok[:, h * CH:(h + 1) * CH],
                        mt_sb[:, hh * CH:(hh + 1) * CH], start=True, stop=False),
                        waits=[(SV, m_mtc[g])] if hh == 0 else ())
                    m_Y[h] = P.emit("t", lambda e, h=h, hh=hh, ybank=ybank: e.matmul(
                        ybank[0:128, hh * CH:(hh + 1) * CH],
                        carry[:, h * HD:(h + 1) * HD],
                        cst[:, hh * CH:(hh + 1) * CH], start=False, stop=True),
                        inc=ST)
                m_Ylast[g] = m_Y[4 * g + 3]
                for hh in range(4):
                    h = 4 * g + hh
                    P.emit("t", lambda e, h=h, hh=hh, sbank=sbank: e.matmul(
                        sbank[0:128, hh * CH:(hh + 1) * CH],
                        bsc[:, hh * CH:(hh + 1) * CH],
                        xtok[:, h * CH:(h + 1) * CH], start=(hh == 0),
                        stop=(hh == 3)),
                        waits=[(SS, m_bsc4[g]), (SV, P.at(SV))] if hh == 0 else (),
                        inc=(ST if hh == 3 else None))
                m_stlast[g] = P.at(ST)
                if g >= 1:
                    for hh in range(4):
                        hq = 4 * (g - 1) + hh
                        P.emit("t", lambda e, g=g, hh=hh: e.matmul(
                            pb[0][0:1, 384:384 + CH], ones_col[:],
                            ysq[:, hh * CH:(hh + 1) * CH],
                            start=(g == 1 and hh == 0), stop=False),
                            waits=[(SV, m_ysq4[g - 1])] if hh == 0 else (),
                            inc=(ST if hh == 3 else None))
                    m_ssqg[g - 1] = P.at(ST)
                if g + 1 < NG:
                    emit_prep_group(g + 1)
                for hh in range(4):
                    h = 4 * g + hh
                    P.emit("s", lambda e, h=h: e.mul(
                        carry[:, h * HD:(h + 1) * HD],
                        carry[:, h * HD:(h + 1) * HD], cdb128[:, h:h + 1]),
                        waits=[(ST, m_Y[h]), (SS, m_cdb)] if hh == 0 else
                              [(ST, m_Y[h])], inc=SS)
                m_cmlast[g] = P.at(SS)
                sz_ap = bass.AP(actB, 4 * g * T + col, [[8704, 128], [T, 4], [1, 128]])
                m_y14[g] = P.emit("v", lambda e, g=g, ybank=ybank, sz_ap=sz_ap:
                                  e.tensor_tensor(
                    y1_sb[:], ybank[0:128, 0:512], sz_ap, ALU.mult),
                    waits=[(ST, m_Ylast[g]),
                           (SS, m_ynw4[g - 1] if g >= 1 else 0)], inc=SV)
                m_ysq4[g] = P.emit("v", lambda e: e.tensor_tensor(
                    ysq[:], y1_sb[:], y1_sb[:], ALU.mult),
                    waits=[(ST, m_ssqg[g - 1] if g >= 1 else m_prev_ssqfin)],
                    inc=SV)
                P.emit("v", lambda e, g=g, sbank=sbank: e.tensor_tensor(
                    carry[:, g * 512:(g + 1) * 512],
                    carry[:, g * 512:(g + 1) * 512],
                    sbank[0:128, 0:512], ALU.add),
                    waits=[(ST, m_stlast[g]), (SS, m_cmlast[g])], inc=SV)
                for hh in range(4):
                    h = 4 * g + hh
                    P.emit("s", lambda e, h=h, hh=hh, col=col: e.mul(
                        ynw(h)[:, col:col + CH], y1_sb[:, hh * CH:(hh + 1) * CH],
                        nw_sb[:, h:h + 1]),
                        waits=[(SV, m_y14[g])] if hh == 0 else (), inc=SS)
                m_ynw4[g] = P.at(SS)
            for hh in range(4):
                P.emit("t", lambda e, hh=hh: e.matmul(
                    pb[0][0:1, 384:384 + CH], ones_col[:],
                    ysq[:, hh * CH:(hh + 1) * CH],
                    start=False, stop=(hh == 3)),
                    waits=[(SV, m_ysq4[NG - 1])] if hh == 0 else (),
                    inc=(ST if hh == 3 else None))
            m_ssqfin = P.at(ST)
            m_prev_ssqfin = m_ssqfin
            m_ssq_ev = P.emit("v", lambda e, col=col: e.tensor_copy(
                ssqrow[0:1, col:col + CH], pb[0][0:1, 384:384 + CH]),
                waits=[(ST, m_ssqfin)], inc=SV)
        m_ssm_v = P.at(SV)
        m_ssm_s = P.at(SS)

        if dbg and o == n_outer - 1:
            P.dma("s", "o", lambda e: e.dma_start(
                out=dbg_h["d_carry"][:, :], in_=carry[:]),
                waits=[(SV, m_ssm_v)])

        P.emit("v", lambda e: e.tensor_scalar(rstdrow[0:1, :], ssqrow[0:1, :],
                                              1.0 / DI, EPS, ALU.mult, ALU.add),
               waits=[(SV, m_ssm_v)], inc=SV)
        P.emit("s", lambda e: e.activation(rstdrow[0:1, :], rstdrow[0:1, :], AF.Sqrt),
               waits=[(SV, P.at(SV))], inc=SS)
        P.emit("v", lambda e: e.reciprocal(rstdrow[0:1, :], rstdrow[0:1, :]),
               waits=[(SS, P.at(SS))], inc=SV)
        m_rstdr = P.at(SV)

        m_ssm_t = P.at(ST)
        m_wo = [None] * 16

        def emit_wo_dma(k, waits):
            w2 = list(waits)
            if o == 1:
                w2 += pst_snap[18 + k]
            m_wo[k] = [P.dma(
                "y", "w", lambda e, k=k, lo=lo, srcw=srcw: e.dma_start(
                    out=wqW(k % 8, lo),
                    in_=srcw.rearrange("(a p) n -> a p n", p=128)[k, :, :]),
                waits=w2)
                for lo, srcw in ((0, wout16h_h), (1, wout16l_h))]

        m_ysp = [0] * 16

        def emit_ynw_split(k, extra=()):
            r = (k % 2) * T
            P.emit("v", lambda e, k=k, r=r: e.tensor_copy(
                hhb[:, r:r + T], ynw(k)), waits=extra, inc=SV)
            m_ysp[k] = P.emit("v", lambda e, k=k, r=r: e.tensor_tensor(
                hlb[:, r:r + T], ynw(k), hhb[:, r:r + T], ALU.subtract),
                inc=SV)

        if o == 0:
            emit_prep_chunk(18, ldwaits=[(ST, m_ssm_t)])
            emit_prep_chunk(19, ldwaits=[(ST, m_ssm_t)])
        else:
            for k in range(8):
                emit_wo_dma(k, [(ST, m_inproj_t)] if k < 2 else ())
        emit_ynw_split(0, extra=[(SS, m_ssm_s), (ST, m_ssm_t)])
        emit_ynw_split(1)
        for k in range(16):
            if o == 0:
                waits = [(SS, m_pcp[18 + k]), (SV, m_psb[18 + k])]
                s3w = (18 + k) % 3
                wqh = wbuf16[:, s3w * 4096:s3w * 4096 + 1024]
                wql = wbuf16[:, s3w * 4096 + 2048:s3w * 4096 + 3072]
            else:
                waits = list(m_wo[k])
                wqh = wqW(k % 8, 0)
                wql = wqW(k % 8, 1)
            waits.append((SV, m_ysp[k]))
            if k == 0:
                waits += [(SV, m_ssm_v), (SS, m_ssm_s)]
            for tsub in range(2):
                bks = (pb[0], pb[1]) if tsub == 0 else (pb[2], pb[3])
                r = (k % 2) * T + tsub * 128
                yh = hhb[:, r:r + 128]
                yl = hlb[:, r:r + 128]
                seq = [(yh, wqh, 0), (yh, wqh, 1), (yh, wql, 0),
                       (yh, wql, 1), (yl, wqh, 0), (yl, wqh, 1)]
                for si, (ys, ws, nn_) in enumerate(seq):
                    P.emit("t", lambda e, k=k, si=si, ys=ys, ws=ws, nn_=nn_,
                           bks=bks: e.matmul(
                        bks[nn_][0:128, 0:512], ys,
                        ws[:, nn_ * 512:(nn_ + 1) * 512],
                        start=(k == 0 and si < 2),
                        stop=(k == 15 and si >= 4)),
                        waits=waits if (tsub == 0 and si == 0) else (),
                        inc=(ST if (tsub == 1 and si == 5) else None))
                    waits = []
            if o == 0:
                m_pmm[18 + k] = P.at(ST)
                if 20 + k < NPCH:
                    emit_prep_chunk(20 + k)
            elif k + 8 < 16:
                emit_wo_dma(k + 8, [(ST, P.at(ST))])
            if k + 2 < 16:
                emit_ynw_split(k + 2, extra=[(ST, P.at(ST))])
            if k == 3:
                for tsub in range(2):
                    P.emit("t", lambda e, tsub=tsub: e.transpose(
                        pb[4][0:128, tsub:tsub + 1],
                        rstdrow[0:1, tsub * 128:(tsub + 1) * 128], ident[0:1, 0:1]),
                        waits=[(SV, m_rstdr)] if tsub == 0 else (), inc=ST)
                m_rstdt = P.at(ST)
                P.emit("v", lambda e: e.tensor_copy(
                    rstd_col[:, 0:2], pb[4][0:128, 0:2]),
                    waits=[(ST, m_rstdt)], inc=SV)
                m_rstdc = P.at(SV)
        m_att_mm = P.at(ST)
        m_prev_attmm = m_att_mm
        for tsub in range(2):
            bks = (pb[0], pb[1]) if tsub == 0 else (pb[2], pb[3])
            wev = [(ST, m_att_mm)] if tsub == 0 else []
            if o == 0 and tsub == 0:
                wev += [(SS, m_pcp[32]), (SV, m_psb[32])]
            for nn_ in range(2):
                P.emit("v", lambda e, tsub=tsub, bks=bks, nn_=nn_, wev=wev:
                       e.scalar_tensor_tensor(
                    att_sb[:, tsub * DM + nn_ * 512:tsub * DM + nn_ * 512 + 512],
                    bks[nn_][0:128, 0:512], rstd_col[:, tsub:tsub + 1],
                    u_sb[:, tsub * DM + nn_ * 512:tsub * DM + nn_ * 512 + 512],
                    ALU.mult, ALU.add),
                    waits=wev if nn_ == 0 else (), inc=SV)
                wev = []
        m_att_v = P.at(SV)
        m_u_done = P.at(SV)
        if o + 1 < n_outer:
            m_u[o + 1] = [[P.dma("y", "w", lambda e, o=o, tsub=tsub: e.dma_start(
                out=u_sb[:, tsub * DM:(tsub + 1) * DM],
                in_=u_h.rearrange("(a p) n -> p a n", p=128)
                [:, 2 * (o + 1) + tsub, :]),
                waits=[(SV, m_u_done)])] for tsub in range(2)]
            emit_LN_half(o + 1, 0)
            ln_half_pending = True
        else:
            ln_half_pending = False
        ln_xntok_next = None

        m_dbgst = []
        if dbg and o == n_outer - 1:
            for nm, src in [("d_xnf", xn_f[:]), ("d_sz", actB[:, 0:4096]),
                            ("d_xbcc", actB[:, 4096:8704]), ("d_dts", dts[:]),
                            ("d_ynw", arenaA[:, 0:4096]), ("d_att", att_sb[:])]:
                P.dma("s", "o", lambda e, nm=nm, src=src: e.dma_start(
                    out=dbg_h[nm][:, :], in_=src),
                    waits=[(SV, m_att_v)])
            m_dbgst = P.snap("o")

        gu0_extra = (m_prep + P.snap("p") + [(ST, P.at(ST))]) if o == 0 else []
        if o == 0:
            emit_gu_dma(0, 0, gu0_extra)
            emit_gu_dma(0, 1, gu0_extra)
        emit_gu_dma(1, 0, [(ST, P.at(ST))] + gu0_extra)
        emit_gu_dma(1, 1, [(ST, P.at(ST))] + gu0_extra)

        m_wd = [None] * 24

        def emit_wd_dma(k, waits):
            m_wd[k] = [P.dma("g", "g", lambda e, k=k: e.dma_start(
                out=wbuf16[:, 8192 + (k % 4) * 1024:8192 + (k % 4) * 1024 + 1024],
                in_=wd16_h.rearrange("(a p) n -> a p n", p=128)[k, :, :]),
                waits=waits)]

        for k in range(4):
            emit_wd_dma(k, ([(ST, m_gu_start)]
                            + ((m_prep + P.snap("p")) if o == 0 else []))
                        if k == 0 else ())

        m_silu = [0] * 24
        m_hev = [0] * 24
        m_lntp_cur = 0

        def sgslot(j):
            return (sg[:, (j % 4) * T:(j % 4) * T + T] if j % 4 < 2
                    else ffacc[:, (j % 4 - 2) * T:(j % 4 - 1) * T])

        for g in range(6):
            for jj in range(4):
                j = g * 4 + jj
                for gu in range(2):
                    buf = GUSLOTS[(2 * g + gu) % 6]
                    bank = pb[j % 4] if gu == 0 else pb[4 + (j % 4)]
                    for k in range(8):
                        waits = []
                        if k == 0:
                            waits.extend(m_gu[g][gu])
                            if gu == 0:
                                waits.append((SS, m_silu[j - 4]) if j >= 4
                                             else (SV, m_att_v))
                            else:
                                waits.append((SV, m_hev[j - 4]) if j >= 4
                                             else (SV, m_att_v))
                            if g == 3 and gu == 0 and m_lntp_cur > 0:
                                waits.append((SV, m_lntp_cur))
                        for xsrc in (xnh, xnl):
                            P.emit("t", lambda e, buf=buf, k=k, jj=jj, bank=bank,
                                   xsrc=xsrc: e.matmul(
                                bank[0:128, 0:T],
                                buf[:, k * 512 + jj * 128:k * 512 + (jj + 1) * 128],
                                xsrc[:, k * T:(k + 1) * T],
                                start=(k == 0 and xsrc is xnh),
                                stop=(k == 7 and xsrc is xnl)),
                                waits=waits if xsrc is xnh else (),
                                inc=(ST if (k == 7 and xsrc is xnl) else None))
                            waits = []
                    m_mm = P.at(ST)
                    if gu == 0:
                        w2 = [(ST, m_mm)]
                        if j >= 4:
                            w2.append((SV, m_hev[j - 4]))
                        m_silu[j] = P.emit("s", lambda e, j=j, bank=bank: e.activation(
                            sgslot(j), bank[0:128, 0:T],
                            AF.Silu, scale=QS), waits=w2, inc=SS)
                    else:
                        m_hev[j] = P.emit("v", lambda e, j=j, bank=bank: e.tensor_tensor(
                            hffn(j), bank[0:128, 0:T],
                            sgslot(j), ALU.mult),
                            waits=[(ST, m_mm), (SS, m_silu[j])]
                            + (m_dbgst if j == 0 else []), inc=SV)
            for pn in (2 * g + 4, 2 * g + 5):
                if pn < 12:
                    emit_gu_dma(pn // 2, pn % 2, [(ST, P.at(ST))])
            if g == 1 and ln_half_pending:
                ln_xntok_next = emit_LN_half(o + 1, 1)
            if g == 2 and ln_xntok_next is not None:
                ln_state[o + 1] = emit_LN_tp(ln_xntok_next)
                m_lntp_cur = ln_state[o + 1]
            if g == 4:
                m_guDE = P.at(ST)
        m_h_v = P.at(SV)
        m_prev_hv = m_h_v
        m_prev_gu_ss = P.at(SS)
        m_prev_guDE = m_guDE
        m_prev_gu_end = P.at(ST)
        if o + 1 < n_outer and (o + 1) in ln_state:
            xnsplit_state[o + 1] = emit_xnsplit(ln_state[o + 1],
                                                m_prev_gu_end)

        m_hsp = [0] * 24
        m_dnk = [0] * 24
        for k in range(24):
            hc = (k % 2) * T
            wsp = [(SV, m_hev[k])] if False else []
            if k >= 2:
                wsp.append((ST, m_dnk[k - 2]))
            P.emit("v", lambda e, k=k, hc=hc: e.tensor_copy(
                hhb[:, hc:hc + T], hffn(k)), waits=wsp, inc=SV)
            m_hsp[k] = P.emit("v", lambda e, k=k, hc=hc: e.tensor_tensor(
                hlb[:, hc:hc + T], hffn(k), hhb[:, hc:hc + T], ALU.subtract),
                inc=SV)
            waits = list(m_wd[k]) + [(SV, m_hsp[k])]
            if k == 0:
                waits += [(SV, m_h_v), (SV, m_att_v)]
            for tsub in range(2):
                bks = (pb[0], pb[1]) if tsub == 0 else (pb[2], pb[3])
                for nn_ in range(2):
                    for hsrc in (hhb, hlb):
                        P.emit("t", lambda e, k=k, hc=hc, tsub=tsub, nn_=nn_,
                               bks=bks, hsrc=hsrc: e.matmul(
                            bks[nn_][0:128, 0:512],
                            hsrc[:, hc + tsub * 128:hc + (tsub + 1) * 128],
                            wbuf16[:, 8192 + (k % 4) * 1024 + nn_ * 512:
                                   8192 + (k % 4) * 1024 + (nn_ + 1) * 512],
                            start=(k == 0 and hsrc is hhb),
                            stop=(k == 23 and hsrc is hlb)),
                            waits=waits if (tsub == 0 and nn_ == 0 and hsrc is hhb)
                            else (),
                            inc=(ST if (tsub == 1 and nn_ == 1 and hsrc is hlb)
                                 else None))
            m_dnk[k] = P.at(ST)
            if k + 4 < 24:
                emit_wd_dma(k + 4, [(ST, P.at(ST))])
        m_dn = P.at(ST)
        for tsub in range(2):
            bks = (pb[0], pb[1]) if tsub == 0 else (pb[2], pb[3])
            w1 = [(ST, m_dn)]
            if tsub == 1:
                w1.append((SV, P.at(SV)))
            P.emit("s", lambda e, bks=bks: e.mul(
                ffacc[:, 0:512], bks[0][0:128, 0:512], QS * QS), waits=w1, inc=SS)
            P.emit("s", lambda e, bks=bks: e.mul(
                ffacc[:, 512:1024], bks[1][0:128, 0:512], QS * QS), inc=SS)
            m_ff = P.at(SS)
            wout_w = [(SS, m_ff)] + (m_prev_store if tsub == 0 else [])
            if o == 0 and tsub == 0:
                wout_w += [(SS, m_pcp[33]), (SV, m_psb[33])]
            P.emit("v", lambda e, tsub=tsub, wout_w=wout_w: e.tensor_tensor(
                out_sb[:, tsub * DM:(tsub + 1) * DM], ffacc[:],
                att_sb[:, tsub * DM:(tsub + 1) * DM], ALU.add),
                waits=wout_w, inc=SV)
            m_out_v = P.at(SV)
            P.dma("s", "o", lambda e, o=o, tsub=tsub: e.dma_start(
                out=out_h.rearrange("(a p) n -> p a n", p=128)[:, 2 * o + tsub, :],
                in_=out_sb[:, tsub * DM:(tsub + 1) * DM]),
                waits=[(SV, m_out_v)])
        m_prev_store = P.snap("o")
        m_prev_tensor_end = P.at(ST)
        m_gu_start = m_prev_tensor_end
        m_prev_ffev = P.at(SS)

    final_waits = [(k, v) for k, v in P.cnt.items()
                   if v > 0 and k not in (ST, SV, SS)]

    import contextlib
    with contextlib.ExitStack() as stack:
        sems = {k: stack.enter_context(nc.semaphore(f"sem_{k}"))
                for k in P.cnt}
        block = stack.enter_context(nc.Block())

        own_sem = {"v": SV, "s": SS, "g": SG}

        def runner(eng_key):
            def _run(eng):
                own = own_sem.get(eng_key)
                prev = 0
                for waits, fn, inc, inc_by in P.q[eng_key]:
                    if own is not None and prev > 0:
                        eng.wait_ge(sems[own], prev)
                    for sem_key, val in waits:
                        if val and val > 0:
                            eng.wait_ge(sems[sem_key], val)
                    ins = fn(eng)
                    if inc is not None and ins is not None:
                        ins.then_inc(sems[inc], inc_by)
                    if inc is not None and inc == own:
                        prev += inc_by
                if eng_key == "y":
                    for sem_key, val in final_waits:
                        eng.wait_ge(sems[sem_key], val)
            return _run

        block.tensor(runner("t"))
        block.vector(runner("v"))
        block.scalar(runner("s"))
        block.sync(runner("y"))
        block.gpsimd(runner("g"))

    for g in reversed(ctx):
        g.__exit__(None, None, None)
    return nc


def make_in_maps(inputs, n_cores=8):
    c = np.ascontiguousarray
    base = {
        "winT": c(np.asarray(inputs["L0_attn_W_in"], dtype=np.float32).T),
        "convw": c(np.asarray(inputs["L0_attn_conv_w"], dtype=np.float32)
                   .reshape(18, 128, 4).transpose(1, 0, 2).reshape(128, 72)),
        "convb": c(np.asarray(inputs["L0_attn_conv_b"], dtype=np.float32)
                   .reshape(18, 128).T),
        "dtb": c(np.asarray(inputs["L0_attn_dt_bias"], dtype=np.float32)
                 .reshape(NH, 1)),
        "alog": c(np.asarray(inputs["L0_attn_A_log"], dtype=np.float32)
                  .reshape(NH, 1)),
        "dvec": c(np.asarray(inputs["L0_attn_D"], dtype=np.float32).reshape(1, NH)),
        "nw": c(np.asarray(inputs["L0_attn_norm_weight"], dtype=np.float32)
                .reshape(NH, 128).T),
        "woutT": c(np.asarray(inputs["L0_attn_W_out"], dtype=np.float32).T),
        "wgT": c(np.asarray(inputs["L0_ffn_w_gate"], dtype=np.int32).T),
        "wuT": c(np.asarray(inputs["L0_ffn_w_up"], dtype=np.int32).T),
        "wdT": c(np.asarray(inputs["L0_ffn_w_down"], dtype=np.int32).T),
    }
    u = np.asarray(inputs["L0_attn_u"], dtype=np.float32)
    maps = []
    for i in range(n_cores):
        m = dict(base)
        m["u"] = c(u[i])
        maps.append(m)
    return maps


def kernel(**inputs):
    nc = build_nc(n_outer=8)
    in_maps = make_in_maps(inputs, n_cores=8)
    res = run_bass_kernel_spmd(nc, in_maps, core_ids=list(range(8)))
    return np.stack([np.asarray(r["out"], dtype=np.float32) for r in res.results],
                    axis=0)
